# version 12 (final)
# baseline (speedup 1.0000x reference)
import numpy as np
import ml_dtypes
import concourse.bass as bass
import concourse.bacc as bacc
import concourse.mybir as mybir
from concourse.bass_utils import run_bass_kernel_spmd
from contextlib import ExitStack

F32 = mybir.dt.float32
BF16 = mybir.dt.bfloat16
AF = mybir.ActivationFunctionType
ALU = mybir.AluOpType
AX = mybir.AxisListType
bfdt = ml_dtypes.bfloat16

T = 4096
D = 2048
NTT = T // 128
NDC = D // 128
LN_EPS = 1e-5

ENGINES = ("sync", "tensor", "vector", "scalar", "gpsimd")


class Ctx:
    def __init__(self, nc, es):
        self.nc = nc
        self.es = es
        self.stages = []
        self.consts = {}
        self.dram = {}
        self.A1 = es.enter_context(nc.sbuf_tensor("A1", [128, 131072], mybir.dt.uint8))
        self.A2 = es.enter_context(nc.sbuf_tensor("A2", [128, 57344], mybir.dt.uint8))
        self.xb = [es.enter_context(nc.sbuf_tensor(f"xb{i}", [128, D], F32)) for i in range(2)]
        self.st1 = es.enter_context(nc.sbuf_tensor("st1", [128, 32], F32))
        self.ident = es.enter_context(nc.sbuf_tensor("ident", [128, 128], BF16))
        self.onesb = es.enter_context(nc.sbuf_tensor("onesb", [1, 128], BF16))
        self.ceps = es.enter_context(nc.sbuf_tensor("ceps", [128, 1], F32))
        self.identf = es.enter_context(nc.sbuf_tensor("identf", [128, 128], F32))
        self.onesf = es.enter_context(nc.sbuf_tensor("onesf", [1, 128], F32))
        self.onescol = es.enter_context(nc.sbuf_tensor("onescol", [128, 1], BF16))
        self.ceps2 = es.enter_context(nc.sbuf_tensor("ceps2", [128, 1], F32))
        self.ceps3 = es.enter_context(nc.sbuf_tensor("ceps3", [128, 1], F32))
        self.roper = es.enter_context(nc.sbuf_tensor("roper", [128, 128], BF16))
        self.addconst("c_ident", np.eye(128, dtype=bfdt))
        self.addconst("c_ones", np.ones((1, 128), dtype=bfdt))
        self.addconst("c_eps", np.full((128, 1), LN_EPS, np.float32))
        self.addconst("c_identf", np.eye(128, dtype=np.float32))
        self.addconst("c_onesf", np.ones((1, 128), np.float32))
        self.addconst("c_onescol", np.ones((128, 1), dtype=bfdt))
        self.addconst("c_eps2", np.full((128, 1), 1.1920929e-07, np.float32))
        self.addconst("c_eps3", np.full((128, 1), 1e-6, np.float32))
        rr = np.zeros((128, 128), dtype=bfdt)
        for i in range(64):
            rr[i + 64, i] = bfdt(-1.0)
            rr[i, i + 64] = bfdt(1.0)
        self.addconst("c_roper", rr)

        self.pp = [es.enter_context(nc.psum_tensor(f"pp{i}", [128, 512], F32))
                   for i in range(8)]
        self.BAR = es.enter_context(nc.semaphore("BAR"))
        self.spool = {
            "c": [[es.enter_context(nc.semaphore(f"spc{b}_{i}"))
                   for i in range(16)] for b in range(2)],
            "hw": [[es.enter_context(nc.semaphore(f"sph{b}_{i}"))
                    for i in range(13)] for b in range(2)],
            "sw": [[es.enter_context(nc.semaphore(f"spw{b}_{i}"))
                    for i in range(6)] for b in range(2)],
        }
        self.sp_used = {"c": 0, "hw": 0, "sw": 0}

    def ppb(self, i):
        return self.pp[i][:].bitcast(BF16)[:, 0:512]

    def a1(self, dt, off_bytes, n_elem):
        esz = 2 if dt == BF16 else 4
        assert off_bytes % esz == 0
        return self.A1[:].bitcast(dt)[:, off_bytes // esz: off_bytes // esz + n_elem]

    def a2(self, dt, off_bytes, n_elem):
        esz = 2 if dt == BF16 else 4
        assert off_bytes % esz == 0
        return self.A2[:].bitcast(dt)[:, off_bytes // esz: off_bytes // esz + n_elem]

    def addconst(self, name, arr):
        if name not in self.consts:
            self.consts[name] = arr
            dt = BF16 if arr.dtype == bfdt else F32
            self.dram[name] = self.nc.dram_tensor(name, list(arr.shape), dt,
                                                  kind="ExternalInput").ap()
        return self.dram[name]

    def sem(self, name, kind="c"):
        b = len(self.stages) % 2
        pool = self.spool[kind][b]
        assert self.sp_used[kind] < len(pool), \
            f"{kind} sem pool exhausted at {name}"
        h = pool[self.sp_used[kind]]
        self.sp_used[kind] += 1
        return h

    def scratch(self, name, shape, dt):
        if name not in self.dram:
            self.dram[name] = self.nc.dram_tensor(name, list(shape), dt,
                                                  kind="Internal").ap()
        return self.dram[name]

    def inp(self, name, shape, dt=F32):
        if name not in self.dram:
            self.dram[name] = self.nc.dram_tensor(name, list(shape), dt,
                                                  kind="ExternalInput").ap()
        return self.dram[name]

    def outp(self, name, shape, dt=F32):
        if name not in self.dram:
            self.dram[name] = self.nc.dram_tensor(name, list(shape), dt,
                                                  kind="ExternalOutput").ap()
        return self.dram[name]

    def stage(self, **engine_fns):
        self.stages.append(engine_fns)
        self.sp_used = {"c": 0, "hw": 0, "sw": 0}


class _SemState:

    def __init__(self, pool_ids):
        self.pool = pool_ids
        self.base = {}
        self.delta = {}
        self.mult = []

    def base_of(self, sem):
        return self.base.get(id(sem), 0)

    def add_inc(self, sem, n):
        if id(sem) in self.pool:
            m = 1
            for t in self.mult:
                m *= t
            self.delta[id(sem)] = self.delta.get(id(sem), 0) + n * m

    def flush_stage(self):
        for i, d in self.delta.items():
            self.base[i] = self.base.get(i, 0) + d
        self.delta = {}
        assert not self.mult


class _InstrProxy:
    def __init__(self, ins, st):
        object.__setattr__(self, "_ins", ins)
        object.__setattr__(self, "_st", st)

    def then_inc(self, sem, n):
        self._st.add_inc(sem, n)
        r = self._ins.then_inc(sem, n)
        return _InstrProxy(r, self._st) if r is not None else r

    def __getattr__(self, name):
        return getattr(self._ins, name)


class _ForiProxy:
    def __init__(self, eng, st, a, b):
        self.cm = eng.Fori(a, b)
        self.st = st
        self.trip = b - a

    def __enter__(self):
        self.st.mult.append(self.trip)
        return self.cm.__enter__()

    def __exit__(self, *a):
        self.st.mult.pop()
        return self.cm.__exit__(*a)


class _EngProxy:
    def __init__(self, eng, st):
        object.__setattr__(self, "_e", eng)
        object.__setattr__(self, "_st", st)

    def Fori(self, a, b):
        return _ForiProxy(self._e, self._st, a, b)

    def wait_ge(self, sem, val):
        return self._e.wait_ge(sem, val + self._st.base_of(sem))

    def __getattr__(self, name):
        attr = getattr(self._e, name)
        if callable(attr):
            st = self._st

            def call(*a, **k):
                r = attr(*a, **k)
                return _InstrProxy(r, st) if hasattr(r, "then_inc") else r
            return call
        return attr


def finalize(ctx):
    nc = ctx.nc

    with nc.Block() as blk:
        def pre_sync(e):
            n = 0
            for nm, buf in (("c_ident", ctx.ident), ("c_ones", ctx.onesb),
                            ("c_eps", ctx.ceps), ("c_identf", ctx.identf),
                            ("c_onesf", ctx.onesf), ("c_onescol", ctx.onescol),
                            ("c_eps2", ctx.ceps2), ("c_eps3", ctx.ceps3),
                            ("c_roper", ctx.roper)):
                e.dma_start(out=buf[:], in_=ctx.dram[nm][:]).then_inc(ctx.BAR, 16)
                n += 16
            e.wait_ge(ctx.BAR, n)
        blk.sync(pre_sync)
    nc.all_engine_barrier()

    pool_ids = {id(h) for bank in ctx.spool["sw"] for h in bank}
    sst = _SemState(pool_ids)
    for k, st in enumerate(ctx.stages):
        with nc.Block() as blk:
            for ename in ENGINES:
                fn = st.get(ename)
                if fn is not None:
                    getattr(blk, ename)(
                        lambda e, fn=fn: fn(_EngProxy(e, sst)))
        sst.flush_stage()
        nc.all_engine_barrier()
        nc.gpsimd.drain()
        for kind in ("c", "hw"):
            for h in ctx.spool[kind][(k + 1) % 2]:
                nc.gpsimd.sem_clear(h)
        nc.all_engine_barrier()


def new_program():
    es = ExitStack()
    nc = bacc.Bacc("TRN2", target_bir_lowering=False)
    ctx = Ctx(nc, es)
    return nc, es, ctx


def emit_ln(ctx, x_name, mode="ln", xnT_dram=None, xplus_dram=None):
    nc = ctx.nc
    assert xplus_dram is None or mode == "ln_rms"
    sXD = ctx.sem("lnXD", "hw") if xnT_dram else None
    sLX = ctx.sem("lnLX") if xplus_dram else None
    sLXD = ctx.sem("lnLXD", "hw") if xplus_dram else None
    sLD = [ctx.sem("lnLD0", "hw"), ctx.sem("lnLD1", "hw")]
    sQ = ctx.sem("lnQ")
    sLN = ctx.sem("lnLN")
    sR = ctx.sem("lnR")
    sVS = ctx.sem("lnVS")
    sA = ctx.sem("lnA")
    sTR = ctx.sem("lnTR")
    sEV = ctx.sem("lnEV")

    x = ctx.dram[x_name].rearrange("(n p) d -> n p d", p=128)
    xb = ctx.xb
    st1 = ctx.st1
    xn = [ctx.a2(BF16, 0, D), ctx.a2(BF16, 4096, D)]
    dump = ctx.a2(BF16, 8192, D)
    lnb = [ctx.a2(F32, 12288, D), ctx.a2(F32, 20480, D)] if xplus_dram else None
    xnT3 = ctx.a1(BF16, 0, NDC * T).rearrange("p (c t) -> p c t", t=T)

    VN = 4 if mode == "ln" else 6
    SN = 1 if mode == "ln" else 4

    def col(b, j):
        return st1[:, b * 16 + j: b * 16 + j + 1]

    def f_sync(e):
        xd4 = ctx.dram[xnT_dram].rearrange("(c p) t -> p c t", p=128) \
            if xnT_dram else None
        xpl = ctx.dram[xplus_dram].rearrange("(n p) d -> n p d", p=128) \
            if xplus_dram else None
        with e.Fori(0, NTT // 2) as i:
            e.wait_ge(sA, i * 2)
            e.dma_start(out=xb[0][:], in_=x[i * 2]).then_inc(sLD[0], 16)
            e.wait_ge(sA, i * 2 + 1)
            e.dma_start(out=xb[1][:], in_=x[i * 2 + 1]).then_inc(sLD[1], 16)
            if xnT_dram is not None:
                for b in range(2):
                    e.wait_ge(sEV, i * 2 + b + 2)
                    e.dma_start(out=xd4[:, :, bass.ds((i * 2 + b) * 128, 128)],
                                in_=xnT3[:, :, bass.ds((i * 2 + b) * 128, 128)]
                                ).then_inc(sXD, 16)
            if xplus_dram is not None:
                for b in range(2):
                    e.wait_ge(sLX, (i * 2 + b) * 2 + 2)
                    e.dma_start(out=xpl[i * 2 + b],
                                in_=lnb[b]).then_inc(sLXD, 16)
        if xnT_dram is not None:
            e.wait_ge(sXD, NTT * 16)
        if xplus_dram is not None:
            e.wait_ge(sLXD, NTT * 16)

    def f_vector(e):
        e.nop().then_inc(sA, 1)
        e.nop().then_inc(sVS, VN)
        if mode == "ln_rms":
            e.nop().then_inc(sR, 1)
        with e.Fori(0, NTT // 2) as i:
            for b in range(2):
                it = i * 2 + b
                base = VN * it + VN
                e.wait_ge(sLD[b], i * 16 + 16)
                e.wait_ge(sR, it)
                e.reduce_sum(col(b, 0), xb[b][:], AX.X).then_inc(sVS, 1)
                e.wait_ge(sVS, base + 1)
                e.tensor_scalar(col(b, 2), col(b, 0), 1.0 / D, None,
                                ALU.mult).then_inc(sVS, 1)
                e.wait_ge(sVS, base + 2)
                e.tensor_tensor(col(b, 3), col(b, 2), col(b, 2),
                                ALU.mult).then_inc(sVS, 1)
                e.wait_ge(sQ, it + 1)
                e.wait_ge(sVS, base + 3)
                e.scalar_tensor_tensor(col(b, 4), col(b, 1), 1.0 / D,
                                       col(b, 3), ALU.mult,
                                       ALU.subtract).then_inc(sVS, 1)
                if mode == "ln_rms":
                    e.wait_ge(sLN, it * SN + 2)
                    e.tensor_tensor(col(b, 8), col(b, 4), col(b, 7),
                                    ALU.mult).then_inc(sVS, 1)
                    e.wait_ge(sLN, it * SN + 4)
                    e.tensor_tensor(col(b, 6), col(b, 11), col(b, 10),
                                    ALU.mult).then_inc(sVS, 1)
                    e.nop().then_inc(sR, 1)
                    e.wait_ge(sVS, base + 6)
                if xplus_dram is not None:
                    e.wait_ge(sLN, it * SN + 2)
                    e.wait_ge(sLXD, i * 32)
                    e.tensor_scalar(lnb[b], xb[b][:], col(b, 2), col(b, 11),
                                    ALU.subtract, ALU.mult).then_inc(sLX, 1)
                    e.wait_ge(sLX, it * 2 + 1)
                    e.tensor_tensor(lnb[b], lnb[b], xb[b][:],
                                    ALU.add).then_inc(sLX, 1)
                e.wait_ge(sR, it + 2)
                e.wait_ge(sTR, it)
                e.tensor_scalar(xn[b], xb[b][:], col(b, 2), col(b, 6),
                                ALU.subtract, ALU.mult).then_inc(sA, 1)

    def f_scalar(e):
        e.nop().then_inc(sEV, 1)
        if mode == "ln":
            e.nop().then_inc(sR, 1)
        with e.Fori(0, NTT // 2) as i:
            for b in range(2):
                it = i * 2 + b
                e.wait_ge(sVS, VN * it)
                e.wait_ge(sA, it)
                e.wait_ge(sLD[b], i * 16 + 16)
                e.activation(dump, xb[b][:], AF.Square,
                             accum_out=col(b, 1)).then_inc(sQ, 1)
                e.wait_ge(sVS, VN * it + VN + 4)
                e.activation(col(b, 5), col(b, 4), AF.Ln,
                             bias=ctx.ceps[:]).then_inc(sLN, 1)
                e.wait_ge(sLN, it * SN + 1)
                if mode == "ln":
                    e.activation(col(b, 6), col(b, 5), AF.Exp,
                                 scale=-0.5).then_inc(sR, 1)
                else:
                    e.activation(col(b, 11), col(b, 5), AF.Exp,
                                 scale=-0.5)
                    e.activation(col(b, 7), col(b, 5), AF.Exp,
                                 scale=-1.0).then_inc(sLN, 1)
                    e.wait_ge(sVS, VN * it + VN + 5)
                    e.activation(col(b, 9), col(b, 8), AF.Ln,
                                 bias=ctx.ceps[:]).then_inc(sLN, 1)
                    e.wait_ge(sLN, it * SN + 3)
                    e.activation(col(b, 10), col(b, 9), AF.Exp,
                                 scale=-0.5).then_inc(sLN, 1)
            for b in range(2):
                it = i * 2 + b
                e.wait_ge(sTR, it + 2)
                for g in range(4):
                    ins_ = e.activation(
                        xnT3[:, g * 4:(g + 1) * 4, bass.ts(it, 128)],
                        ctx.ppb(b * 4 + g)[:].rearrange("p (c q) -> p c q", q=128),
                        AF.Copy)
                ins_.then_inc(sEV, 1)

    def f_tensor(e):
        e.nop().then_inc(sTR, 1)
        with e.Fori(0, NTT // 2) as i:
            for b in range(2):
                it = i * 2 + b
                e.wait_ge(sA, it + 2)
                e.wait_ge(sEV, it)
                for g in range(4):
                    for c in range(4):
                        cc = g * 4 + c
                        ins_ = e.transpose(
                            ctx.ppb(b * 4 + g)[:, c * 128:(c + 1) * 128],
                            xn[b][:, cc * 128:(cc + 1) * 128],
                            ctx.ident[:])
                ins_.then_inc(sTR, 1)

    ctx.stage(sync=f_sync, vector=f_vector, tensor=f_tensor, scalar=f_scalar)




def emit_dump(ctx, view, out_name, shape, dt=BF16):
    o = ctx.outp(out_name, shape, dt)
    s = ctx.sem("dump", "hw")

    def f_sync(e):
        e.dma_start(out=o[:], in_=view).then_inc(s, 16)
        e.wait_ge(s, 16)

    ctx.stage(sync=f_sync)


SQRT_DK_INV = float(np.float32(1.0) / np.float32(np.sqrt(np.float32(128.0))))


def emit_mha_qkv(ctx, wqkv_name, qkT_name, v_name,
                 qkbias_name=None, vbias_name=None, n_qk_chunks=32,
                 v_col0=4096, n_v_cols=2048):
    nc = ctx.nc
    wqkv = ctx.dram[wqkv_name]
    qkT = ctx.dram[qkT_name]
    v = ctx.dram[v_name] if v_name else None
    qkb = ctx.dram[qkbias_name] if qkbias_name else None
    vb = ctx.dram[vbias_name] if vbias_name else None

    sW = [ctx.sem("qkvW0", "sw"), ctx.sem("qkvW1", "sw")]
    sMM = ctx.sem("qkvMM")
    sEVA = ctx.sem("qkvEVA")
    sEVB = ctx.sem("qkvEVB")
    sST = [ctx.sem("qkvST0", "hw"), ctx.sem("qkvST1", "hw")]
    sSTB = [ctx.sem("qkvSTB0", "hw"), ctx.sem("qkvSTB1", "hw")]
    sB1 = ctx.sem("qkvB1", "sw")
    sB2 = ctx.sem("qkvB2", "sw")
    sBC = [ctx.sem("qkvBC0", "sw"), ctx.sem("qkvBC1", "sw")]

    wcb = [ctx.a2(BF16, 0, 2048), ctx.a2(BF16, 4096, 2048)]
    qkbig = [ctx.a2(BF16, 8192, 4096), ctx.a2(BF16, 16384, 4096)]
    vwb = ctx.a2(BF16, 24576, 8192)
    biascol = ctx.a2(F32, 40960, 4)
    vbias = ctx.a2(F32, 41216, 2048)
    v_sb = [ctx.a2(BF16, 49408, 512), ctx.a2(BF16, 50432, 512)]
    xnT3 = ctx.a1(BF16, 0, NDC * T).rearrange("p (c t) -> p c t", t=T)
    w4 = wqkv.rearrange("(c p) o -> p c o", p=128)

    NQK = n_qk_chunks
    NVC = n_v_cols // 512
    NG_A = NQK * 8

    def f_gpsimd(e):
        if vb is not None:
            e.dma_start(out=vbias[:, 0:n_v_cols], in_=vb[:]).then_inc(sB2, 16)
        with e.Fori(0, NQK // 2) as o2:
            for p in range(2):
                e.wait_ge(sMM, o2 * 16 + p * 8)
                e.dma_start(out=wcb[p].rearrange("q (c r) -> q c r", r=128),
                            in_=w4[:, :, bass.ts(o2 * 2 + p, 128)]).then_inc(sW[p], 16)
                if qkb is not None:
                    e.wait_ge(sEVA, o2 * 16 + p * 8 + 2)
                    e.dma_start(out=biascol[:, p:p + 1],
                                in_=qkb[bass.ds(o2 * 2 + p, 1), :]).then_inc(sBC[p], 16)
        for vc in range(NVC):
            e.wait_ge(sMM, 8 + NG_A + vc * 32)
            e.dma_start(out=vwb.rearrange("q (c r) -> q c r", r=512),
                        in_=w4[:, :, v_col0 + vc * 512: v_col0 + (vc + 1) * 512]
                        ).then_inc(sW[0], 16)

    def f_tensor(e):
        e.nop().then_inc(sMM, 8)
        with e.Fori(0, NQK // 2) as o2:
            for p in range(2):
                e.wait_ge(sW[p], o2 * 16 + 16)
                for ts in range(8):
                    lg = p * 8 + ts
                    if ts % 2 == 0:
                        e.wait_ge(sEVA, o2 * 16 + lg + 2)
                    for dc in range(NDC):
                        ins_ = e.matmul(ctx.pp[ts % 2][:],
                                        wcb[p][:, dc * 128:(dc + 1) * 128],
                                        xnT3[:, dc, bass.ts(ts, 512)],
                                        start=(dc == 0), stop=(dc == NDC - 1))
                    ins_.then_inc(sMM, 1)
        for vc in range(NVC):
            e.wait_ge(sW[0], (NQK // 2) * 16 + (vc + 1) * 16)
            for tt in range(NTT):
                lb = vc * 32 + tt
                if lb >= 2:
                    e.wait_ge(sEVB, lb - 1)
                else:
                    e.wait_ge(sEVA, 2 + NG_A)
                for dc in range(NDC):
                    ins_ = e.matmul(ctx.pp[tt % 2][:],
                                    xnT3[:, dc, bass.ts(tt, 128)],
                                    vwb[:, dc * 512:(dc + 1) * 512],
                                    start=(dc == 0), stop=(dc == NDC - 1))
                ins_.then_inc(sMM, 1)

    def f_scalar(e):
        e.nop().then_inc(sEVA, 2)
        with e.Fori(0, NQK // 2) as o2:
            for p in range(2):
                e.wait_ge(sST[p], o2 * 16)
                if qkb is not None:
                    e.wait_ge(sBC[p], o2 * 16 + 16)
                for ts in range(8):
                    lg = p * 8 + ts
                    if ts % 2 == 0:
                        e.wait_ge(sMM, o2 * 16 + lg + 10)
                    if qkb is not None:
                        ins_ = e.activation(qkbig[p][:, ts * 512:(ts + 1) * 512],
                                            ctx.pp[ts % 2][:], AF.Identity,
                                            bias=biascol[:, p:p + 1])
                    else:
                        ins_ = e.activation(qkbig[p][:, ts * 512:(ts + 1) * 512],
                                            ctx.pp[ts % 2][:], AF.Copy)
                    ins_.then_inc(sEVA, 1)

    def f_vector(e):
        if vb is not None:
            e.wait_ge(sB2, 16)
        for vc in range(NVC):
            for tt in range(NTT):
                lb = vc * 32 + tt
                e.wait_ge(sMM, 8 + NG_A + lb + 1)
                e.wait_ge(sSTB[tt % 2], 16 * (lb // 2))
                if vb is not None:
                    ins_ = e.scalar_tensor_tensor(
                        v_sb[tt % 2], ctx.pp[tt % 2][:], 1.0,
                        vbias[:, vc * 512:(vc + 1) * 512],
                        ALU.mult, ALU.add)
                else:
                    ins_ = e.tensor_copy(v_sb[tt % 2], ctx.pp[tt % 2][:])
                ins_.then_inc(sEVB, 1)

    def f_sync(e):
        qkT4 = qkT.rearrange("(c p) t -> c p t", p=128)
        with e.Fori(0, NQK // 2) as o2:
            for p in range(2):
                e.wait_ge(sEVA, o2 * 16 + p * 8 + 10)
                e.dma_start(out=qkT4[o2 * 2 + p],
                            in_=qkbig[p]).then_inc(sST[p], 16)
        if v is not None:
            v4 = v.rearrange("(n p) o -> n p o", p=128)
            for vc in range(NVC):
                for tt in range(NTT):
                    lb = vc * 32 + tt
                    e.wait_ge(sEVB, lb + 1)
                    e.dma_start(out=v4[tt, :, vc * 512:(vc + 1) * 512],
                                in_=v_sb[tt % 2]).then_inc(sSTB[tt % 2], 16)
        e.wait_ge(sST[0], NQK * 8)
        e.wait_ge(sST[1], NQK * 8)
        if v is not None:
            e.wait_ge(sSTB[0], NVC * 256)
            e.wait_ge(sSTB[1], NVC * 256)

    ctx.stage(sync=f_sync, tensor=f_tensor, vector=f_vector,
              scalar=f_scalar, gpsimd=f_gpsimd)



def emit_attn_core(ctx, qT_name, kT_name, v_name, attnT_name,
                   n_heads=16, kv_heads=16, causal=True, scale=SQRT_DK_INV):
    nc = ctx.nc
    qT = ctx.dram[qT_name]
    kT = ctx.dram[kT_name]
    v = ctx.dram[v_name]
    attnT = ctx.dram[attnT_name]
    GRP = n_heads // kv_heads
    NQ = 8
    KC = lambda Q: (Q + 1) if causal else NQ

    for i in range(4):
        m = np.where(np.arange(512)[None, :] <= np.arange(128)[:, None] + i * 128,
                     0.0, -1e30).astype(np.float32)
        ctx.addconst(f"c_mask{i}", m)
    ctx.addconst("c_zeros", np.zeros((128, 512), np.float32))

    sQL = ctx.sem("atQL", "hw"); sKL = ctx.sem("atKL", "hw"); sVL = ctx.sem("atVL", "hw")
    sMK = ctx.sem("atMK", "hw")
    sSC = ctx.sem("atSC")
    sSE = ctx.sem("atSE")
    sNM = ctx.sem("atNM")
    sEX = ctx.sem("atEX")
    sRC = ctx.sem("atRC")
    sRT = ctx.sem("atRT")
    sRE = ctx.sem("atRE")
    sRB = ctx.sem("atRB")
    sRV = ctx.sem("atRV")
    sPT = ctx.sem("atPT")
    sPE = ctx.sem("atPE")
    sPV = ctx.sem("atPV")
    sAE = ctx.sem("atAE")
    sAS = ctx.sem("atAS", "hw")

    probsT = ctx.a1(BF16, 0, 16384)
    qh = ctx.a1(BF16, 32768, 4096)
    kh = ctx.a1(BF16, 40960, 4096)
    vh = ctx.a1(BF16, 49152, 4096)
    scores = [ctx.a1(F32, 57344 + s * 16384, 4096) for s in range(4)]
    rb = ctx.a1(F32, 122880, 512)
    probs = [ctx.a2(BF16, 0 + s * 8192, 4096) for s in range(4)]
    at_sb = [ctx.a2(BF16, 32768, 512), ctx.a2(BF16, 33792, 512)]
    reciprow = ctx.a2(F32, 34816, 512)
    masks = ctx.a2(F32, 36864, 2560)
    st1 = ctx.st1

    qT4 = qT.rearrange("(c p) t -> c p t", p=128)
    kT4 = kT.rearrange("(c p) t -> c p t", p=128)
    v5 = v.rearrange("(n p) (g q) -> g p n q", p=128, q=128)
    at4 = attnT.rearrange("(c p) t -> c p t", p=128)

    NSC = sum(4 * KC(Q) for Q in range(NQ))
    NPT = sum(4 * KC(Q) for Q in range(NQ))
    NSUB = 4 * NQ
    CPT = [sum(4 * KC(q) for q in range(Q)) for Q in range(NQ + 1)]

    def mk(kc, Q, sub):
        if causal and kc == Q:
            return masks[:, sub * 512:(sub + 1) * 512]
        return masks[:, 2048:2560]

    def heads_loop(e, body):
        if GRP == 1:
            with e.Fori(0, n_heads) as h:
                body(e, h, h, 0, True)
        else:
            with e.Fori(0, kv_heads) as g:
                with e.Fori(0, GRP) as hg:
                    body(e, g * GRP + hg, g, hg, False)

    def f_sync(e):
        for i in range(4):
            e.dma_start(out=masks[:, i * 512:(i + 1) * 512],
                        in_=ctx.dram[f"c_mask{i}"][:]).then_inc(sMK, 16)
        e.dma_start(out=masks[:, 2048:2560], in_=ctx.dram["c_zeros"][:]).then_inc(sMK, 16)
        if GRP == 1:
            with e.Fori(0, n_heads) as h:
                e.wait_ge(sSC, h * NSC)
                e.dma_start(out=kh, in_=kT4[h]).then_inc(sKL, 16)
                e.wait_ge(sPV, h * NQ)
                e.dma_start(out=vh.rearrange("p (n q) -> p n q", q=128),
                            in_=v5[h]).then_inc(sVL, 16)
                e.dma_start(out=qh, in_=qT4[h]).then_inc(sQL, 16)
                for Q in range(NQ):
                    e.wait_ge(sAE, h * NQ + Q + 3)
                    e.dma_start(out=at4[h, :, bass.ts(Q, 512)],
                                in_=at_sb[Q % 2]).then_inc(sAS, 16)
        else:
            with e.Fori(0, kv_heads) as g:
                e.wait_ge(sSC, g * GRP * NSC)
                e.dma_start(out=kh, in_=kT4[g]).then_inc(sKL, 16)
                e.wait_ge(sPV, g * GRP * NQ)
                e.dma_start(out=vh.rearrange("p (n q) -> p n q", q=128),
                            in_=v5[g]).then_inc(sVL, 16)
                with e.Fori(0, GRP) as hg:
                    h = g * GRP + hg
                    e.wait_ge(sSC, h * NSC)
                    e.dma_start(out=qh, in_=qT4[h]).then_inc(sQL, 16)
                    for Q in range(NQ):
                        e.wait_ge(sAE, h * NQ + Q + 3)
                        e.dma_start(out=at4[h, :, bass.ts(Q, 512)],
                                    in_=at_sb[Q % 2]).then_inc(sAS, 16)
        e.wait_ge(sAS, n_heads * NQ * 16)

    def t_body(e, h, g, hg, single):
        e.wait_ge(sQL, h * 16 + 16)
        e.wait_ge(sKL, g * 16 + 16)
        e.wait_ge(sVL, g * 16 + 16)
        gi = 0
        pt = 0
        for Q in range(NQ):
            kcq = KC(Q)
            for sub in range(4):
                for kc in range(kcq):
                    if gi % 2 == 0:
                        e.wait_ge(sSE, h * NSC + gi + 2)
                    e.matmul(ctx.pp[gi % 2][:],
                             qh[:, Q * 512 + sub * 128: Q * 512 + (sub + 1) * 128],
                             kh[:, kc * 512:(kc + 1) * 512],
                             start=True, stop=True).then_inc(sSC, 1)
                    gi += 1
            e.wait_ge(sRC, h * NQ * 4 + Q * 4 + 4)
            e.wait_ge(sRE, h * NQ + Q)
            for sub in range(4):
                ins_ = e.transpose(ctx.pp[7][0:1, sub * 128:(sub + 1) * 128],
                                   st1[:, 12 + sub:13 + sub], ctx.identf[:])
            ins_.then_inc(sRT, 1)
            e.wait_ge(sRE, h * NQ + Q + 1)
            e.wait_ge(sRV, h * NQ + Q)
            e.matmul(ctx.pp[6][:], ctx.onesf[:], reciprow[0:1, :],
                     start=True, stop=True).then_inc(sRB, 1)
            for tkc in range(4 * kcq):
                if pt % 2 == 0:
                    e.wait_ge(sPE, h * NPT + pt + 2)
                for sub in range(4):
                    e.wait_ge(sEX, h * NSUB + Q * 4 + sub + 5)
                    ins_ = e.transpose(
                        ctx.ppb(2 + pt % 2)[:, sub * 128:(sub + 1) * 128],
                        probs[sub][:, tkc * 128:(tkc + 1) * 128],
                        ctx.ident[:])
                ins_.then_inc(sPT, 1)
                pt += 1
            e.wait_ge(sAE, h * NQ + Q + 2)
            for tkc in range(4 * kcq):
                e.wait_ge(sPE, h * NPT + pt - 4 * kcq + tkc + 3)
                ins_ = e.matmul(ctx.pp[4 + Q % 2][:],
                                vh[:, tkc * 128:(tkc + 1) * 128],
                                probsT[:, tkc * 512:(tkc + 1) * 512],
                                start=(tkc == 0), stop=(tkc == 4 * kcq - 1))
            ins_.then_inc(sPV, 1)

    def v_body(e, h, g, hg, single):
        gi = 0
        pt = 0
        for Q in range(NQ):
            kcq = KC(Q)
            for sub in range(4):
                e.wait_ge(sEX, h * NSUB + Q * 4 + sub + 1)
                for kc in range(kcq):
                    e.wait_ge(sSC, h * NSC + gi + 1)
                    e.scalar_tensor_tensor(
                        scores[sub][:, kc * 512:(kc + 1) * 512],
                        ctx.pp[gi % 2][:], 1.0, mk(kc, Q, sub),
                        ALU.mult, ALU.add).then_inc(sSE, 1)
                    gi += 1
                e.wait_ge(sSE, 2 + h * NSC + gi)
                e.reduce_max(st1[:, 2 * sub:2 * sub + 1],
                             scores[sub][:, 0:kcq * 512], AX.X,
                             negate=True).then_inc(sNM, 1)
                e.wait_ge(sNM, h * NSUB * 2 + Q * 8 + sub * 2 + 1)
                e.tensor_scalar(st1[:, 2 * sub + 1:2 * sub + 2],
                                st1[:, 2 * sub:2 * sub + 1], scale, None,
                                ALU.mult).then_inc(sNM, 1)
            for sub in range(4):
                e.wait_ge(sEX, h * NSUB + Q * 4 + sub + 5)
                e.wait_ge(sRT, h * NQ + Q)
                e.reciprocal(st1[:, 12 + sub:13 + sub],
                             st1[:, 8 + sub:9 + sub]).then_inc(sRC, 1)
            e.wait_ge(sRB, h * NQ + Q + 1)
            e.tensor_copy(rb, ctx.pp[6][:]).then_inc(sRV, 1)
            e.wait_ge(sPV, h * NQ + Q)
            for tkc in range(4 * kcq):
                e.wait_ge(sPT, h * NPT + pt + 1)
                e.tensor_copy(probsT[:, tkc * 512:(tkc + 1) * 512],
                              ctx.ppb(2 + pt % 2)[:]).then_inc(sPE, 1)
                pt += 1
            e.wait_ge(sPV, h * NQ + Q + 1)
            e.wait_ge(sAS, (h * NQ + Q) * 16)
            e.wait_ge(sRV, h * NQ + Q + 1)
            e.tensor_tensor(at_sb[Q % 2], ctx.pp[4 + Q % 2][:], rb,
                            ALU.mult).then_inc(sAE, 1)

    def s_body(e, h, g, hg, single):
        for Q in range(NQ):
            kcq = KC(Q)
            for sub in range(4):
                e.wait_ge(sNM, h * NSUB * 2 + Q * 8 + sub * 2 + 2)
                e.wait_ge(sRC, h * NQ * 4 + Q * 4)
                e.wait_ge(sPT, h * NPT + CPT[Q])
                e.activation(probs[sub][:, 0:kcq * 512],
                             scores[sub][:, 0:kcq * 512], AF.Exp,
                             bias=st1[:, 2 * sub + 1:2 * sub + 2], scale=scale,
                             accum_out=st1[:, 8 + sub:9 + sub]).then_inc(sEX, 1)
            e.wait_ge(sRT, h * NQ + Q + 1)
            e.wait_ge(sRB, h * NQ + Q)
            e.activation(reciprow[0:1, :], ctx.pp[7][0:1, 0:512],
                         AF.Copy).then_inc(sRE, 1)

    def f_tensor(e):
        heads_loop(e, t_body)

    def f_vector(e):
        e.nop().then_inc(sSE, 2)
        e.nop().then_inc(sPE, 2)
        e.nop().then_inc(sAE, 2)
        e.wait_ge(sMK, 80)
        heads_loop(e, v_body)

    def f_scalar(e):
        e.nop().then_inc(sEX, 4)
        heads_loop(e, s_body)

    ctx.stage(sync=f_sync, tensor=f_tensor, vector=f_vector, scalar=f_scalar)


def emit_attn_core2(ctx, qT_name, kT_name, v_name, attnT_name,
                    n_heads=16, kv_heads=16, causal=True, scale=SQRT_DK_INV):
    nc = ctx.nc
    qT = ctx.dram[qT_name]
    kT = ctx.dram[kT_name]
    v = ctx.dram[v_name]
    attnT = ctx.dram[attnT_name]
    GRP = n_heads // kv_heads
    NQ = 8
    KC = lambda Q: (Q + 1) if causal else NQ

    for i in range(4):
        m = np.where(np.arange(512)[None, :] <= np.arange(128)[:, None] + i * 128,
                     0.0, -1e30).astype(np.float32)
        ctx.addconst(f"c_mask{i}", m)
    ctx.addconst("c_zeros", np.zeros((128, 512), np.float32))

    sQL = ctx.sem("atQL", "hw"); sKL = ctx.sem("atKL", "hw"); sVL = ctx.sem("atVL", "hw")
    sMK = ctx.sem("atMK", "hw")
    sSC = ctx.sem("atSC")
    sSE = ctx.sem("atSE")
    sEX = ctx.sem("atEX")
    sRC = ctx.sem("atRC")
    sDG = ctx.sem("atDG")
    sPT = ctx.sem("atPT")
    sPE = ctx.sem("atPE")
    sPV = ctx.sem("atPV")
    sAE = ctx.sem("atAE")
    sAS = ctx.sem("atAS", "hw")

    probsT = ctx.a1(BF16, 0, 16384)
    qh = ctx.a1(BF16, 32768, 4096)
    kh = ctx.a1(BF16, 40960, 4096)
    vh = ctx.a1(BF16, 49152, 4096)
    scores = [ctx.a1(F32, 57344 + s * 16384, 4096) for s in range(4)]
    probs = [ctx.a2(BF16, 0 + s * 8192, 4096) for s in range(4)]
    at_sb = [ctx.a2(BF16, 32768, 512), ctx.a2(BF16, 33792, 512)]
    masks = ctx.a2(F32, 36864, 2560)
    diag = [[ctx.a2(BF16, 47104 + (p * 4 + s) * 256, 128) for s in range(4)]
            for p in range(2)]
    st1 = ctx.st1

    qT4 = qT.rearrange("(c p) t -> c p t", p=128)
    kT4 = kT.rearrange("(c p) t -> c p t", p=128)
    v5 = v.rearrange("(n p) (g q) -> g p n q", p=128, q=128)
    at4 = attnT.rearrange("(c p) t -> c p t", p=128)

    NSC = sum(4 * KC(Q) for Q in range(NQ))
    NPT = sum(4 * KC(Q) for Q in range(NQ))
    NSUB = 4 * NQ
    CPT = [sum(4 * KC(q) for q in range(Q)) for Q in range(NQ + 1)]

    def mk(kc, Q, sub):
        if causal and kc == Q:
            return masks[:, sub * 512:(sub + 1) * 512]
        return masks[:, 2048:2560]

    def heads_loop(e, body):
        if GRP == 1:
            with e.Fori(0, n_heads) as h:
                body(e, h, h, 0)
        else:
            with e.Fori(0, kv_heads) as g:
                with e.Fori(0, GRP) as hg:
                    body(e, g * GRP + hg, g, hg)

    def f_sync(e):
        for i in range(4):
            e.dma_start(out=masks[:, i * 512:(i + 1) * 512],
                        in_=ctx.dram[f"c_mask{i}"][:]).then_inc(sMK, 16)
        e.dma_start(out=masks[:, 2048:2560], in_=ctx.dram["c_zeros"][:]).then_inc(sMK, 16)
        if GRP == 1:
            with e.Fori(0, n_heads) as h:
                e.wait_ge(sSC, h * NSC)
                e.dma_start(out=kh, in_=kT4[h]).then_inc(sKL, 16)
                e.wait_ge(sPV, h * NQ)
                e.dma_start(out=vh.rearrange("p (n q) -> p n q", q=128),
                            in_=v5[h]).then_inc(sVL, 16)
                e.dma_start(out=qh, in_=qT4[h]).then_inc(sQL, 16)
                for Q in range(NQ):
                    e.wait_ge(sAE, h * NQ + Q + 3)
                    e.dma_start(out=at4[h, :, bass.ts(Q, 512)],
                                in_=at_sb[Q % 2]).then_inc(sAS, 16)
        else:
            with e.Fori(0, kv_heads) as g:
                e.wait_ge(sSC, g * GRP * NSC)
                e.dma_start(out=kh, in_=kT4[g]).then_inc(sKL, 16)
                e.wait_ge(sPV, g * GRP * NQ)
                e.dma_start(out=vh.rearrange("p (n q) -> p n q", q=128),
                            in_=v5[g]).then_inc(sVL, 16)
                with e.Fori(0, GRP) as hg:
                    h = g * GRP + hg
                    e.wait_ge(sSC, h * NSC)
                    e.dma_start(out=qh, in_=qT4[h]).then_inc(sQL, 16)
                    for Q in range(NQ):
                        e.wait_ge(sAE, h * NQ + Q + 3)
                        e.dma_start(out=at4[h, :, bass.ts(Q, 512)],
                                    in_=at_sb[Q % 2]).then_inc(sAS, 16)
        e.wait_ge(sAS, n_heads * NQ * 16)

    def t_scores(e, h, Q):
        gi = CPT[Q]
        kcq = KC(Q)
        for sub in range(4):
            for kc in range(kcq):
                if gi % 3 == 0:
                    e.wait_ge(sSE, h * NSC + gi + 3)
                e.matmul(ctx.pp[gi % 3][:],
                         qh[:, Q * 512 + sub * 128: Q * 512 + (sub + 1) * 128],
                         kh[:, kc * 512:(kc + 1) * 512],
                         start=True, stop=True).then_inc(sSC, 1)
                gi += 1

    def t_body(e, h, g, hg):
        e.wait_ge(sQL, h * 16 + 16)
        e.wait_ge(sKL, g * 16 + 16)
        e.wait_ge(sVL, g * 16 + 16)
        t_scores(e, h, 0)
        for Q in range(NQ):
            if Q + 1 < NQ:
                t_scores(e, h, Q + 1)
            kcq = KC(Q)
            pt = CPT[Q]
            e.wait_ge(sDG, h * NSUB + Q * 4 + 4)
            for tkc in range(4 * kcq):
                if pt % 2 == 0:
                    e.wait_ge(sPE, h * NPT + pt + 2)
                for sub in range(4):
                    ins_ = e.matmul(
                        ctx.pp[3 + pt % 2][:, sub * 128:(sub + 1) * 128],
                        probs[sub][:, tkc * 128:(tkc + 1) * 128],
                        diag[Q % 2][sub], start=True, stop=True,
                        skip_group_check=True)
                ins_.then_inc(sPT, 1)
                pt += 1
            e.wait_ge(sAE, h * NQ + Q + 2)
            e.wait_ge(sPE, h * NPT + CPT[Q] + 4 * kcq + 2)
            for tkc in range(4 * kcq):
                ins_ = e.matmul(ctx.pp[5 + Q % 2][:],
                                vh[:, tkc * 128:(tkc + 1) * 128],
                                probsT[:, tkc * 512:(tkc + 1) * 512],
                                start=(tkc == 0), stop=(tkc == 4 * kcq - 1))
            ins_.then_inc(sPV, 1)

    def v_em(e, h, Q):
        kcq = KC(Q)
        gi = CPT[Q]
        for sub in range(4):
            e.wait_ge(sEX, h * NSUB + Q * 4 + sub + 1)
            for kc in range(kcq):
                e.wait_ge(sSC, h * NSC + gi + 1)
                e.scalar_tensor_tensor(
                    scores[sub][:, kc * 512:(kc + 1) * 512],
                    ctx.pp[gi % 3][:], 1.0, mk(kc, Q, sub),
                    ALU.mult, ALU.add).then_inc(sSE, 1)
                gi += 1

    def v_rd(e, h, Q):
        for sub in range(4):
            e.wait_ge(sEX, h * NSUB + Q * 4 + sub + 5)
            e.reciprocal(st1[:, 12 + sub:13 + sub],
                         st1[:, 8 + sub:9 + sub]).then_inc(sRC, 1)
            e.wait_ge(sRC, h * NSUB + Q * 4 + sub + 1)
            e.wait_ge(sPT, h * NPT + CPT[Q] + (0 if Q >= 2 else
                      CPT[NQ - 1 + Q] - NPT - CPT[Q]) +
                      (CPT[Q - 1] - CPT[Q] if Q >= 2 else 0) + NPT)
            e.tensor_scalar(diag[Q % 2][sub], ctx.ident[:],
                            st1[:, 12 + sub:13 + sub], None,
                            ALU.mult).then_inc(sDG, 1)

    def v_body(e, h, g, hg):
        v_em(e, h, 0)
        for Q in range(NQ):
            kcq = KC(Q)
            v_rd(e, h, Q)
            if Q + 1 < NQ:
                v_em(e, h, Q + 1)
            pt = CPT[Q]
            for tkc in range(4 * kcq):
                if pt % 2 == 0:
                    e.wait_ge(sPT, h * NPT + pt + 1 + NPT)
                    e.wait_ge(sPV, h * NQ + Q)
                    e.tensor_copy(probsT[:, tkc * 512:(tkc + 1) * 512],
                                  ctx.pp[3 + pt % 2][:]).then_inc(sPE, 1)
                pt += 1

    def s_atsb(e, h, Qp):
        e.wait_ge(sPV, h * NQ + Qp + 1)
        e.wait_ge(sAS, (h * NQ + Qp) * 16)
        e.activation(at_sb[Qp % 2], ctx.pp[5 + Qp % 2][:],
                     AF.Identity).then_inc(sAE, 1)

    def s_body(e, h, g, hg):
        for Q in range(NQ):
            kcq = KC(Q)
            for sub in range(4):
                e.wait_ge(sSE, 3 + h * NSC + CPT[Q] + (sub + 1) * kcq)
                e.wait_ge(sRC, h * NSUB + Q * 4)
                e.wait_ge(sPT, h * NPT + CPT[Q] + NPT)
                e.activation(probs[sub][:, 0:kcq * 512],
                             scores[sub][:, 0:kcq * 512], AF.Exp,
                             bias=0.0, scale=scale,
                             accum_out=st1[:, 8 + sub:9 + sub]).then_inc(sEX, 1)
            if Q >= 1:
                s_atsb(e, h, Q - 1)
            pt = CPT[Q]
            for tkc in range(4 * kcq):
                if pt % 2 == 1:
                    e.wait_ge(sPT, h * NPT + pt + 1 + NPT)
                    e.wait_ge(sPV, h * NQ + Q)
                    e.activation(probsT[:, tkc * 512:(tkc + 1) * 512],
                                 ctx.pp[3 + pt % 2][:],
                                 AF.Copy).then_inc(sPE, 1)
                pt += 1
        s_atsb(e, h, NQ - 1)

    def f_tensor(e):
        e.nop().then_inc(sPT, NPT)
        heads_loop(e, t_body)

    def f_vector(e):
        e.nop().then_inc(sSE, 3)
        e.wait_ge(sMK, 80)
        heads_loop(e, v_body)

    def f_scalar(e):
        e.nop().then_inc(sEX, 4)
        e.nop().then_inc(sPE, 2)
        e.nop().then_inc(sAE, 2)
        heads_loop(e, s_body)

    ctx.stage(sync=f_sync, tensor=f_tensor, vector=f_vector, scalar=f_scalar)


def emit_proj_resid(ctx, lhsT_name, w_name, x_in_name, x_out_name,
                    bias_name=None, n_k=NDC, n_out=D):
    nc = ctx.nc
    lhsT = ctx.dram[lhsT_name]
    w = ctx.dram[w_name]
    x_in = ctx.dram[x_in_name].rearrange("(n p) d -> n p d", p=128)
    x_out = ctx.dram[x_out_name].rearrange("(n p) d -> n p d", p=128)
    bias = ctx.dram[bias_name] if bias_name else None
    NOC = n_out // 512

    sWL = ctx.sem("prWL", "sw")
    sBL = ctx.sem("prBL", "sw")
    sAL = [ctx.sem("prAL0", "hw"), ctx.sem("prAL1", "hw")]
    sXL = [ctx.sem("prXL0", "hw"), ctx.sem("prXL1", "hw")]
    sMM = ctx.sem("prMM")
    sRB = ctx.sem("prRB")
    sEV = ctx.sem("prEV")
    sXS = [ctx.sem("prXS0", "hw"), ctx.sem("prXS1", "hw")]

    wA = ctx.a1(BF16, 0, n_k * n_out)
    biasb = ctx.a2(F32, 0, n_out)
    xnew = [ctx.a2(F32, 8192, D), ctx.a2(F32, 16384, D)]
    at_sb = [ctx.a2(BF16, 24576, n_k * 128), ctx.a2(BF16, 24576 + 4096, n_k * 128)]
    rbt = [ctx.a2(F32, 32768 + 8192 * b, D) for b in range(2)]
    xb = ctx.xb

    w4 = w.rearrange("(c p) o -> p c o", p=128)
    l4 = lhsT.rearrange("(c p) t -> p c t", p=128)

    def f_gpsimd(e):
        e.dma_start(out=wA.rearrange("p (c o) -> p c o", o=n_out),
                    in_=w4).then_inc(sWL, 16)
        if bias is not None:
            e.dma_start(out=biasb, in_=bias[:]).then_inc(sBL, 16)

    def f_sync(e):
        with e.Fori(0, NTT // 2) as i:
            for b in range(2):
                it = i * 2 + b
                e.wait_ge(sMM, it * 4)
                e.dma_start(out=at_sb[b].rearrange("p (c t) -> p c t", t=128),
                            in_=l4[:, :, bass.ts(it, 128)]).then_inc(sAL[b], 16)
                if bias is not None:
                    e.wait_ge(sRB, it)
                else:
                    e.wait_ge(sEV, it * 4 + 2)
                e.dma_start(out=xb[b][:], in_=x_in[it]).then_inc(sXL[b], 16)
            for b in range(2):
                it = i * 2 + b
                e.wait_ge(sEV, it * 4 + 10)
                e.dma_start(out=x_out[it], in_=xnew[b]).then_inc(sXS[b], 16)
        e.wait_ge(sXS[0], NTT * 8)
        e.wait_ge(sXS[1], NTT * 8)

    def f_tensor(e):
        e.wait_ge(sWL, 16)
        with e.Fori(0, NTT // 2) as i:
            for b in range(2):
                it = i * 2 + b
                e.wait_ge(sAL[b], i * 16 + 16)
                for oc in range(NOC):
                    g = it * NOC + oc
                    if oc % 2 == 0:
                        e.wait_ge(sEV, g + 6)
                    for dc in range(n_k):
                        ins_ = e.matmul(ctx.pp[oc % 2][:],
                                        at_sb[b][:, dc * 128:(dc + 1) * 128],
                                        wA[:, dc * n_out + oc * 512:
                                           dc * n_out + (oc + 1) * 512],
                                        start=(dc == 0), stop=(dc == n_k - 1))
                    ins_.then_inc(sMM, 1)

    def f_vector(e):
        e.nop().then_inc(sEV, 6)
        if bias is not None:
            e.wait_ge(sBL, 16)
        with e.Fori(0, NTT // 2) as i:
            for b in range(2):
                it = i * 2 + b
                e.wait_ge(sXL[b], i * 16 + 16)
                if bias is not None:
                    e.wait_ge(sEV, it * 4 + 2)
                    e.tensor_tensor(rbt[b], xb[b][:], biasb,
                                    ALU.add).then_inc(sRB, 1)
                src = rbt[b] if bias is not None else xb[b][:]
                for oc in range(NOC):
                    g = it * NOC + oc
                    e.wait_ge(sMM, g + 1)
                    e.wait_ge(sXS[b], i * 16)
                    if bias is not None:
                        e.wait_ge(sRB, it + 1)
                    e.tensor_tensor(xnew[b][:, oc * 512:(oc + 1) * 512],
                                    ctx.pp[oc % 2][:],
                                    src[:, oc * 512:(oc + 1) * 512],
                                    ALU.add).then_inc(sEV, 1)

    ctx.stage(sync=f_sync, tensor=f_tensor, vector=f_vector, gpsimd=f_gpsimd)


def runhw(nc, in_maps, core_ids, **kw):
    if not nc.is_finalized():
        nc.finalize()
    return run_bass_kernel_spmd(nc, in_maps, core_ids=core_ids, **kw)


def emit_wtrans(ctx, w_name, wT_name, O, I, scale_name=None, quant=False):
    nc = ctx.nc
    w = ctx.dram[w_name]
    wT = ctx.dram[wT_name]
    scale = ctx.dram[scale_name] if scale_name else None
    assert O % 512 == 0
    NO4 = O // 512
    I_s = 2048 if quant else min(I, 4096)
    assert I % I_s == 0
    NIH = I // I_s
    NIC = I_s // 128
    NG = I_s // 64

    if quant:
        wb = [ctx.a2(BF16, 0, I_s), ctx.a2(BF16, 45056, I_s)]
        WQ = [ctx.a2(BF16, 4096 + j * 2 * I_s, I_s) for j in range(4)]
        AB = ctx.a2(F32, 20480, I_s)
        SGN = ctx.a2(BF16, 28672, I_s)
        MASK = ctx.a2(BF16, 32768, I_s)
        SSC = ctx.a2(BF16, 36864, I_s)
        EE = ctx.a2(BF16, 49152, I_s)
        T1 = ctx.a2(BF16, 53248, I_s)
        SUMS = ctx.a2(F32, 40960, NG)
        SCALE = ctx.a2(BF16, 41216, NG)
        THR = ctx.a2(F32, 41344, NG)
        stg = [ctx.a2(BF16, 41984, 512), ctx.a2(BF16, 43008, 512)]
        scol = [ctx.a2(F32, 44032, 1), ctx.a2(F32, 44036, 1)]
        srcs = WQ
    else:
        wb = ctx.a2(BF16, 0, 4 * I_s)
        stg = [ctx.a2(BF16, 32768, 512), ctx.a2(BF16, 33792, 512)]
        scol = [ctx.a2(F32, 34816, 1), ctx.a2(F32, 34820, 1)]
        srcs = None

    w3 = w.rearrange("(oj p) i -> p oj i", p=128)
    wT4 = wT.rearrange("(c p) o -> c p o", p=128)

    for ih in range(NIH):
        isl = slice(ih * I_s, (ih + 1) * I_s)
        icg0 = ih * NIC
        kbase = 0
        sWB = [ctx.sem("wtWB0", "sw"), ctx.sem("wtWB1", "sw")] if quant \
            else ctx.sem("wtWB", "sw")
        sQS = ctx.sem("wtQS")
        sQV = ctx.sem("wtQV")
        sTR = ctx.sem("wtTR")
        sEV = ctx.sem("wtEV")
        sST = [ctx.sem("wtST0", "hw"), ctx.sem("wtST1", "hw")]
        sSC = [ctx.sem("wtSC0", "sw"), ctx.sem("wtSC1", "sw")]

        def f_gpsimd(e, ih=ih, isl=isl, icg0=icg0, kbase=kbase, sWB=sWB, sQS=sQS, sQV=sQV, sTR=sTR, sEV=sEV, sST=sST, sSC=sSC):
            if quant:
                def qload(e, k, j, op4):
                    m = k * 4 + j
                    if not (isinstance(m, int) and m < 2):
                        e.wait_ge(sQV, (m - 1) * 8)
                    e.dma_start(out=wb[j % 2],
                                in_=w3[:, bass.ds(op4 * 4 + j, 1), isl][:, 0, :]
                                ).then_inc(sWB[j % 2], 16)
                for j in range(4):
                    qload(e, kbase, j, 0)
                with e.Fori(1, NO4) as op4:
                    k = kbase + op4
                    for j in range(4):
                        qload(e, k, j, op4)
                return
            with e.Fori(0, NO4) as op4:
                k = kbase + op4
                if False:
                    pass
                else:
                    e.wait_ge(sTR, k * NIC)
                    e.dma_start(out=wb.rearrange("q (j i) -> q j i", i=I_s),
                                in_=w3[:, bass.ds(op4 * 4, 4), isl]).then_inc(sWB, 16)
                if scale is not None:
                    for ic in range(NIC):
                        e.wait_ge(sEV, k * NIC + ic + 1)
                        e.dma_start(out=scol[ic % 2],
                                    in_=scale[bass.ds((icg0 + ic) * 128, 128)]
                                    ).then_inc(sSC[ic % 2], 16)

        def f_scalar(e, kbase=kbase, sWB=sWB, sQS=sQS, sQV=sQV):
            if not quant:
                return None
            with e.Fori(0, NO4) as op4:
                k = kbase + op4
                for j in range(4):
                    e.wait_ge(sWB[j % 2], (k * 2 + j // 2) * 16 + 16)
                    e.wait_ge(sQV, (k * 4 + j) * 8)
                    e.activation(AB, wb[j % 2], AF.Abs).then_inc(sQS, 1)
                    e.activation(SGN, wb[j % 2], AF.Sign).then_inc(sQS, 1)

        def f_vector(e, kbase=kbase, ih=ih, sQS=sQS, sQV=sQV, sTR=sTR, sEV=sEV, sST=sST, sSC=sSC):
            e.nop().then_inc(sEV, 2)
            with e.Fori(0, NO4) as op4:
                k = kbase + op4
                if quant:
                    for j in range(4):
                        q8 = (k * 4 + j) * 8
                        e.wait_ge(sQS, (k * 4 + j) * 2 + 1)
                        e.tensor_reduce(SUMS.unsqueeze(2),
                                        AB.rearrange("q (g u) -> q g u", u=64),
                                        AX.X, ALU.add).then_inc(sQV, 1)
                        e.wait_ge(sQV, q8 + 1)
                        e.tensor_scalar(SCALE, SUMS, 1.0 / 64, 1e-8, ALU.mult,
                                        ALU.max).then_inc(sQV, 1)
                        e.wait_ge(sQV, q8 + 2)
                        e.tensor_scalar(THR, SCALE, 0.5009765625, None,
                                        ALU.mult).then_inc(sQV, 1)
                        e.wait_ge(sQV, q8 + 3)
                        e.tensor_tensor(MASK.rearrange("q (g u) -> q g u", u=64),
                                        AB.rearrange("q (g u) -> q g u", u=64),
                                        THR.unsqueeze(2).broadcast_to([128, NG, 64]),
                                        ALU.is_gt).then_inc(sQV, 1)
                        e.wait_ge(sQS, (k * 4 + j) * 2 + 2)
                        e.wait_ge(sQV, q8 + 4)
                        e.tensor_tensor(SSC.rearrange("q (g u) -> q g u", u=64),
                                        SGN.rearrange("q (g u) -> q g u", u=64),
                                        SCALE.unsqueeze(2).broadcast_to([128, NG, 64]),
                                        ALU.mult).then_inc(sQV, 1)
                        e.wait_ge(sQV, q8 + 5)
                        e.tensor_tensor(EE, SSC, wb[j % 2],
                                        ALU.subtract).then_inc(sQV, 1)
                        e.wait_ge(sQV, q8 + 6)
                        e.tensor_tensor(T1, wb[j % 2], EE,
                                        ALU.add).then_inc(sQV, 1)
                        e.wait_ge(sQV, q8 + 7)
                        e.wait_ge(sTR, k * NIC)
                        e.tensor_tensor(WQ[j], T1, MASK, ALU.mult).then_inc(sQV, 1)
                for ic in range(NIC):
                    e.wait_ge(sTR, k * NIC + ic + 1)
                    e.wait_ge(sST[ic % 2], (k * (NIC // 2) + ic // 2) * 16)
                    if scale is not None:
                        e.wait_ge(sSC[ic % 2], (k * (NIC // 2) + ic // 2) * 16 + 16)
                        ins_ = e.tensor_scalar(stg[ic % 2], ctx.ppb(ic % 2)[:],
                                               scol[ic % 2], None, ALU.mult)
                    else:
                        ins_ = e.tensor_copy(stg[ic % 2], ctx.ppb(ic % 2)[:])
                    ins_.then_inc(sEV, 1)

        def f_tensor(e, kbase=kbase, sWB=sWB, sQV=sQV, sEV=sEV, sTR=sTR):
            with e.Fori(0, NO4) as op4:
                k = kbase + op4
                if quant:
                    e.wait_ge(sQV, k * 32 + 32)
                else:
                    e.wait_ge(sWB, k * 16 + 16)
                for ic in range(NIC):
                    e.wait_ge(sEV, k * NIC + ic + 1)
                    for j in range(4):
                        src = srcs[j][:, ic * 128:(ic + 1) * 128] if quant \
                            else wb[:, j * I_s + ic * 128: j * I_s + (ic + 1) * 128]
                        ins_ = e.transpose(
                            ctx.ppb(ic % 2)[:, j * 128:(j + 1) * 128],
                            src, ctx.ident[:])
                    ins_.then_inc(sTR, 1)

        def f_sync(e, icg0=icg0, kbase=kbase, sEV=sEV, sST=sST):
            with e.Fori(0, NO4) as op4:
                k = kbase + op4
                for ic in range(NIC):
                    e.wait_ge(sEV, 2 + k * NIC + ic + 1)
                    e.dma_start(out=wT4[icg0 + ic, :, bass.ts(op4, 512)],
                                in_=stg[ic % 2]).then_inc(sST[ic % 2], 16)
            e.wait_ge(sST[0], (kbase + NO4) * (NIC // 2) * 16)
            e.wait_ge(sST[1], (kbase + NO4) * (NIC // 2) * 16)

        ctx.stage(sync=f_sync, tensor=f_tensor, vector=f_vector,
                  gpsimd=f_gpsimd,
                  **({"scalar": f_scalar} if quant else {}))


def emit_l1a_core(ctx, wT_names, fb_name, gated_name):
    nc = ctx.nc
    wT = [ctx.dram[n] for n in wT_names]
    fb = ctx.dram[fb_name]
    gated = ctx.dram[gated_name]
    NSC = 16

    sWL = ctx.sem("laWL", "sw")
    sBC = ctx.sem("laBC", "sw")
    sMM = ctx.sem("laMM")
    sAC = ctx.sem("laAC")
    sVE = ctx.sem("laVE")
    sSN = ctx.sem("laSN")
    sGT = ctx.sem("laGT")
    sGS = ctx.sem("laGS", "hw")

    fbuf = ctx.a2(BF16, 0, T)
    zbuf = ctx.a2(BF16, 8192, T)
    og = ctx.a2(BF16, 16384, T)
    hbuf = ctx.a2(BF16, 24576, T)
    SG = ctx.a2(F32, 32768, 512)
    GA = ctx.a2(F32, 34816, 512)
    wcb = [ctx.a2(BF16, 36864 + wi * 4096, 2048) for wi in range(5)]
    biascol = ctx.st1[:, 0:1]
    xnT3 = ctx.a1(BF16, 0, NDC * T).rearrange("p (c t) -> p c t", t=T)
    g4 = gated.rearrange("(c p) t -> c p t", p=128)
    wT4 = [x.rearrange("(c p) s -> p c s", p=128) for x in wT]

    def f_gpsimd(e):
        with e.Fori(0, NSC) as sc:
            e.wait_ge(sMM, sc * 40)
            for wi in range(5):
                e.dma_start(out=wcb[wi].rearrange("p (c r) -> p c r", r=128),
                            in_=wT4[wi][:, :, bass.ts(sc, 128)]).then_inc(sWL, 16)
            e.wait_ge(sAC, sc * 24)
            e.dma_start(out=biascol, in_=fb[bass.ds(sc, 1), :]).then_inc(sBC, 16)

    def f_tensor(e):
        with e.Fori(0, NSC) as sc:
            e.wait_ge(sWL, sc * 80 + 80)
            for ts in range(8):
                w = None
                e.wait_ge(sAC, (sc * 8 + ts) * 3)
                e.wait_ge(sVE, (sc * 8 + ts) * 2)
                for wi in range(5):
                    for dc in range(NDC):
                        ins_ = e.matmul(ctx.pp[wi][:],
                                        wcb[wi][:, dc * 128:(dc + 1) * 128],
                                        xnT3[:, dc, bass.ts(ts, 512)],
                                        start=(dc == 0), stop=(dc == NDC - 1))
                    ins_.then_inc(sMM, 1)

    def f_scalar(e):
        with e.Fori(0, NSC) as sc:
            e.wait_ge(sBC, sc * 16 + 16)
            e.wait_ge(sSN, sc)
            for ts in range(8):
                wv = sc * 8 + ts
                e.wait_ge(sMM, wv * 5 + 1)
                e.activation(fbuf[:, ts * 512:(ts + 1) * 512], ctx.pp[0][:],
                             AF.Sigmoid, bias=biascol).then_inc(sAC, 1)
                e.wait_ge(sMM, wv * 5 + 2)
                e.wait_ge(sVE, wv * 2)
                e.activation(SG, ctx.pp[1][:], AF.Sigmoid).then_inc(sAC, 1)
                e.wait_ge(sMM, wv * 5 + 4)
                e.activation(GA, ctx.pp[3][:], AF.Silu).then_inc(sAC, 1)

    def f_vector(e):
        with e.Fori(0, NSC) as sc:
            e.wait_ge(sSN, sc)
            e.wait_ge(sGT, sc)
            e.wait_ge(sGS, sc * 16)
            for ts in range(8):
                wv = sc * 8 + ts
                e.wait_ge(sMM, wv * 5 + 3)
                e.wait_ge(sAC, wv * 3 + 2)
                e.tensor_tensor(zbuf[:, ts * 512:(ts + 1) * 512], ctx.pp[2][:],
                                SG, ALU.mult).then_inc(sVE, 1)
                e.wait_ge(sMM, wv * 5 + 5)
                e.wait_ge(sAC, wv * 3 + 3)
                e.tensor_tensor(og[:, ts * 512:(ts + 1) * 512], ctx.pp[4][:],
                                GA, ALU.mult).then_inc(sVE, 1)
            e.wait_ge(sVE, sc * 16 + 16)
            e.wait_ge(sAC, sc * 24 + 24)
            e.tensor_tensor_scan(hbuf, fbuf, zbuf, 0.0, ALU.mult,
                                 ALU.add).then_inc(sSN, 1)
            e.wait_ge(sSN, sc + 1)
            e.tensor_tensor(zbuf, og, hbuf, ALU.mult).then_inc(sGT, 1)

    def f_sync(e):
        with e.Fori(0, NSC) as sc:
            e.wait_ge(sGT, sc + 1)
            e.dma_start(out=g4[sc], in_=zbuf).then_inc(sGS, 16)
        e.wait_ge(sGS, NSC * 16)

    ctx.stage(sync=f_sync, tensor=f_tensor, vector=f_vector,
              scalar=f_scalar, gpsimd=f_gpsimd)


def emit_ffn_gu(ctx, wguT_name, hT_name, rstdT_name):
    nc = ctx.nc
    wguT = ctx.dram[wguT_name]
    hT = ctx.dram[hT_name]
    rstdT = ctx.dram[rstdT_name]
    NHC = 64
    NW = NHC * 8

    sWG = [[ctx.sem(f"fgWG{b}{h}", "sw") for h in range(2)] for b in range(2)]
    sMM = ctx.sem("fgMM")
    sSI = ctx.sem("fgSI")
    sHB = ctx.sem("fgHB")
    sSQ = ctx.sem("fgSQ")
    sSM = ctx.sem("fgSM")
    sHS = ctx.sem("fgHS", "hw")
    sRD = ctx.sem("fgRD")

    hbuf = ctx.a2(BF16, 0, T)
    GS = ctx.a2(BF16, 16384, 512)
    h2 = [ctx.a2(BF16, 17408, 512), ctx.a2(BF16, 18432, 512)]
    wgb = [ctx.a2(BF16, 20480, 4096), ctx.a2(BF16, 28672, 4096)]
    meanb = ctx.a2(F32, 40960, 32)
    lnvb = ctx.a2(F32, 41088, 32)
    rstdb = ctx.a2(F32, 41216, 32)
    rstdT_sb = ctx.a2(F32, 41472, 128)
    xnT3 = ctx.a1(BF16, 0, NDC * T).rearrange("p (c t) -> p c t", t=T)
    wg4 = wguT.rearrange("(c p) o -> p c o", p=128)
    hT4 = hT.rearrange("(c p) t -> c p t", p=128)

    def t_sum(e, wvp, tsp):
        e.wait_ge(sSQ, wvp + 1)
        for tsub in range(4):
            ins_ = e.matmul(ctx.pp[4][:, tsp * 4 + tsub: tsp * 4 + tsub + 1],
                            h2[tsp % 2][:, tsub * 128:(tsub + 1) * 128],
                            ctx.onescol[:],
                            start=(wvp == 0 if isinstance(wvp, int) else False),
                            stop=False, skip_group_check=True)
        ins_.then_inc(sSM, 1)

    def t_wave(e, hc, ts, wb):
        wv = hc * 8 + ts
        e.wait_ge(sSI, wv)
        e.wait_ge(sHB, wv)
        for dc in range(NDC):
            ins_ = e.matmul(ctx.pp[ts % 2][:],
                            wb[:, dc * 128:(dc + 1) * 128],
                            xnT3[:, dc, bass.ts(ts, 512)],
                            start=(dc == 0), stop=(dc == NDC - 1))
        ins_.then_inc(sMM, 1)
        for dc in range(NDC):
            ins_ = e.matmul(ctx.pp[2 + ts % 2][:],
                            wb[:, 2048 + dc * 128: 2048 + (dc + 1) * 128],
                            xnT3[:, dc, bass.ts(ts, 512)],
                            start=(dc == 0), stop=(dc == NDC - 1))
        ins_.then_inc(sMM, 1)
        if isinstance(wv, int) and wv == 0:
            return
        t_sum(e, wv - 1, (ts - 1) % 8)

    def f_tensor(e):
        e.nop().then_inc(sSM, 1)
        e.wait_ge(sWG[0][0], 16)
        e.wait_ge(sWG[0][1], 16)
        for ts in range(8):
            t_wave(e, 0, ts, wgb[0])
        e.wait_ge(sWG[1][0], 16)
        e.wait_ge(sWG[1][1], 16)
        for ts in range(8):
            t_wave(e, 1, ts, wgb[1])
        with e.Fori(1, NHC // 2) as hc2:
            e.wait_ge(sWG[0][0], hc2 * 16 + 16)
            e.wait_ge(sWG[0][1], hc2 * 16 + 16)
            for ts in range(8):
                t_wave(e, hc2 * 2, ts, wgb[0])
            e.wait_ge(sWG[1][0], hc2 * 16 + 16)
            e.wait_ge(sWG[1][1], hc2 * 16 + 16)
            for ts in range(8):
                t_wave(e, hc2 * 2 + 1, ts, wgb[1])
        t_sum(e, NW - 1, 7)
        e.wait_ge(sRD, 3)
        e.transpose(ctx.pp[5][0:32, 0:128], rstdb, ctx.identf[:]).then_inc(sRD, 1)

    def f_gpsimd(e):
        def load(e, hc, b, gate_val):
            if gate_val is not None:
                e.wait_ge(sMM, gate_val)
            e.dma_start(out=wgb[b][:, 0:2048].rearrange("p (c r) -> p c r", r=128),
                        in_=wg4[:, :, bass.ds(hc * 128, 128)]).then_inc(sWG[b][0], 16)
            e.dma_start(out=wgb[b][:, 2048:4096].rearrange("p (c r) -> p c r", r=128),
                        in_=wg4[:, :, bass.ds(8192 + hc * 128, 128)]).then_inc(sWG[b][1], 16)
        load(e, 0, 0, None)
        load(e, 1, 1, None)
        with e.Fori(1, NHC // 2) as hc2:
            load(e, hc2 * 2, 0, (hc2 * 2 - 1) * 16)
            load(e, hc2 * 2 + 1, 1, hc2 * 2 * 16)

    def f_scalar(e):
        with e.Fori(0, NHC) as hc:
            for ts in range(8):
                wv = hc * 8 + ts
                e.wait_ge(sMM, wv * 2 + 1)
                e.wait_ge(sHB, wv)
                e.activation(GS, ctx.pp[ts % 2][:], AF.Silu).then_inc(sSI, 1)
                e.wait_ge(sHB, wv + 1)
                e.wait_ge(sSM, wv)
                e.activation(h2[ts % 2], hbuf[:, ts * 512:(ts + 1) * 512],
                             AF.Square).then_inc(sSQ, 1)
        e.wait_ge(sRD, 1)
        e.activation(lnvb, meanb, AF.Ln, bias=ctx.ceps2[:]).then_inc(sRD, 1)
        e.wait_ge(sRD, 2)
        e.activation(rstdb, lnvb, AF.Exp, scale=-0.5).then_inc(sRD, 1)
        e.wait_ge(sRD, 4)
        e.activation(rstdT_sb[0:32, :], ctx.pp[5][0:32, 0:128],
                     AF.Copy).then_inc(sRD, 1)

    def f_vector(e):
        with e.Fori(0, NHC) as hc:
            for ts in range(8):
                wv = hc * 8 + ts
                e.wait_ge(sMM, wv * 2 + 2)
                e.wait_ge(sSI, wv + 1)
                e.wait_ge(sHS, hc * 16)
                e.tensor_tensor(hbuf[:, ts * 512:(ts + 1) * 512],
                                ctx.pp[2 + ts % 2][:], GS,
                                ALU.mult).then_inc(sHB, 1)
        e.wait_ge(sSM, NW + 1)
        e.tensor_scalar(meanb, ctx.pp[4][:, 0:32], 1.0 / 8192.0, None,
                        ALU.mult).then_inc(sRD, 1)

    def f_sync(e):
        with e.Fori(0, NHC) as hc:
            e.wait_ge(sSQ, hc * 8 + 8)
            e.dma_start(out=hT4[hc], in_=hbuf).then_inc(sHS, 16)
        e.wait_ge(sHS, NHC * 16)
        e.wait_ge(sRD, 5)
        e.dma_start(out=rstdT[:], in_=rstdT_sb[0:32, :]).then_inc(sHS, 16)
        e.wait_ge(sHS, NHC * 16 + 16)

    ctx.stage(sync=f_sync, tensor=f_tensor, vector=f_vector,
              scalar=f_scalar, gpsimd=f_gpsimd)


def emit_ffn_down(ctx, hT_name, wpT_name, rstdT_name, x_in_name, x_out_name):
    nc = ctx.nc
    hT = ctx.dram[hT_name]
    wpT = ctx.dram[wpT_name]
    rstdT = ctx.dram[rstdT_name]
    x_in = ctx.dram[x_in_name].rearrange("(n p) d -> n p d", p=128)
    x_out = ctx.dram[x_out_name].rearrange("(n p) d -> n p d", p=128)
    xpart = ctx.scratch(x_out_name + "_part", [T, D], F32) \
        .rearrange("(n p) d -> n p d", p=128)

    sWL = ctx.sem("fdWL", "sw")
    sAL = [ctx.sem("fdAL0", "hw"), ctx.sem("fdAL1", "hw")]
    sXL = [ctx.sem("fdXL0", "hw"), ctx.sem("fdXL1", "hw")]
    sRL = [ctx.sem("fdRL0", "hw"), ctx.sem("fdRL1", "hw")]
    sPL = [ctx.sem("fdPL0", "hw"), ctx.sem("fdPL1", "hw")]
    sPS = [ctx.sem("fdPS0", "hw"), ctx.sem("fdPS1", "hw")]
    sMM = ctx.sem("fdMM")
    sEV = ctx.sem("fdEV")
    sVX = ctx.sem("fdVX")
    sXS = [ctx.sem("fdXS0", "hw"), ctx.sem("fdXS1", "hw")]

    wA = ctx.a1(BF16, 0, 32 * D)
    at_sb = [ctx.a2(BF16, 0, 32 * 128), ctx.a2(BF16, 8192, 32 * 128)]
    xacc = [ctx.a2(F32, 16384, D), ctx.a2(F32, 24576, D)]
    xnew = [ctx.a2(F32, 32768, D), ctx.a2(F32, 40960, D)]
    rsl = [ctx.a2(F32, 49216, 1), ctx.a2(F32, 49220, 1)]
    xb = ctx.xb
    w4 = wpT.rearrange("(c p) o -> p c o", p=128)
    l4 = hT.rearrange("(c p) t -> p c t", p=128)

    def f_gpsimd(e):
        for kh in range(2):
            if kh == 1:
                e.wait_ge(sMM, NTT * 4)
            e.dma_start(out=wA.rearrange("p (c o) -> p c o", o=D),
                        in_=w4[:, kh * 32:(kh + 1) * 32, :]).then_inc(sWL, 16)

    def f_sync(e):
        with e.Fori(0, NTT // 2) as i:
            for b in range(2):
                it = i * 2 + b
                e.wait_ge(sMM, it * 4)
                e.dma_start(out=at_sb[b].rearrange("p (c t) -> p c t", t=128),
                            in_=l4[:, 0:32, bass.ts(it, 128)]).then_inc(sAL[b], 16)
            for b in range(2):
                it = i * 2 + b
                e.wait_ge(sEV, it * 4 + 10)
                e.dma_start(out=xpart[it], in_=xacc[b]).then_inc(sPS[b], 16)
        with e.Fori(0, NTT // 2) as i:
            for b in range(2):
                it = i * 2 + b
                e.wait_ge(sMM, NTT * 4 + it * 4)
                e.dma_start(out=at_sb[b].rearrange("p (c t) -> p c t", t=128),
                            in_=l4[:, 32:64, bass.ts(it, 128)]).then_inc(sAL[b], 16)
                e.wait_ge(sPS[b], (NTT // 2) * 16)
                e.wait_ge(sVX, it * 4)
                e.dma_start(out=xacc[b][:], in_=xpart[it]).then_inc(sPL[b], 16)
                e.wait_ge(sEV, NTT * 4 + it * 4 + 2)
                e.dma_start(out=xb[b][:], in_=x_in[it]).then_inc(sXL[b], 16)
                e.dma_start(out=rsl[b], in_=rstdT[bass.ds(it, 1), :]).then_inc(sRL[b], 16)
            for b in range(2):
                it = i * 2 + b
                e.wait_ge(sEV, NTT * 4 + it * 4 + 10)
                e.dma_start(out=x_out[it], in_=xnew[b]).then_inc(sXS[b], 16)
        e.wait_ge(sXS[0], NTT * 8)
        e.wait_ge(sXS[1], NTT * 8)

    def f_tensor(e):
        for kh in range(2):
            e.wait_ge(sWL, kh * 16 + 16)
            with e.Fori(0, NTT // 2) as i:
                for b in range(2):
                    it = i * 2 + b
                    e.wait_ge(sAL[b], kh * NTT * 8 + i * 16 + 16)
                    for oc in range(4):
                        g = kh * NTT * 4 + it * 4 + oc
                        if oc % 2 == 0:
                            e.wait_ge(sEV, g + 6)
                        for dc in range(32):
                            ins_ = e.matmul(ctx.pp[oc % 2][:],
                                            at_sb[b][:, dc * 128:(dc + 1) * 128],
                                            wA[:, dc * D + oc * 512:
                                               dc * D + (oc + 1) * 512],
                                            start=(dc == 0), stop=(dc == 31))
                        ins_.then_inc(sMM, 1)

    def f_vector(e):
        e.nop().then_inc(sEV, 6)
        with e.Fori(0, NTT // 2) as i:
            for b in range(2):
                it = i * 2 + b
                e.wait_ge(sPS[b], i * 16)
                for oc in range(4):
                    e.wait_ge(sMM, it * 4 + oc + 1)
                    e.tensor_copy(xacc[b][:, oc * 512:(oc + 1) * 512],
                                  ctx.pp[oc % 2][:]).then_inc(sEV, 1)
        with e.Fori(0, NTT // 2) as i:
            for b in range(2):
                it = i * 2 + b
                e.wait_ge(sXL[b], i * 16 + 16)
                e.wait_ge(sRL[b], i * 16 + 16)
                e.wait_ge(sPL[b], i * 16 + 16)
                for oc in range(4):
                    g = NTT * 4 + it * 4 + oc
                    e.wait_ge(sMM, g + 1)
                    e.wait_ge(sXS[b], i * 16)
                    e.tensor_tensor(xnew[b][:, oc * 512:(oc + 1) * 512],
                                    ctx.pp[oc % 2][:],
                                    xacc[b][:, oc * 512:(oc + 1) * 512],
                                    ALU.add).then_inc(sVX, 1)
                    e.wait_ge(sVX, it * 4 + oc + 1)
                    e.scalar_tensor_tensor(
                        xnew[b][:, oc * 512:(oc + 1) * 512],
                        xnew[b][:, oc * 512:(oc + 1) * 512],
                        rsl[b], xb[b][:, oc * 512:(oc + 1) * 512],
                        ALU.mult, ALU.add).then_inc(sEV, 1)

    ctx.stage(sync=f_sync, tensor=f_tensor, vector=f_vector, gpsimd=f_gpsimd)


def emit_l1b_rmsrope(ctx, qkraw_name, qT_name, kT_name):
    nc = ctx.nc
    qkraw = ctx.dram[qkraw_name]
    qT = ctx.dram[qT_name]
    kT = ctx.dram[kT_name]

    sTB = ctx.sem("rrTB", "hw")
    sLQ = ctx.sem("rrLQ", "hw")
    sSQ = ctx.sem("rrSQ")
    sSM = ctx.sem("rrSM")
    sMEAN = ctx.sem("rrMEAN")
    sLN = ctx.sem("rrLN")
    sTRr = ctx.sem("rrTRr")
    sRRE = ctx.sem("rrRRE")
    sBC = ctx.sem("rrBC")
    sRBE = ctx.sem("rrRBE")
    sVq = ctx.sem("rrVq")
    sROT = ctx.sem("rrROT")
    sTC = ctx.sem("rrTC")
    sST = ctx.sem("rrST", "hw")

    qcos = ctx.a1(F32, 0, T)
    qsin = ctx.a1(F32, 16384, T)
    kcos = ctx.a1(F32, 32768, T)
    ksin = ctx.a1(F32, 49152, T)
    qsl = ctx.a1(BF16, 65536, 16 * 512)
    ksl = ctx.a1(BF16, 81920, 4 * 512)
    rbq = ctx.a1(F32, 90112, 512)
    rbk = ctx.a1(F32, 92160, 512)
    sq = ctx.a2(BF16, 0, 512)
    qn = ctx.a2(BF16, 1024, 512)
    t1 = ctx.a2(F32, 3072, 512)
    t2 = ctx.a2(F32, 5120, 512)
    ohat = ctx.a2(BF16, 7168, 512)
    rrow = ctx.a2(F32, 8192, 512)
    costb = ctx.a2(F32, 10240, 512)
    sintb = ctx.a2(F32, 12288, 512)
    st1 = ctx.st1
    qk4 = qkraw.rearrange("(c p) t -> p c t", p=128)
    qT4 = qT.rearrange("(c p) t -> c p t", p=128)
    kT4 = kT.rearrange("(c p) t -> c p t", p=128)
    NCH = 20

    def chsrc(c):
        return qsl[:, c * 512:(c + 1) * 512] if c < 16 else \
            ksl[:, (c - 16) * 512:(c - 16) * 512 + 512]

    def f_sync(e):
        for nm, buf in (("l1b_qcos", qcos), ("l1b_qsin", qsin),
                        ("l1b_kcos", kcos), ("l1b_ksin", ksin)):
            e.dma_start(out=buf, in_=ctx.dram[nm][:]).then_inc(sTB, 16)
        with e.Fori(0, 8) as ts:
            e.wait_ge(sVq, ts * 80)
            e.dma_start(out=qsl.rearrange("p (c u) -> p c u", u=512),
                        in_=qk4[:, 0:16, bass.ts(ts, 512)]).then_inc(sLQ, 16)
            e.dma_start(out=ksl.rearrange("p (c u) -> p c u", u=512),
                        in_=qk4[:, 16:20, bass.ts(ts, 512)]).then_inc(sLQ, 16)
            for c in range(NCH):
                e.wait_ge(sVq, ts * 80 + c * 4 + 4)
                if c < 16:
                    e.dma_start(out=qT4[c, :, bass.ts(ts, 512)],
                                in_=ohat).then_inc(sST, 16)
                else:
                    e.dma_start(out=kT4[c - 16, :, bass.ts(ts, 512)],
                                in_=ohat).then_inc(sST, 16)
        e.wait_ge(sST, 8 * NCH * 16)

    def f_scalar(e):
        with e.Fori(0, 8) as ts:
            e.wait_ge(sLQ, ts * 32 + 32)
            for c in range(NCH):
                e.wait_ge(sSM, ts * NCH + c)
                e.activation(sq, chsrc(c), AF.Square).then_inc(sSQ, 1)
            for side in range(2):
                b = side * 16
                e.wait_ge(sMEAN, ts * 2 + side + 1)
                e.activation(st1[:, 4 + b:8 + b], st1[:, 0 + b:4 + b], AF.Ln,
                             bias=ctx.ceps3[:]).then_inc(sLN, 1)
                e.wait_ge(sLN, ts * 4 + side * 2 + 1)
                e.activation(st1[:, 8 + b:12 + b], st1[:, 4 + b:8 + b], AF.Exp,
                             scale=-0.5).then_inc(sLN, 1)
                e.wait_ge(sTRr, ts * 2 + side + 1)
                e.wait_ge(sBC, ts * 2 + side)
                e.activation(rrow[0:1, :], ctx.pp[7][0:1, 0:512],
                             AF.Copy).then_inc(sRRE, 1)

    def f_tensor(e):
        with e.Fori(0, 8) as ts:
            for c in range(NCH):
                e.wait_ge(sSQ, ts * NCH + c + 1)
                bank = 0 if c < 16 else 1
                col0 = (c == 0) or (c == 16)
                if col0:
                    e.wait_ge(sMEAN, ts * 2)
                for tsub in range(4):
                    ins_ = e.matmul(ctx.pp[bank][:, tsub: tsub + 1],
                                    sq[:, tsub * 128:(tsub + 1) * 128],
                                    ctx.onescol[:],
                                    start=col0, stop=False,
                                    skip_group_check=True)
                ins_.then_inc(sSM, 1)
            for side in range(2):
                b = side * 16
                e.wait_ge(sLN, ts * 4 + side * 2 + 2)
                e.wait_ge(sRRE, ts * 2 + side)
                for tsub in range(4):
                    ins_ = e.transpose(
                        ctx.pp[7][0:1, tsub * 128:(tsub + 1) * 128],
                        st1[:, 8 + b + tsub: 9 + b + tsub],
                        ctx.identf[:])
                ins_.then_inc(sTRr, 1)
                e.wait_ge(sRRE, ts * 2 + side + 1)
                e.wait_ge(sRBE, ts * 2 + side)
                e.matmul(ctx.pp[6][:], ctx.onesf[:], rrow[0:1, :],
                         start=True, stop=True).then_inc(sBC, 1)
            for c in range(NCH):
                base = ts * 80 + c * 4
                e.wait_ge(sVq, base + 1)
                e.matmul(ctx.pp[4 + c % 2][:], ctx.roper[:], qn,
                         start=True, stop=True).then_inc(sROT, 1)


    def f_vector(e):
        e.wait_ge(sTB, 64)
        with e.Fori(0, 8) as ts:
            for side in range(2):
                e.wait_ge(sSM, ts * NCH + (16 if side == 0 else 20))
                e.wait_ge(sLN, ts * 4 + side * 2)
                e.tensor_scalar(st1[:, side * 16: side * 16 + 4],
                                ctx.pp[side][:, 0:4],
                                1.0 / (2048.0 if side == 0 else 512.0), None,
                                ALU.mult).then_inc(sMEAN, 1)
            for side in range(2):
                e.wait_ge(sBC, ts * 2 + side + 1)
                e.wait_ge(sVq, ts * 80)
                e.tensor_copy(rbq if side == 0 else rbk,
                              ctx.pp[6][:]).then_inc(sRBE, 1)
            for c in range(NCH):
                base = ts * 80 + c * 4
                rb = rbq if c < 16 else rbk
                if c in (0, 16):
                    cosb = qcos if c < 16 else kcos
                    sinb = qsin if c < 16 else ksin
                    e.wait_ge(sVq, ts * 80 + (0 if c == 0 else 64))
                    e.tensor_copy(costb, cosb[:, bass.ts(ts, 512)]).then_inc(sTC, 1)
                    e.tensor_copy(sintb, sinb[:, bass.ts(ts, 512)]).then_inc(sTC, 1)
                e.wait_ge(sTC, ts * 4 + (2 if c < 16 else 4))
                e.wait_ge(sRBE, ts * 2 + (1 if c < 16 else 2))
                e.wait_ge(sROT, ts * NCH + c)
                e.tensor_tensor(qn, chsrc(c), rb, ALU.mult).then_inc(sVq, 1)
                e.wait_ge(sVq, base + 1)
                e.tensor_tensor(t1, qn, costb, ALU.mult).then_inc(sVq, 1)
                e.wait_ge(sROT, ts * NCH + c + 1)
                e.tensor_tensor(t2, ctx.pp[4 + c % 2][:], sintb,
                                ALU.mult).then_inc(sVq, 1)
                e.wait_ge(sVq, base + 3)
                e.wait_ge(sST, (ts * NCH + c) * 16)
                e.tensor_tensor(ohat, t1, t2, ALU.add).then_inc(sVq, 1)

    ctx.stage(sync=f_sync, tensor=f_tensor, vector=f_vector, scalar=f_scalar)


def emit_router_prep(ctx, router_name, routerT_name):
    nc = ctx.nc
    router = ctx.dram[router_name]
    routerT = ctx.dram[routerT_name].rearrange("(c p) u -> c p u", p=128)
    sL = ctx.sem("rpL", "sw"); sT = ctx.sem("rpT"); sE = ctx.sem("rpE")
    sS = ctx.sem("rpS", "hw")
    rt_sb = ctx.a2(BF16, 0, 2048)
    ot_sb = ctx.a2(BF16, 4096, 128)

    def f_gpsimd(e):
        e.dma_start(out=rt_sb[0:8, :], in_=router[:]).then_inc(sL, 16)

    def f_tensor(e):
        e.wait_ge(sL, 16)
        for ic in range(16):
            ins_ = e.transpose(ctx.ppb(0)[:, ic * 8:(ic + 1) * 8],
                               rt_sb[0:8, ic * 128:(ic + 1) * 128],
                               ctx.ident[0:8, 0:8])
        ins_.then_inc(sT, 1)

    def f_vector(e):
        e.wait_ge(sT, 1)
        e.tensor_copy(ot_sb, ctx.ppb(0)[:, 0:128]).then_inc(sE, 1)

    def f_sync(e):
        e.wait_ge(sE, 1)
        for ic in range(16):
            e.dma_start(out=routerT[ic],
                        in_=ot_sb[:, ic * 8:(ic + 1) * 8]).then_inc(sS, 16)
        e.wait_ge(sS, 256)

    ctx.stage(sync=f_sync, tensor=f_tensor, vector=f_vector, gpsimd=f_gpsimd)


def emit_moe_router(ctx, routerT_name, moe_wT_name):
    nc = ctx.nc
    routerT = ctx.dram[routerT_name]
    moe_wT = ctx.dram[moe_wT_name]
    sRL = ctx.sem("mrRL", "sw")
    sMM = ctx.sem("mrMM")
    sV = ctx.sem("mrV")
    sP = ctx.sem("mrP")
    sTR = ctx.sem("mrTR")
    sE = ctx.sem("mrE")
    sS = ctx.sem("mrS", "hw")

    rb_sb = ctx.a2(BF16, 0, 128)
    sc = ctx.a2(F32, 512, 8)
    eq1 = ctx.a2(F32, 1024, 8)
    sm = ctx.a2(F32, 1536, 8)
    eq2 = ctx.a2(F32, 2048, 8)
    w8 = ctx.a2(F32, 2560, 8)
    wT_sb = ctx.a2(F32, 3072, 128)
    st1 = ctx.st1
    xnT3 = ctx.a1(BF16, 0, NDC * T).rearrange("p (c t) -> p c t", t=T)
    rT4 = routerT.rearrange("(c p) u -> p c u", p=128)

    def f_gpsimd(e):
        e.dma_start(out=rb_sb.rearrange("p (c u) -> p c u", u=8),
                    in_=rT4).then_inc(sRL, 16)

    def f_tensor(e):
        e.wait_ge(sRL, 16)
        for tt in range(NTT):
            e.wait_ge(sV, tt * 9)
            for dc in range(NDC):
                ins_ = e.matmul(ctx.pp[0][:, 0:8],
                                xnT3[:, dc, bass.ts(tt, 128)],
                                rb_sb[:, dc * 8:(dc + 1) * 8],
                                start=(dc == 0), stop=(dc == NDC - 1))
            ins_.then_inc(sMM, 1)
            e.wait_ge(sV, tt * 9 + 17)
            e.wait_ge(sE, tt)
            e.transpose(ctx.pp[1][0:8, 0:128], w8, ctx.identf[:]).then_inc(sTR, 1)

    def f_vector(e):
        e.nop().then_inc(sV, 8)
        for tt in range(NTT):
            base = tt * 9 + 8
            e.wait_ge(sMM, tt + 1)
            e.tensor_copy(sc, ctx.pp[0][:, 0:8]).then_inc(sV, 1)
            e.wait_ge(sV, base + 1)
            e.reduce_max(st1[:, 0:1], sc, AX.X).then_inc(sV, 1)
            e.wait_ge(sV, base + 2)
            e.tensor_single_scalar(eq1, sc, st1[:, 0:1], ALU.is_equal).then_inc(sV, 1)
            e.wait_ge(sV, base + 3)
            e.scalar_tensor_tensor(sm, eq1, -1e30, sc, ALU.mult,
                                   ALU.add).then_inc(sV, 1)
            e.wait_ge(sV, base + 4)
            e.reduce_max(st1[:, 1:2], sm, AX.X).then_inc(sV, 1)
            e.wait_ge(sV, base + 5)
            e.tensor_single_scalar(eq2, sm, st1[:, 1:2], ALU.is_equal).then_inc(sV, 1)
            e.wait_ge(sV, base + 6)
            e.wait_ge(sP, tt * 2)
            e.tensor_tensor(st1[:, 4:5], st1[:, 0:1], st1[:, 1:2],
                            ALU.subtract).then_inc(sV, 1)
            e.wait_ge(sP, tt * 2 + 2)
            e.wait_ge(sTR, tt)
            e.tensor_scalar(w8, eq2, st1[:, 3:4], None, ALU.mult).then_inc(sV, 1)
            e.wait_ge(sV, base + 8)
            e.scalar_tensor_tensor(w8, eq1, st1[:, 2:3], w8, ALU.mult,
                                   ALU.add).then_inc(sV, 1)

    def f_scalar(e):
        for tt in range(NTT):
            e.wait_ge(sV, tt * 9 + 15)
            e.activation(st1[:, 2:3], st1[:, 4:5], AF.Sigmoid).then_inc(sP, 1)
            e.activation(st1[:, 3:4], st1[:, 4:5], AF.Sigmoid,
                         scale=-1.0).then_inc(sP, 1)
            e.wait_ge(sTR, tt + 1)
            e.wait_ge(sS, tt * 16)
            e.activation(wT_sb[0:8, :], ctx.pp[1][0:8, 0:128],
                         AF.Copy).then_inc(sE, 1)

    def f_sync(e):
        for tt in range(NTT):
            e.wait_ge(sE, tt + 1)
            e.dma_start(out=moe_wT[:, bass.ts(tt, 128)],
                        in_=wT_sb[0:8, :]).then_inc(sS, 16)
        e.wait_ge(sS, NTT * 16)

    ctx.stage(sync=f_sync, tensor=f_tensor, vector=f_vector,
              scalar=f_scalar, gpsimd=f_gpsimd)


def emit_moe_gu(ctx, wgT_name, wuT_name, hT_name):
    nc = ctx.nc
    wgT = ctx.dram[wgT_name]
    wuT = ctx.dram[wuT_name]
    hT = ctx.dram[hT_name]
    NJ = 48

    sWL = [[ctx.sem(f"mgWL{b}{h}", "sw") for h in range(2)] for b in range(2)]
    sMM = ctx.sem("mgMM")
    sSI = ctx.sem("mgSI")
    sHB = ctx.sem("mgHB")
    sST = [ctx.sem("mgST0", "hw"), ctx.sem("mgST1", "hw")]

    wgb = [ctx.a2(BF16, 0, 2048), ctx.a2(BF16, 4096, 2048)]
    wub = [ctx.a2(BF16, 8192, 2048), ctx.a2(BF16, 12288, 2048)]
    GT = ctx.a2(BF16, 16384, 512)
    hst = [ctx.a2(BF16, 17408, 512), ctx.a2(BF16, 18432, 512)]
    xnT3 = ctx.a1(BF16, 0, NDC * T).rearrange("p (c t) -> p c t", t=T)
    wg4 = wgT.rearrange("(c p) o -> p c o", p=128)
    wu4 = wuT.rearrange("(c p) o -> p c o", p=128)
    hT4 = hT.rearrange("(c p) t -> c p t", p=128)

    def f_gpsimd(e):
        def load(e, j, b, gate_val):
            if gate_val is not None:
                e.wait_ge(sMM, gate_val)
            e.dma_start(out=wgb[b].rearrange("p (c r) -> p c r", r=128),
                        in_=wg4[:, :, bass.ds(j * 128, 128)]).then_inc(sWL[b][0], 16)
            e.dma_start(out=wub[b].rearrange("p (c r) -> p c r", r=128),
                        in_=wu4[:, :, bass.ds(j * 128, 128)]).then_inc(sWL[b][1], 16)
        load(e, 0, 0, None)
        load(e, 1, 1, None)
        with e.Fori(1, NJ // 2) as j2:
            load(e, j2 * 2, 0, (j2 * 2 - 1) * 16)
            load(e, j2 * 2 + 1, 1, j2 * 2 * 16)

    def t_jwave(e, j, b):
        for ts in range(8):
            wv = j * 8 + ts
            e.wait_ge(sSI, wv)
            e.wait_ge(sHB, wv)
            for dc in range(NDC):
                ins_ = e.matmul(ctx.pp[ts % 2][:],
                                wgb[b][:, dc * 128:(dc + 1) * 128],
                                xnT3[:, dc, bass.ts(ts, 512)],
                                start=(dc == 0), stop=(dc == NDC - 1))
            ins_.then_inc(sMM, 1)
            for dc in range(NDC):
                ins_ = e.matmul(ctx.pp[2 + ts % 2][:],
                                wub[b][:, dc * 128:(dc + 1) * 128],
                                xnT3[:, dc, bass.ts(ts, 512)],
                                start=(dc == 0), stop=(dc == NDC - 1))
            ins_.then_inc(sMM, 1)

    def f_tensor(e):
        e.wait_ge(sWL[0][0], 16)
        e.wait_ge(sWL[0][1], 16)
        t_jwave(e, 0, 0)
        e.wait_ge(sWL[1][0], 16)
        e.wait_ge(sWL[1][1], 16)
        t_jwave(e, 1, 1)
        with e.Fori(1, NJ // 2) as j2:
            e.wait_ge(sWL[0][0], j2 * 16 + 16)
            e.wait_ge(sWL[0][1], j2 * 16 + 16)
            t_jwave(e, j2 * 2, 0)
            e.wait_ge(sWL[1][0], j2 * 16 + 16)
            e.wait_ge(sWL[1][1], j2 * 16 + 16)
            t_jwave(e, j2 * 2 + 1, 1)

    def f_scalar(e):
        with e.Fori(0, NJ) as j:
            for ts in range(8):
                wv = j * 8 + ts
                e.wait_ge(sMM, wv * 2 + 1)
                e.wait_ge(sHB, wv)
                e.activation(GT, ctx.pp[ts % 2][:], AF.Silu).then_inc(sSI, 1)

    def f_vector(e):
        with e.Fori(0, NJ) as j:
            for ts in range(8):
                wv = j * 8 + ts
                e.wait_ge(sMM, wv * 2 + 2)
                e.wait_ge(sSI, wv + 1)
                e.wait_ge(sST[ts % 2], (j * 4 + ts // 2) * 16)
                e.tensor_tensor(hst[ts % 2], ctx.pp[2 + ts % 2][:], GT,
                                ALU.mult).then_inc(sHB, 1)

    def f_sync(e):
        with e.Fori(0, NJ) as j:
            for ts in range(8):
                wv = j * 8 + ts
                e.wait_ge(sHB, wv + 1)
                e.dma_start(out=hT4[j][:, bass.ts(ts, 512)],
                            in_=hst[ts % 2]).then_inc(sST[ts % 2], 16)
        e.wait_ge(sST[0], NJ * 4 * 16)
        e.wait_ge(sST[1], NJ * 4 * 16)

    ctx.stage(sync=f_sync, tensor=f_tensor, vector=f_vector,
              scalar=f_scalar, gpsimd=f_gpsimd)


def emit_moe_scale(ctx, hT_name, moe_wT_name):
    nc = ctx.nc
    hT = ctx.dram[hT_name]
    moe_wT = ctx.dram[moe_wT_name]
    sWL = [ctx.sem("msWL0", "hw"), ctx.sem("msWL1", "hw")]
    sBC = ctx.sem("msBC")
    sBE = ctx.sem("msBE")
    sHL = [ctx.sem("msHL0", "hw"), ctx.sem("msHL1", "hw")]
    sSC = ctx.sem("msSC")
    sST = [ctx.sem("msST0", "sw"), ctx.sem("msST1", "sw")]

    wrow = ctx.a2(F32, 0, 512)
    web = ctx.a2(F32, 2048, 512)
    hbuf = [ctx.a2(BF16, 4096 + p * 6144, 6 * 512) for p in range(2)]
    obuf = [ctx.a2(BF16, 16384 + p * 6144, 6 * 512) for p in range(2)]

    def f_sync(e):
        with e.Fori(0, 8) as ex:
            for ts in range(8):
                i = ex * 8 + ts
                e.wait_ge(sBC, i)
                e.dma_start(out=wrow[0:1, :],
                            in_=moe_wT[bass.ds(ex, 1), bass.ts(ts, 512)]
                            ).then_inc(sWL[0], 16)
                e.wait_ge(sSC, i * 6)
                e.dma_start(out=hbuf[ts % 2].rearrange("p (c u) -> p c u", u=512),
                            in_=hT.rearrange("(c p) t -> p c t", p=128)
                            [:, bass.ds(ex * 6, 6), bass.ts(ts, 512)]
                            ).then_inc(sHL[ts % 2], 16)

    def f_gpsimd(e):
        with e.Fori(0, 8) as ex:
            for ts in range(8):
                i = ex * 8 + ts
                e.wait_ge(sSC, i * 6 + 6)
                e.dma_start(out=hT.rearrange("(c p) t -> p c t", p=128)
                            [:, bass.ds(ex * 6, 6), bass.ts(ts, 512)],
                            in_=obuf[ts % 2].rearrange("p (c u) -> p c u", u=512)
                            ).then_inc(sST[ts % 2], 16)
        e.wait_ge(sST[0], 8 * 4 * 16)
        e.wait_ge(sST[1], 8 * 4 * 16)

    def f_tensor(e):
        with e.Fori(0, 8) as ex:
            for ts in range(8):
                i = ex * 8 + ts
                e.wait_ge(sWL[0], i * 16 + 16)
                e.wait_ge(sBE, i)
                e.matmul(ctx.pp[6][:], ctx.onesf[:], wrow[0:1, :],
                         start=True, stop=True).then_inc(sBC, 1)

    def f_vector(e):
        with e.Fori(0, 8) as ex:
            for ts in range(8):
                i = ex * 8 + ts
                e.wait_ge(sBC, i + 1)
                e.wait_ge(sSC, i * 6)
                e.tensor_copy(web, ctx.pp[6][:]).then_inc(sBE, 1)
                e.wait_ge(sBE, i + 1)
                e.wait_ge(sHL[ts % 2], (ex * 4 + ts // 2) * 16 + 16)
                e.wait_ge(sST[ts % 2], (ex * 4 + ts // 2) * 16)
                for fc in range(6):
                    e.tensor_tensor(obuf[ts % 2][:, fc * 512:(fc + 1) * 512],
                                    hbuf[ts % 2][:, fc * 512:(fc + 1) * 512],
                                    web, ALU.mult).then_inc(sSC, 1)

    ctx.stage(sync=f_sync, tensor=f_tensor, vector=f_vector, gpsimd=f_gpsimd)


def emit_moe_down(ctx, hT_name, wdT_name, x_in_name, x_out_name):
    nc = ctx.nc
    hT = ctx.dram[hT_name]
    wdT = ctx.dram[wdT_name]
    x_in = ctx.dram[x_in_name].rearrange("(n p) d -> n p d", p=128)
    x_out = ctx.dram[x_out_name].rearrange("(n p) d -> n p d", p=128)

    sWL = ctx.sem("mdWL", "sw")
    sAL = [ctx.sem("mdAL0", "sw"), ctx.sem("mdAL1", "sw")]
    sXL = [ctx.sem("mdXL0", "hw"), ctx.sem("mdXL1", "hw")]
    sMM = ctx.sem("mdMM")
    sEV = ctx.sem("mdEV")
    sXS = [ctx.sem("mdXS0", "hw"), ctx.sem("mdXS1", "hw")]

    wA = ctx.a1(BF16, 0, 49152).rearrange("p (c e o) -> p c e o", e=8, o=1024)
    at_sb = [ctx.a2(BF16, 0, 6144), ctx.a2(BF16, 12288, 6144)]
    xnew = [ctx.a2(F32, 24576, 1024), ctx.a2(F32, 28672, 1024)]
    xb = ctx.xb
    w4 = wdT.rearrange("(c p) o -> p c o", p=128)
    l4 = hT.rearrange("(c p) t -> p c t", p=128)

    def f_gpsimd(e):
        for oh in range(2):
            if oh == 1:
                e.wait_ge(sMM, 64)
            for ex in range(8):
                e.dma_start(out=wA[:, :, ex, :],
                            in_=w4[:, :, ex * 2048 + oh * 1024:
                                   ex * 2048 + (oh + 1) * 1024]
                            ).then_inc(sWL, 16)
            with e.Fori(0, NTT // 2) as i:
                for b in range(2):
                    it = i * 2 + b
                    e.wait_ge(sMM, oh * 64 + it * 2)
                    e.dma_start(out=at_sb[b].rearrange("p (c t) -> p c t", t=128),
                                in_=l4[:, :, bass.ts(it, 128)]).then_inc(sAL[b], 16)

    def f_sync(e):
        for oh in range(2):
            with e.Fori(0, NTT // 2) as i:
                for b in range(2):
                    it = i * 2 + b
                    e.wait_ge(sEV, oh * 64 + it * 2 + 4)
                    e.dma_start(out=xb[b][:, 0:1024],
                                in_=x_in[it][:, oh * 1024:(oh + 1) * 1024]
                                ).then_inc(sXL[b], 16)
                for b in range(2):
                    it = i * 2 + b
                    e.wait_ge(sEV, oh * 64 + it * 2 + 8)
                    e.dma_start(out=x_out[it][:, oh * 1024:(oh + 1) * 1024],
                                in_=xnew[b]).then_inc(sXS[b], 16)
        e.wait_ge(sXS[0], NTT * 16)
        e.wait_ge(sXS[1], NTT * 16)

    def f_tensor(e):
        for oh in range(2):
            e.wait_ge(sWL, oh * 128 + 128)
            with e.Fori(0, NTT // 2) as i:
                for b in range(2):
                    it = i * 2 + b
                    e.wait_ge(sAL[b], oh * NTT * 8 + i * 16 + 16)
                    for oc in range(2):
                        g = oh * 64 + it * 2 + oc
                        if oc == 0:
                            e.wait_ge(sEV, g + 6)
                        kk = 0
                        for ex in range(8):
                            for fc in range(6):
                                j = ex * 6 + fc
                                ins_ = e.matmul(
                                    ctx.pp[oc % 2][:],
                                    at_sb[b][:, j * 128:(j + 1) * 128],
                                    wA[:, fc, ex, oc * 512:(oc + 1) * 512],
                                    start=(kk == 0), stop=(kk == 47))
                                kk += 1
                        ins_.then_inc(sMM, 1)

    def f_vector(e):
        e.nop().then_inc(sEV, 6)
        for oh in range(2):
            with e.Fori(0, NTT // 2) as i:
                for b in range(2):
                    it = i * 2 + b
                    e.wait_ge(sXL[b], oh * NTT * 8 + i * 16 + 16)
                    for oc in range(2):
                        g = oh * 64 + it * 2 + oc
                        e.wait_ge(sMM, g + 1)
                        e.wait_ge(sXS[b], oh * NTT * 8 + i * 16)
                        e.tensor_tensor(xnew[b][:, oc * 512:(oc + 1) * 512],
                                        ctx.pp[oc % 2][:],
                                        xb[b][:, oc * 512:(oc + 1) * 512],
                                        ALU.add).then_inc(sEV, 1)

    ctx.stage(sync=f_sync, tensor=f_tensor, vector=f_vector, gpsimd=f_gpsimd)



def emit_mla_cqkv(ctx, qdwT_name, kvdwT_name, bq_name, bkv_name,
                  xnTd_name, cqT_name, ckvT_name):
    nc = ctx.nc
    cqT = ctx.dram[cqT_name]
    ckvT = ctx.dram[ckvT_name]
    xd4 = ctx.dram[xnTd_name].rearrange("(c p) t -> p c t", p=128)

    sWL = ctx.sem("mcWL", "sw")
    sXS = ctx.sem("mcXS", "sw")
    sMM = ctx.sem("mcMM")
    sEV = ctx.sem("mcEV")
    sRS = ctx.sem("mcRS")
    sAP = ctx.sem("mcAP")
    sTR = ctx.sem("mcTR")
    sTE = ctx.sem("mcTE")
    sST = ctx.sem("mcST", "hw")

    wqA = ctx.a1(BF16, 0, 16 * 1024)
    wkA = ctx.a1(BF16, 32768, 16 * 512)
    cstg = ctx.a2(BF16, 0, 2048)
    cqsb = ctx.a2(F32, 4096, 1024)
    cqn = ctx.a2(BF16, 8192, 1024)
    ctstg = ctx.a2(BF16, 10240, 1024)
    kvsb = ctx.a2(F32, 12288, 512)
    kvn = ctx.a2(BF16, 14336, 512)
    ktstg = ctx.a2(BF16, 15360, 512)
    brow = ctx.a2(BF16, 16384, 1536)
    dumpq = ctx.a2(BF16, 19456, 1024)
    dumpkv = ctx.a2(BF16, 21504, 512)
    st1 = ctx.st1
    cqT4 = cqT.rearrange("(c p) t -> p c t", p=128)
    ckvT4 = ckvT.rearrange("(c p) t -> p c t", p=128)
    qdwT = ctx.dram[qdwT_name]
    kvdwT = ctx.dram[kvdwT_name]

    def f_gpsimd(e):
        e.dma_start(out=wqA.rearrange("p (c o) -> p c o", o=1024),
                    in_=qdwT.rearrange("(c p) o -> p c o", p=128)).then_inc(sWL, 16)
        e.dma_start(out=wkA.rearrange("p (c o) -> p c o", o=512),
                    in_=kvdwT.rearrange("(c p) o -> p c o", p=128)).then_inc(sWL, 16)
        e.dma_start(out=brow[0:1, 0:1024], in_=ctx.dram[bq_name][:]).then_inc(sWL, 16)
        e.dma_start(out=brow[0:1, 1024:1536], in_=ctx.dram[bkv_name][:]).then_inc(sWL, 16)
        with e.Fori(0, NTT) as tt:
            e.wait_ge(sMM, tt * 3)
            e.dma_start(out=cstg.rearrange("p (c q) -> p c q", q=128),
                        in_=xd4[:, :, bass.ds(tt * 128, 128)]).then_inc(sXS, 16)

    def f_tensor(e):
        e.wait_ge(sWL, 64)
        with e.Fori(0, NTT) as tt:
            e.wait_ge(sXS, tt * 16 + 16)
            for grp in range(3):
                bofs = [0, 512, 1024][grp]
                wsrc = wqA if grp < 2 else wkA
                ostride = 1024 if grp < 2 else 512
                obase = [0, 512, 0][grp]
                e.wait_ge(sEV, tt * 3 + grp)
                for dc in range(NDC):
                    e.matmul(ctx.pp[grp][:],
                             cstg[:, dc * 128:(dc + 1) * 128],
                             wsrc[:, dc * ostride + obase: dc * ostride + obase + 512],
                             start=(dc == 0), stop=False)
                ins_ = e.matmul(ctx.pp[grp][:], ctx.onesb[:],
                                brow[0:1, bofs: bofs + 512],
                                start=False, stop=True)
                ins_.then_inc(sMM, 1)
            for grp in range(3):
                src = cqn if grp < 2 else kvn
                sof = [0, 512, 0][grp]
                e.wait_ge(sAP, tt * 2 + (1 if grp < 2 else 2))
                e.wait_ge(sTE, tt * 3 + grp)
                for q4 in range(4):
                    ins_ = e.transpose(
                        ctx.ppb(4 + grp % 2)[:, q4 * 128:(q4 + 1) * 128],
                        src[:, sof + q4 * 128: sof + (q4 + 1) * 128],
                        ctx.ident[:])
                ins_.then_inc(sTR, 1)

    def f_vector(e):
        with e.Fori(0, NTT) as tt:
            e.wait_ge(sAP, tt * 2)
            e.wait_ge(sRS, tt * 6)
            for grp in range(3):
                e.wait_ge(sMM, tt * 3 + grp + 1)
                dst = cqsb[:, 0:512] if grp == 0 else \
                    cqsb[:, 512:1024] if grp == 1 else kvsb
                e.tensor_copy(dst, ctx.pp[grp][:]).then_inc(sEV, 1)
            e.wait_ge(sRS, tt * 6 + 3)
            e.wait_ge(sTR, tt * 3)
            e.tensor_scalar(cqn, cqsb, st1[:, 6:7], None,
                            ALU.mult).then_inc(sAP, 1)
            e.wait_ge(sRS, tt * 6 + 6)
            e.tensor_scalar(kvn, kvsb, st1[:, 9:10], None,
                            ALU.mult).then_inc(sAP, 1)

    def f_scalar(e):
        with e.Fori(0, NTT) as tt:
            e.wait_ge(sEV, tt * 3 + 2)
            e.wait_ge(sAP, tt * 2)
            e.activation(dumpq, cqsb, AF.Square,
                         accum_out=st1[:, 5:6]).then_inc(sRS, 1)
            e.wait_ge(sRS, tt * 6 + 1)
            e.activation(st1[:, 7:8], st1[:, 5:6], AF.Ln, bias=ctx.ceps2[:],
                         scale=1.0 / 1024.0).then_inc(sRS, 1)
            e.wait_ge(sRS, tt * 6 + 2)
            e.activation(st1[:, 6:7], st1[:, 7:8], AF.Exp,
                         scale=-0.5).then_inc(sRS, 1)
            e.wait_ge(sEV, tt * 3 + 3)
            e.activation(dumpkv, kvsb, AF.Square,
                         accum_out=st1[:, 8:9]).then_inc(sRS, 1)
            e.wait_ge(sRS, tt * 6 + 4)
            e.activation(st1[:, 7:8], st1[:, 8:9], AF.Ln, bias=ctx.ceps2[:],
                         scale=1.0 / 512.0).then_inc(sRS, 1)
            e.wait_ge(sRS, tt * 6 + 5)
            e.activation(st1[:, 9:10], st1[:, 7:8], AF.Exp,
                         scale=-0.5).then_inc(sRS, 1)
            for grp in range(3):
                dst = ctstg[:, 0:512] if grp == 0 else \
                    ctstg[:, 512:1024] if grp == 1 else ktstg
                e.wait_ge(sTR, tt * 3 + grp + 1)
                e.wait_ge(sST, tt * 32)
                e.activation(dst, ctx.ppb(4 + grp % 2)[:], AF.Copy).then_inc(sTE, 1)

    def f_sync(e):
        with e.Fori(0, NTT) as tt:
            e.wait_ge(sTE, tt * 3 + 2)
            e.dma_start(out=cqT4[:, :, bass.ds(tt * 128, 128)],
                        in_=ctstg.rearrange("p (c q) -> p c q", q=128)
                        ).then_inc(sST, 16)
            e.wait_ge(sTE, tt * 3 + 3)
            e.dma_start(out=ckvT4[:, :, bass.ds(tt * 128, 128)],
                        in_=ktstg.rearrange("p (c q) -> p c q", q=128)
                        ).then_inc(sST, 16)
        e.wait_ge(sST, NTT * 32)

    ctx.stage(sync=f_sync, tensor=f_tensor, vector=f_vector,
              scalar=f_scalar, gpsimd=f_gpsimd)


def emit_fproj(ctx, wT_name, I, O, rhs_name, out_name, biasrows_name=None):
    nc = ctx.nc
    wT = ctx.dram[wT_name]
    out = ctx.dram[out_name]
    rhs_d = ctx.dram[rhs_name]
    br = ctx.dram[biasrows_name] if biasrows_name else None
    NK = I // 128
    NOC = O // 128
    WB = NK * O * 2
    NB = 2 if WB + NK * 2048 + 4096 <= 57344 - 2560 else 1
    assert WB + NB * NK * 1024 + 4096 <= 57344 - 2560 + 4096
    assert NOC % 2 == 0

    sWL = ctx.sem("fpWL", "sw")
    sBL = ctx.sem("fpBL", "sw")
    sRL = [ctx.sem("fpRL0", "sw"), ctx.sem("fpRL1", "sw")]
    sMM = ctx.sem("fpMM")
    sEV = ctx.sem("fpEV")
    sST = [ctx.sem("fpST0", "hw"), ctx.sem("fpST1", "hw")]

    wA = ctx.a2(BF16, 0, NK * O)
    rslc = [ctx.a2(BF16, WB + b * NK * 1024, NK * 512) for b in range(NB)]
    ostg = [ctx.a2(BF16, WB + NB * NK * 1024, 512),
            ctx.a2(BF16, WB + NB * NK * 1024 + 1024, 512)]
    bias_sb = ctx.a2(F32, WB + NB * NK * 1024 + 2048, 128)
    o4 = out.rearrange("(c p) t -> c p t", p=128)
    r4 = rhs_d.rearrange("(c p) t -> p c t", p=128)

    def f_gpsimd(e):
        e.dma_start(out=wA.rearrange("p (c o) -> p c o", o=O),
                    in_=wT.rearrange("(c p) o -> p c o", p=128)).then_inc(sWL, 16)
        if br is not None:
            e.dma_start(out=bias_sb[:, 0:NOC],
                        in_=br.rearrange("c p -> p c"),
                        allow_slow_non_contiguous=True).then_inc(sBL, 16)
        with e.Fori(0, 4) as i:
            for b in range(2):
                e.wait_ge(sMM, (i * 2 + b) * NOC + (NOC if NB == 1 else 0))
                e.dma_start(out=rslc[b % NB].rearrange("p (c u) -> p c u", u=512),
                            in_=r4[:, :, bass.ds((i * 2 + b) * 512, 512)]
                            ).then_inc(sRL[b % NB], 16)

    def f_tensor(e):
        e.nop().then_inc(sMM, NOC)
        e.wait_ge(sWL, 16)
        with e.Fori(0, 4) as i:
            for b in range(2):
                if NB == 2:
                    e.wait_ge(sRL[b], i * 16 + 16)
                else:
                    e.wait_ge(sRL[0], i * 32 + b * 16 + 16)
                for oc in range(NOC):
                    g = (i * 2 + b) * NOC + oc
                    if oc % 2 == 0:
                        e.wait_ge(sEV, g + 2)
                    for kc in range(NK):
                        ins_ = e.matmul(ctx.pp[oc % 2][:],
                                        wA[:, kc * O + oc * 128:
                                           kc * O + (oc + 1) * 128],
                                        rslc[b % NB][:, kc * 512:(kc + 1) * 512],
                                        start=(kc == 0), stop=(kc == NK - 1))
                    ins_.then_inc(sMM, 1)

    def f_scalar(e):
        e.nop().then_inc(sEV, 2)
        if br is not None:
            e.wait_ge(sBL, 16)
        with e.Fori(0, 4) as i:
            for b in range(2):
                for oc in range(NOC):
                    g = (i * 2 + b) * NOC + oc
                    e.wait_ge(sMM, NOC + g + 1)
                    e.wait_ge(sST[oc % 2],
                              ((i * 2 + b) * (NOC // 2) + oc // 2) * 16)
                    if br is not None:
                        ins_ = e.activation(ostg[oc % 2], ctx.pp[oc % 2][:],
                                            AF.Identity,
                                            bias=bias_sb[:, oc: oc + 1])
                    else:
                        ins_ = e.activation(ostg[oc % 2], ctx.pp[oc % 2][:],
                                            AF.Copy)
                    ins_.then_inc(sEV, 1)

    def f_sync(e):
        with e.Fori(0, 4) as i:
            for b in range(2):
                for oc in range(NOC):
                    g = (i * 2 + b) * NOC + oc
                    e.wait_ge(sEV, g + 3)
                    e.dma_start(out=o4[oc][:, bass.ds((i * 2 + b) * 512, 512)],
                                in_=ostg[oc % 2]).then_inc(sST[oc % 2], 16)
        e.wait_ge(sST[0], 8 * (NOC // 2) * 16)
        e.wait_ge(sST[1], 8 * (NOC // 2) * 16)

    ctx.stage(sync=f_sync, tensor=f_tensor, scalar=f_scalar, gpsimd=f_gpsimd)


def emit_mla_assemble(ctx, kvT_name, kropeO_name, kT_name, v_name):
    nc = ctx.nc
    kvT = ctx.dram[kvT_name]
    krope = ctx.dram[kropeO_name]
    kT = ctx.dram[kT_name]
    v = ctx.dram[v_name]

    sCP = ctx.sem("maCP", "sw")
    sVL = ctx.sem("maVL", "sw")
    sTR = ctx.sem("maTR")
    sTE = ctx.sem("maTE")
    sVS = ctx.sem("maVS", "hw")

    ksb = ctx.a2(BF16, 0, 4096)
    vstg = ctx.a2(BF16, 8192, 4096)

    def f_gpsimd(e):
        with e.Fori(0, 16) as h:
            e.dma_start(out=kT[bass.ds(h * 128, 64), :],
                        in_=kvT[bass.ds(h * 192, 64), :]).then_inc(sCP, 16)
            e.dma_start(out=kT[bass.ds(h * 128 + 64, 64), :],
                        in_=krope[bass.ds(h * 64, 64), :]).then_inc(sCP, 16)
            e.wait_ge(sTR, h * 8)
            e.dma_start(out=ksb, in_=kvT[bass.ds(h * 192 + 64, 128), :]
                        ).then_inc(sVL, 16)
        e.wait_ge(sCP, 16 * 32)

    def f_tensor(e):
        with e.Fori(0, 16) as h:
            e.wait_ge(sVL, h * 16 + 16)
            for g in range(8):
                e.wait_ge(sTE, h * 8 + g)
                for q4 in range(4):
                    tb = g * 4 + q4
                    ins_ = e.transpose(ctx.ppb(g % 2)[:, q4 * 128:(q4 + 1) * 128],
                                       ksb[:, tb * 128:(tb + 1) * 128],
                                       ctx.ident[:])
                ins_.then_inc(sTR, 1)

    def f_vector(e):
        with e.Fori(0, 16) as h:
            e.wait_ge(sVS, h * 16)
            for g in range(8):
                e.wait_ge(sTR, h * 8 + g + 1)
                e.tensor_copy(vstg[:, g * 512:(g + 1) * 512],
                              ctx.ppb(g % 2)[:]).then_inc(sTE, 1)

    def f_sync(e):
        v5 = v.rearrange("(n p) (g q) -> g p n q", p=128, q=128)
        with e.Fori(0, 16) as h:
            e.wait_ge(sTE, h * 8 + 8)
            e.dma_start(out=v5[bass.ds(h, 1)][0],
                        in_=vstg.rearrange("p (n q) -> p n q", q=128)
                        ).then_inc(sVS, 16)
        e.wait_ge(sVS, 16 * 16)

    ctx.stage(sync=f_sync, tensor=f_tensor, vector=f_vector, gpsimd=f_gpsimd)


PREP_OUT = {
    "l0a": [("l0a_qdwT", [2048, 1024]), ("l0a_quwT", [1024, 2048]),
            ("l0a_kdwT", [2048, 512]), ("l0a_kuwT", [512, 3072]),
            ("l0a_krwT", [2048, 1024]), ("l0a_owT", [2048, 2048])],
    "l0f": [("l0f_wguT", [2048, 16384]), ("l0f_wpT", [8192, 2048])],
    "l1a": [("l1a_wfT", [2048, 2048]), ("l1a_wzgT", [2048, 2048]),
            ("l1a_wzT", [2048, 2048]), ("l1a_wgaT", [2048, 2048]),
            ("l1a_wgbT", [2048, 2048]), ("l1a_woutT", [2048, 2048])],
    "l1f": [("l1f_rT", [2048, 8]), ("l1f_wgT", [2048, 6144]),
            ("l1f_wuT", [2048, 6144]), ("l1f_wdT", [768, 16384])],
}
PREP_RAW = {
    "l0a": [("l0a_qdw", [1024, 2048]), ("l0a_qnw", [1024]),
            ("l0a_quw", [2048, 1024]), ("l0a_kdw", [512, 2048]),
            ("l0a_knw", [512]), ("l0a_kuw", [3072, 512]),
            ("l0a_krw", [1024, 2048]), ("l0a_ow", [2048, 2048])],
    "l0f": [("l0f_wgu", [16384, 2048]), ("l0f_wp", [2048, 8192])],
    "l1a": [("l1a_wf", [2048, 2048]), ("l1a_wzg", [2048, 2048]),
            ("l1a_wz", [2048, 2048]), ("l1a_wga", [2048, 2048]),
            ("l1a_wgb", [2048, 2048]), ("l1a_wout", [2048, 2048]),
            ("l1a_rms", [2048])],
    "l1f": [("l1f_wr", [8, 2048]), ("l1f_wg", [6144, 2048]),
            ("l1f_wu", [6144, 2048]), ("l1f_wd", [16384, 768])],
}
SUB_FAM = {0: "l0a", 1: "l0f", 2: None, 3: "l0f", 4: "l1a", 5: "l1f",
           6: None, 7: "l1f"}


def emit_prep_family(ctx, fam):
    if fam == "l0a":
        emit_wtrans(ctx, "l0a_qdw", "l0a_qdwT", 1024, 2048)
        emit_wtrans(ctx, "l0a_quw", "l0a_quwT", 2048, 1024,
                    scale_name="l0a_qnw")
        emit_wtrans(ctx, "l0a_kdw", "l0a_kdwT", 512, 2048)
        emit_wtrans(ctx, "l0a_kuw", "l0a_kuwT", 3072, 512,
                    scale_name="l0a_knw")
        emit_wtrans(ctx, "l0a_krw", "l0a_krwT", 1024, 2048)
        emit_wtrans(ctx, "l0a_ow", "l0a_owT", 2048, 2048)
    elif fam == "l0f":
        emit_wtrans(ctx, "l0f_wgu", "l0f_wguT", 16384, 2048, quant=True)
        emit_wtrans(ctx, "l0f_wp", "l0f_wpT", 2048, 8192, quant=True)
    elif fam == "l1a":
        for n in ["wf", "wzg", "wz", "wga", "wgb"]:
            emit_wtrans(ctx, f"l1a_{n}", f"l1a_{n}T", 2048, 2048,
                        scale_name="l1a_rms")
        emit_wtrans(ctx, "l1a_wout", "l1a_woutT", 2048, 2048)
    elif fam == "l1f":
        emit_router_prep(ctx, "l1f_wr", "l1f_rT")
        emit_wtrans(ctx, "l1f_wg", "l1f_wgT", 6144, 2048)
        emit_wtrans(ctx, "l1f_wu", "l1f_wuT", 6144, 2048)
        emit_wtrans(ctx, "l1f_wd", "l1f_wdT", 16384, 768)


def build_preps(ctx, fams):
    innames = []
    for fam in fams:
        for n, shp in PREP_RAW[fam]:
            innames.append(n); ctx.inp(n, shp)
        for n, shp in PREP_OUT[fam]:
            ctx.outp(n, shp, BF16)
        emit_prep_family(ctx, fam)
    return innames, [n for fam in fams for n, _ in PREP_OUT[fam]]


def build_module(ctx, lo=0, hi=8, preps="inline"):
    T_, D_ = T, D
    NH, DK, DC, DCQ, DR, DN = 16, 128, 512, 1024, 64, 64
    E, F_ = 8, 768
    innames = []

    def inp(n, shp, dt=F32):
        innames.append(n)
        return ctx.inp(n, shp, dt)

    subs = range(lo, hi)
    need = set(SUB_FAM[i] for i in subs) - {None}
    if 0 in subs:
        need.add("l0a")
    if 2 in subs:
        need.add("l0b")
    if 6 in subs:
        need.add("l1b")

    sc = ctx.scratch
    bn = {}
    for i in range(lo, hi + 1):
        if i == lo:
            bn[i] = "x0"; inp("x0", [T_, D_])
        elif i == hi:
            bn[i] = "xOUT"; ctx.outp("xOUT", [T_, D_])
        else:
            bn[i] = f"xm{i}"; sc(bn[i], [T_, D_], F32)

    for fam in sorted(need & set(PREP_OUT)):
        if preps == "import":
            for n, shp in PREP_OUT[fam]:
                inp(n, shp, BF16)
        else:
            for n, shp in PREP_RAW[fam]:
                inp(n, shp)
            for n, shp in PREP_OUT[fam]:
                sc(n, shp, BF16)
            emit_prep_family(ctx, fam)

    if "l0a" in need:
        inp("l0a_bq_row", [1, DCQ]); inp("l0a_bkv_row", [1, DC])
        inp("l0a_qub_rows", [16, 128]); inp("l0a_kub_rows", [24, 128])
        inp("l0a_krb_rows", [8, 128]); inp("l0a_ob_tile", [128, D_])
        for n, shp in [("xnTd", [D_, T_]), ("cqT", [DCQ, T_]),
                       ("ckvT", [DC, T_]), ("qTf", [NH * DK, T_]),
                       ("kvTf", [NH * (DN + DK), T_]),
                       ("krTf", [NH * DR, T_]), ("kTa", [NH * DK, T_]),
                       ("va", [T_, NH * DK])]:
            sc(n, shp, BF16)
    if "l0f" in need:
        sc("ffn_hT", [8192, T_], BF16); sc("ffn_rstdT", [32, 128], F32)
    if "l0b" in need:
        inp("l0b_wqkv", [D_, 3 * D_]); inp("l0b_wproj", [D_, D_])
        inp("l0b_qkb", [32, 128]); inp("l0b_vbias", [128, D_])
        inp("l0b_pbias", [128, D_])
        sc("qkT", [4096, T_], BF16); sc("l0b_v", [T_, D_], BF16)
    if "l1a" in need:
        inp("l1a_fb_rows", [16, 128]); inp("l1a_ob_tile", [128, D_])
        sc("gated", [2048, T_], BF16); sc("xpl", [T_, D_], F32)
    if "l1f" in need:
        sc("moe_w", [E, T_], F32); sc("moe_hT", [E * F_, T_], BF16)
    if "l1b" in need:
        inp("l1b_wcat", [D_, 3072]); inp("l1b_wout", [D_, D_])
        inp("l1b_qcos", [128, T_]); inp("l1b_qsin", [128, T_])
        inp("l1b_kcos", [128, T_]); inp("l1b_ksin", [128, T_])
        sc("l1b_qkraw", [2560, T_], BF16); sc("l1b_v", [T_, 512], BF16)
        sc("l1b_qT", [2048, T_], BF16); sc("l1b_kT", [512, T_], BF16)
    if need & {"l0a", "l0b", "l1a", "l1b"}:
        sc("attnT", [D_, T_], BF16)

    def ffn(x_in, x_out):
        emit_ln(ctx, x_in)
        emit_ffn_gu(ctx, "l0f_wguT", "ffn_hT", "ffn_rstdT")
        emit_ffn_down(ctx, "ffn_hT", "l0f_wpT", "ffn_rstdT", x_in, x_out)

    def moe(x_in, x_out):
        emit_ln(ctx, x_in)
        emit_moe_router(ctx, "l1f_rT", "moe_w")
        emit_moe_gu(ctx, "l1f_wgT", "l1f_wuT", "moe_hT")
        emit_moe_scale(ctx, "moe_hT", "moe_w")
        emit_moe_down(ctx, "moe_hT", "l1f_wdT", x_in, x_out)

    def sub0(x_in, x_out):
        emit_ln(ctx, x_in, xnT_dram="xnTd")
        emit_mla_cqkv(ctx, "l0a_qdwT", "l0a_kdwT", "l0a_bq_row",
                      "l0a_bkv_row", "xnTd", "cqT", "ckvT")
        emit_fproj(ctx, "l0a_quwT", DCQ, NH * DK, "cqT", "qTf",
                   "l0a_qub_rows")
        emit_fproj(ctx, "l0a_kuwT", DC, NH * (DN + DK), "ckvT", "kvTf",
                   "l0a_kub_rows")
        emit_fproj(ctx, "l0a_krwT", D_, NH * DR, "xnTd", "krTf",
                   "l0a_krb_rows")
        emit_mla_assemble(ctx, "kvTf", "krTf", "kTa", "va")
        emit_attn_core2(ctx, "qTf", "kTa", "va", "attnT", 16, 16, True)
        emit_proj_resid(ctx, "attnT", "l0a_owT", x_in, x_out, "l0a_ob_tile")

    def sub2(x_in, x_out):
        emit_ln(ctx, x_in)
        emit_mha_qkv(ctx, "l0b_wqkv", "qkT", "l0b_v", "l0b_qkb", "l0b_vbias")
        ctx.dram["l0b_qT_v"] = ctx.dram["qkT"][0:2048, :]
        ctx.dram["l0b_kT_v"] = ctx.dram["qkT"][2048:4096, :]
        emit_attn_core2(ctx, "l0b_qT_v", "l0b_kT_v", "l0b_v", "attnT",
                       16, 16, True)
        emit_proj_resid(ctx, "attnT", "l0b_wproj", x_in, x_out, "l0b_pbias")

    def sub4(x_in, x_out):
        emit_ln(ctx, x_in, mode="ln_rms", xplus_dram="xpl")
        emit_l1a_core(ctx, ["l1a_wfT", "l1a_wzgT", "l1a_wzT", "l1a_wgaT",
                            "l1a_wgbT"], "l1a_fb_rows", "gated")
        emit_proj_resid(ctx, "gated", "l1a_woutT", "xpl", x_out,
                        "l1a_ob_tile")

    def sub6(x_in, x_out):
        emit_ln(ctx, x_in)
        emit_mha_qkv(ctx, "l1b_wcat", "l1b_qkraw", "l1b_v", None, None,
                     n_qk_chunks=20, v_col0=2560, n_v_cols=512)
        emit_l1b_rmsrope(ctx, "l1b_qkraw", "l1b_qT", "l1b_kT")
        emit_attn_core2(ctx, "l1b_qT", "l1b_kT", "l1b_v", "attnT", 16, 4,
                       causal=False, scale=1.0)
        emit_proj_resid(ctx, "attnT", "l1b_wout", x_in, x_out, None)

    emitters = {0: sub0, 1: ffn, 2: sub2, 3: ffn, 4: sub4, 5: moe,
                6: sub6, 7: moe}
    for i in subs:
        emitters[i](bn[i], bn[i + 1])
    return innames


PLAN = "fused"
PLAN_MULTI = [("prep", ["l0a"]), ("prep", ["l0f"]), ("prep", ["l1a"]),
              ("prep", ["l1f"])] + [("subs", i, i + 1) for i in range(8)]
_PROG_CACHE = {}


def _get_program(key):
    if key not in _PROG_CACHE:
        es = ExitStack()
        nc = bacc.Bacc("TRN2", target_bir_lowering=False)
        ctx = Ctx(nc, es)
        with es:
            if key[0] == "prep":
                innames, outnames = build_preps(ctx, list(key[1:]))
            elif key[0] == "full":
                innames = build_module(ctx, 0, 8, preps="inline")
                outnames = ["xOUT"]
            else:
                innames = build_module(ctx, key[1], key[2], preps="import")
                outnames = ["xOUT"]
            finalize(ctx)
        nc.finalize()
        _PROG_CACHE[key] = (nc, dict(ctx.consts), innames, outnames)
    return _PROG_CACHE[key]


def host_inputs(inputs):
    H = 128
    fraction = np.arange(0, H, 2, dtype=np.float32) / H
    timescale = (10000.0 ** fraction).astype(np.float32)
    pos = np.arange(T, dtype=np.float32)
    sinus = np.einsum("i,j->ij", pos, 1.0 / timescale)
    sin64 = np.sin(sinus).astype(np.float32)
    cos64 = np.cos(sinus).astype(np.float32)
    qscale = np.float32(1.0) / np.sqrt(np.float32(H)).astype(np.float32)
    i = inputs
    base = {
        "l0a_qdw": i["L0_attn_a_q_down_w"], "l0a_qnw": i["L0_attn_a_q_norm_w"],
        "l0a_quw": i["L0_attn_a_q_up_w"], "l0a_kdw": i["L0_attn_a_kv_down_w"],
        "l0a_knw": i["L0_attn_a_kv_norm_w"], "l0a_kuw": i["L0_attn_a_kv_up_w"],
        "l0a_krw": i["L0_attn_a_k_rope_w"], "l0a_ow": i["L0_attn_a_out_w"],
        "l0b_wqkv": i["L0_attn_b_W_qkv"], "l0b_wproj": i["L0_attn_b_W_proj"],
        "l0f_wgu": i["L0_ffn_w_gate_up"], "l0f_wp": i["L0_ffn_w_proj"],
        "l1a_wf": i["L1_attn_a_Wf_w"], "l1a_wzg": i["L1_attn_a_Wzg_w"],
        "l1a_wz": i["L1_attn_a_Wz_w"], "l1a_wga": i["L1_attn_a_Wga_w"],
        "l1a_wgb": i["L1_attn_a_Wgb_w"], "l1a_wout": i["L1_attn_a_Wout_w"],
        "l1a_rms": i["L1_attn_a_rms_w"],
        "l1b_wcat": np.concatenate([i["L1_attn_b_Wq"], i["L1_attn_b_Wk"],
                                    i["L1_attn_b_Wv"]], axis=1),
        "l1b_wout": i["L1_attn_b_Wout"],
        "l1f_wr": i["L1_ffn_w_gate_router"],
        "l1f_wg": np.ascontiguousarray(i["L1_ffn_w_gate_proj"]).reshape(6144, 2048),
        "l1f_wu": np.ascontiguousarray(i["L1_ffn_w_up_proj"]).reshape(6144, 2048),
        "l1f_wd": np.ascontiguousarray(i["L1_ffn_w_down_proj"]).reshape(16384, 768),
        "l0a_bq_row": np.ascontiguousarray(i["L0_attn_a_q_down_b"]).reshape(1, 1024),
        "l0a_bkv_row": np.ascontiguousarray(i["L0_attn_a_kv_down_b"]).reshape(1, 512),
        "l0a_qub_rows": np.ascontiguousarray(i["L0_attn_a_q_up_b"]).reshape(16, 128),
        "l0a_kub_rows": np.ascontiguousarray(i["L0_attn_a_kv_up_b"]).reshape(24, 128),
        "l0a_krb_rows": np.ascontiguousarray(i["L0_attn_a_k_rope_b"]).reshape(8, 128),
        "l0a_ob_tile": np.broadcast_to(i["L0_attn_a_out_b"], (128, 2048)),
        "l0b_qkb": np.ascontiguousarray(i["L0_attn_b_b_qkv"][:4096]).reshape(32, 128),
        "l0b_vbias": np.broadcast_to(i["L0_attn_b_b_qkv"][4096:], (128, 2048)),
        "l0b_pbias": np.broadcast_to(i["L0_attn_b_b_proj"], (128, 2048)),
        "l1a_fb_rows": np.ascontiguousarray(i["L1_attn_a_Wf_b"]).reshape(16, 128),
        "l1a_ob_tile": np.broadcast_to(i["L1_attn_a_Wout_b"], (128, 2048)),
        "l1b_qcos": np.concatenate([cos64, cos64], 1).T * qscale,
        "l1b_qsin": np.concatenate([sin64, sin64], 1).T * qscale,
        "l1b_kcos": np.concatenate([cos64, cos64], 1).T,
        "l1b_ksin": np.concatenate([sin64, sin64], 1).T,
    }
    return {k: np.ascontiguousarray(v, dtype=np.float32)
            for k, v in base.items()}


def kernel(**inputs):
    base = host_inputs(inputs)
    X = np.ascontiguousarray(inputs["L0_attn_a_x"], dtype=np.float32)
    shards = [X[c] for c in range(8)]
    if PLAN == "fused":
        nc, consts, innames, outnames = _get_program(("full",))
        feed = dict(consts)
        feed.update({k: base[k] for k in innames if k != "x0"})
        in_maps = [dict(feed, x0=np.ascontiguousarray(shards[c]))
                   for c in range(8)]
        res = run_bass_kernel_spmd(nc, in_maps, core_ids=list(range(8)))
        shards = [np.asarray(res.results[c]["xOUT"]) for c in range(8)]
        return np.stack(shards, axis=0).astype(np.float32)
    for entry in PLAN_MULTI:
        if entry[0] == "prep":
            key = tuple(entry[:1]) + tuple(entry[1])
            nc, consts, innames, outnames = _get_program(key)
            feed = dict(consts)
            feed.update({k: base[k] for k in innames})
            res = run_bass_kernel_spmd(nc, [feed], core_ids=[0])
            for n in outnames:
                base[n] = np.asarray(res.results[0][n])
        else:
            key = entry
            nc, consts, innames, outnames = _get_program(key)
            feed = dict(consts)
            feed.update({k: base[k] for k in innames if k != "x0"})
            in_maps = [dict(feed, x0=np.ascontiguousarray(shards[c]))
                       for c in range(8)]
            res = run_bass_kernel_spmd(nc, in_maps, core_ids=list(range(8)))
            shards = [np.asarray(res.results[c]["xOUT"]) for c in range(8)]
    return np.stack(shards, axis=0).astype(np.float32)


# version 13
# speedup vs baseline: 1.0093x; 1.0093x over previous
import numpy as np
import ml_dtypes
import concourse.bass as bass
import concourse.bacc as bacc
import concourse.mybir as mybir
from concourse.bass_utils import run_bass_kernel_spmd
from contextlib import ExitStack

F32 = mybir.dt.float32
BF16 = mybir.dt.bfloat16
AF = mybir.ActivationFunctionType
ALU = mybir.AluOpType
AX = mybir.AxisListType
bfdt = ml_dtypes.bfloat16

T = 4096
D = 2048
NTT = T // 128
NDC = D // 128
LN_EPS = 1e-5

ENGINES = ("sync", "tensor", "vector", "scalar", "gpsimd")


class Ctx:
    def __init__(self, nc, es):
        self.nc = nc
        self.es = es
        self.stages = []
        self.consts = {}
        self.dram = {}
        self.A1 = es.enter_context(nc.sbuf_tensor("A1", [128, 131072], mybir.dt.uint8))
        self.A2 = es.enter_context(nc.sbuf_tensor("A2", [128, 57344], mybir.dt.uint8))
        self.xb = [es.enter_context(nc.sbuf_tensor(f"xb{i}", [128, D], F32)) for i in range(2)]
        self.st1 = es.enter_context(nc.sbuf_tensor("st1", [128, 32], F32))
        self.ident = es.enter_context(nc.sbuf_tensor("ident", [128, 128], BF16))
        self.onesb = es.enter_context(nc.sbuf_tensor("onesb", [1, 128], BF16))
        self.ceps = es.enter_context(nc.sbuf_tensor("ceps", [128, 1], F32))
        self.identf = es.enter_context(nc.sbuf_tensor("identf", [128, 128], F32))
        self.onesf = es.enter_context(nc.sbuf_tensor("onesf", [1, 128], F32))
        self.onescol = es.enter_context(nc.sbuf_tensor("onescol", [128, 1], BF16))
        self.ceps2 = es.enter_context(nc.sbuf_tensor("ceps2", [128, 1], F32))
        self.ceps3 = es.enter_context(nc.sbuf_tensor("ceps3", [128, 1], F32))
        self.roper = es.enter_context(nc.sbuf_tensor("roper", [128, 128], BF16))
        self.addconst("c_ident", np.eye(128, dtype=bfdt))
        self.addconst("c_ones", np.ones((1, 128), dtype=bfdt))
        self.addconst("c_eps", np.full((128, 1), LN_EPS, np.float32))
        self.addconst("c_identf", np.eye(128, dtype=np.float32))
        self.addconst("c_onesf", np.ones((1, 128), np.float32))
        self.addconst("c_onescol", np.ones((128, 1), dtype=bfdt))
        self.addconst("c_eps2", np.full((128, 1), 1.1920929e-07, np.float32))
        self.addconst("c_eps3", np.full((128, 1), 1e-6, np.float32))
        rr = np.zeros((128, 128), dtype=bfdt)
        for i in range(64):
            rr[i + 64, i] = bfdt(-1.0)
            rr[i, i + 64] = bfdt(1.0)
        self.addconst("c_roper", rr)

        self.pp = [es.enter_context(nc.psum_tensor(f"pp{i}", [128, 512], F32))
                   for i in range(8)]
        self.BAR = es.enter_context(nc.semaphore("BAR"))
        self.spool = {
            "c": [[es.enter_context(nc.semaphore(f"spc{b}_{i}"))
                   for i in range(16)] for b in range(2)],
            "hw": [[es.enter_context(nc.semaphore(f"sph{b}_{i}"))
                    for i in range(13)] for b in range(2)],
            "sw": [[es.enter_context(nc.semaphore(f"spw{b}_{i}"))
                    for i in range(6)] for b in range(2)],
        }
        self.sp_used = {"c": 0, "hw": 0, "sw": 0}

    def ppb(self, i):
        return self.pp[i][:].bitcast(BF16)[:, 0:512]

    def a1(self, dt, off_bytes, n_elem):
        esz = 2 if dt == BF16 else 4
        assert off_bytes % esz == 0
        return self.A1[:].bitcast(dt)[:, off_bytes // esz: off_bytes // esz + n_elem]

    def a2(self, dt, off_bytes, n_elem):
        esz = 2 if dt == BF16 else 4
        assert off_bytes % esz == 0
        return self.A2[:].bitcast(dt)[:, off_bytes // esz: off_bytes // esz + n_elem]

    def addconst(self, name, arr):
        if name not in self.consts:
            self.consts[name] = arr
            dt = BF16 if arr.dtype == bfdt else F32
            self.dram[name] = self.nc.dram_tensor(name, list(arr.shape), dt,
                                                  kind="ExternalInput").ap()
        return self.dram[name]

    def sem(self, name, kind="c"):
        b = len(self.stages) % 2
        pool = self.spool[kind][b]
        assert self.sp_used[kind] < len(pool), \
            f"{kind} sem pool exhausted at {name}"
        h = pool[self.sp_used[kind]]
        self.sp_used[kind] += 1
        return h

    def scratch(self, name, shape, dt):
        if name not in self.dram:
            self.dram[name] = self.nc.dram_tensor(name, list(shape), dt,
                                                  kind="Internal").ap()
        return self.dram[name]

    def inp(self, name, shape, dt=F32):
        if name not in self.dram:
            self.dram[name] = self.nc.dram_tensor(name, list(shape), dt,
                                                  kind="ExternalInput").ap()
        return self.dram[name]

    def outp(self, name, shape, dt=F32):
        if name not in self.dram:
            self.dram[name] = self.nc.dram_tensor(name, list(shape), dt,
                                                  kind="ExternalOutput").ap()
        return self.dram[name]

    def stage(self, **engine_fns):
        self.stages.append(engine_fns)
        self.sp_used = {"c": 0, "hw": 0, "sw": 0}


class _SemState:

    def __init__(self, pool_ids):
        self.pool = pool_ids
        self.base = {}
        self.delta = {}
        self.mult = []

    def base_of(self, sem):
        return self.base.get(id(sem), 0)

    def add_inc(self, sem, n):
        if id(sem) in self.pool:
            m = 1
            for t in self.mult:
                m *= t
            self.delta[id(sem)] = self.delta.get(id(sem), 0) + n * m

    def flush_stage(self):
        for i, d in self.delta.items():
            self.base[i] = self.base.get(i, 0) + d
        self.delta = {}
        assert not self.mult


class _InstrProxy:
    def __init__(self, ins, st):
        object.__setattr__(self, "_ins", ins)
        object.__setattr__(self, "_st", st)

    def then_inc(self, sem, n):
        self._st.add_inc(sem, n)
        r = self._ins.then_inc(sem, n)
        return _InstrProxy(r, self._st) if r is not None else r

    def __getattr__(self, name):
        return getattr(self._ins, name)


class _ForiProxy:
    def __init__(self, eng, st, a, b):
        self.cm = eng.Fori(a, b)
        self.st = st
        self.trip = b - a

    def __enter__(self):
        self.st.mult.append(self.trip)
        return self.cm.__enter__()

    def __exit__(self, *a):
        self.st.mult.pop()
        return self.cm.__exit__(*a)


class _EngProxy:
    def __init__(self, eng, st):
        object.__setattr__(self, "_e", eng)
        object.__setattr__(self, "_st", st)

    def Fori(self, a, b):
        return _ForiProxy(self._e, self._st, a, b)

    def wait_ge(self, sem, val):
        return self._e.wait_ge(sem, val + self._st.base_of(sem))

    def __getattr__(self, name):
        attr = getattr(self._e, name)
        if callable(attr):
            st = self._st

            def call(*a, **k):
                r = attr(*a, **k)
                return _InstrProxy(r, st) if hasattr(r, "then_inc") else r
            return call
        return attr


def finalize(ctx):
    nc = ctx.nc

    with nc.Block() as blk:
        def pre_sync(e):
            n = 0
            for nm, buf in (("c_ident", ctx.ident), ("c_ones", ctx.onesb),
                            ("c_eps", ctx.ceps), ("c_identf", ctx.identf),
                            ("c_onesf", ctx.onesf), ("c_onescol", ctx.onescol),
                            ("c_eps2", ctx.ceps2), ("c_eps3", ctx.ceps3),
                            ("c_roper", ctx.roper)):
                e.dma_start(out=buf[:], in_=ctx.dram[nm][:]).then_inc(ctx.BAR, 16)
                n += 16
            e.wait_ge(ctx.BAR, n)
        blk.sync(pre_sync)
    nc.all_engine_barrier()

    pool_ids = {id(h) for bank in ctx.spool["sw"] for h in bank}
    sst = _SemState(pool_ids)
    for k, st in enumerate(ctx.stages):
        with nc.Block() as blk:
            for ename in ENGINES:
                fn = st.get(ename)
                if fn is not None:
                    getattr(blk, ename)(
                        lambda e, fn=fn: fn(_EngProxy(e, sst)))
        sst.flush_stage()
        nc.all_engine_barrier()
        nc.gpsimd.drain()
        for kind in ("c", "hw"):
            for h in ctx.spool[kind][(k + 1) % 2]:
                nc.gpsimd.sem_clear(h)
        nc.all_engine_barrier()


def new_program():
    es = ExitStack()
    nc = bacc.Bacc("TRN2", target_bir_lowering=False)
    ctx = Ctx(nc, es)
    return nc, es, ctx


def emit_ln(ctx, x_name, mode="ln", xnT_dram=None, xplus_dram=None):
    nc = ctx.nc
    assert xplus_dram is None or mode == "ln_rms"
    sXD = ctx.sem("lnXD", "hw") if xnT_dram else None
    sLX = ctx.sem("lnLX") if xplus_dram else None
    sLXD = ctx.sem("lnLXD", "hw") if xplus_dram else None
    sLD = [ctx.sem("lnLD0", "hw"), ctx.sem("lnLD1", "hw")]
    sQ = ctx.sem("lnQ")
    sLN = ctx.sem("lnLN")
    sR = ctx.sem("lnR")
    sVS = ctx.sem("lnVS")
    sA = ctx.sem("lnA")
    sTR = ctx.sem("lnTR")
    sEV = ctx.sem("lnEV")

    x = ctx.dram[x_name].rearrange("(n p) d -> n p d", p=128)
    xb = ctx.xb
    st1 = ctx.st1
    xn = [ctx.a2(BF16, 0, D), ctx.a2(BF16, 4096, D)]
    dump = ctx.a2(BF16, 8192, D)
    lnb = [ctx.a2(F32, 12288, D), ctx.a2(F32, 20480, D)] if xplus_dram else None
    xnT3 = ctx.a1(BF16, 0, NDC * T).rearrange("p (c t) -> p c t", t=T)

    VN = 4 if mode == "ln" else 6
    SN = 1 if mode == "ln" else 4

    def col(b, j):
        return st1[:, b * 16 + j: b * 16 + j + 1]

    def f_sync(e):
        xd4 = ctx.dram[xnT_dram].rearrange("(c p) t -> p c t", p=128) \
            if xnT_dram else None
        xpl = ctx.dram[xplus_dram].rearrange("(n p) d -> n p d", p=128) \
            if xplus_dram else None
        with e.Fori(0, NTT // 2) as i:
            e.wait_ge(sA, i * 2)
            e.dma_start(out=xb[0][:], in_=x[i * 2]).then_inc(sLD[0], 16)
            e.wait_ge(sA, i * 2 + 1)
            e.dma_start(out=xb[1][:], in_=x[i * 2 + 1]).then_inc(sLD[1], 16)
            if xnT_dram is not None:
                for b in range(2):
                    e.wait_ge(sEV, i * 2 + b + 2)
                    e.dma_start(out=xd4[:, :, bass.ds((i * 2 + b) * 128, 128)],
                                in_=xnT3[:, :, bass.ds((i * 2 + b) * 128, 128)]
                                ).then_inc(sXD, 16)
            if xplus_dram is not None:
                for b in range(2):
                    e.wait_ge(sLX, (i * 2 + b) * 2 + 2)
                    e.dma_start(out=xpl[i * 2 + b],
                                in_=lnb[b]).then_inc(sLXD, 16)
        if xnT_dram is not None:
            e.wait_ge(sXD, NTT * 16)
        if xplus_dram is not None:
            e.wait_ge(sLXD, NTT * 16)

    def f_vector(e):
        e.nop().then_inc(sA, 1)
        e.nop().then_inc(sVS, VN)
        if mode == "ln_rms":
            e.nop().then_inc(sR, 1)
        with e.Fori(0, NTT // 2) as i:
            for b in range(2):
                it = i * 2 + b
                base = VN * it + VN
                e.wait_ge(sLD[b], i * 16 + 16)
                e.wait_ge(sR, it)
                e.reduce_sum(col(b, 0), xb[b][:], AX.X).then_inc(sVS, 1)
                e.wait_ge(sVS, base + 1)
                e.tensor_scalar(col(b, 2), col(b, 0), 1.0 / D, None,
                                ALU.mult).then_inc(sVS, 1)
                e.wait_ge(sVS, base + 2)
                e.tensor_tensor(col(b, 3), col(b, 2), col(b, 2),
                                ALU.mult).then_inc(sVS, 1)
                e.wait_ge(sQ, it + 1)
                e.wait_ge(sVS, base + 3)
                e.scalar_tensor_tensor(col(b, 4), col(b, 1), 1.0 / D,
                                       col(b, 3), ALU.mult,
                                       ALU.subtract).then_inc(sVS, 1)
                if mode == "ln_rms":
                    e.wait_ge(sLN, it * SN + 2)
                    e.tensor_tensor(col(b, 8), col(b, 4), col(b, 7),
                                    ALU.mult).then_inc(sVS, 1)
                    e.wait_ge(sLN, it * SN + 4)
                    e.tensor_tensor(col(b, 6), col(b, 11), col(b, 10),
                                    ALU.mult).then_inc(sVS, 1)
                    e.nop().then_inc(sR, 1)
                    e.wait_ge(sVS, base + 6)
                if xplus_dram is not None:
                    e.wait_ge(sLN, it * SN + 2)
                    e.wait_ge(sLXD, i * 32)
                    e.tensor_scalar(lnb[b], xb[b][:], col(b, 2), col(b, 11),
                                    ALU.subtract, ALU.mult).then_inc(sLX, 1)
                    e.wait_ge(sLX, it * 2 + 1)
                    e.tensor_tensor(lnb[b], lnb[b], xb[b][:],
                                    ALU.add).then_inc(sLX, 1)
                e.wait_ge(sR, it + 2)
                e.wait_ge(sTR, it)
                e.tensor_scalar(xn[b], xb[b][:], col(b, 2), col(b, 6),
                                ALU.subtract, ALU.mult).then_inc(sA, 1)

    def f_scalar(e):
        e.nop().then_inc(sEV, 1)
        if mode == "ln":
            e.nop().then_inc(sR, 1)
        with e.Fori(0, NTT // 2) as i:
            for b in range(2):
                it = i * 2 + b
                e.wait_ge(sVS, VN * it)
                e.wait_ge(sA, it)
                e.wait_ge(sLD[b], i * 16 + 16)
                e.activation(dump, xb[b][:], AF.Square,
                             accum_out=col(b, 1)).then_inc(sQ, 1)
                e.wait_ge(sVS, VN * it + VN + 4)
                e.activation(col(b, 5), col(b, 4), AF.Ln,
                             bias=ctx.ceps[:]).then_inc(sLN, 1)
                e.wait_ge(sLN, it * SN + 1)
                if mode == "ln":
                    e.activation(col(b, 6), col(b, 5), AF.Exp,
                                 scale=-0.5).then_inc(sR, 1)
                else:
                    e.activation(col(b, 11), col(b, 5), AF.Exp,
                                 scale=-0.5)
                    e.activation(col(b, 7), col(b, 5), AF.Exp,
                                 scale=-1.0).then_inc(sLN, 1)
                    e.wait_ge(sVS, VN * it + VN + 5)
                    e.activation(col(b, 9), col(b, 8), AF.Ln,
                                 bias=ctx.ceps[:]).then_inc(sLN, 1)
                    e.wait_ge(sLN, it * SN + 3)
                    e.activation(col(b, 10), col(b, 9), AF.Exp,
                                 scale=-0.5).then_inc(sLN, 1)
            for b in range(2):
                it = i * 2 + b
                e.wait_ge(sTR, it + 2)
                for g in range(4):
                    ins_ = e.activation(
                        xnT3[:, g * 4:(g + 1) * 4, bass.ts(it, 128)],
                        ctx.ppb(b * 4 + g)[:].rearrange("p (c q) -> p c q", q=128),
                        AF.Copy)
                ins_.then_inc(sEV, 1)

    def f_tensor(e):
        e.nop().then_inc(sTR, 1)
        with e.Fori(0, NTT // 2) as i:
            for b in range(2):
                it = i * 2 + b
                e.wait_ge(sA, it + 2)
                e.wait_ge(sEV, it)
                for g in range(4):
                    for c in range(4):
                        cc = g * 4 + c
                        ins_ = e.transpose(
                            ctx.ppb(b * 4 + g)[:, c * 128:(c + 1) * 128],
                            xn[b][:, cc * 128:(cc + 1) * 128],
                            ctx.ident[:])
                ins_.then_inc(sTR, 1)

    ctx.stage(sync=f_sync, vector=f_vector, tensor=f_tensor, scalar=f_scalar)




def emit_dump(ctx, view, out_name, shape, dt=BF16):
    o = ctx.outp(out_name, shape, dt)
    s = ctx.sem("dump", "hw")

    def f_sync(e):
        e.dma_start(out=o[:], in_=view).then_inc(s, 16)
        e.wait_ge(s, 16)

    ctx.stage(sync=f_sync)


SQRT_DK_INV = float(np.float32(1.0) / np.float32(np.sqrt(np.float32(128.0))))


def emit_mha_qkv(ctx, wqkv_name, qkT_name, v_name,
                 qkbias_name=None, vbias_name=None, n_qk_chunks=32,
                 v_col0=4096, n_v_cols=2048):
    nc = ctx.nc
    wqkv = ctx.dram[wqkv_name]
    qkT = ctx.dram[qkT_name]
    v = ctx.dram[v_name] if v_name else None
    qkb = ctx.dram[qkbias_name] if qkbias_name else None
    vb = ctx.dram[vbias_name] if vbias_name else None

    sW = [ctx.sem("qkvW0", "sw"), ctx.sem("qkvW1", "sw")]
    sMM = ctx.sem("qkvMM")
    sEVA = ctx.sem("qkvEVA")
    sEVB = ctx.sem("qkvEVB")
    sST = [ctx.sem("qkvST0", "hw"), ctx.sem("qkvST1", "hw")]
    sSTB = [ctx.sem("qkvSTB0", "hw"), ctx.sem("qkvSTB1", "hw")]
    sB1 = ctx.sem("qkvB1", "sw")
    sB2 = ctx.sem("qkvB2", "sw")
    sBC = [ctx.sem("qkvBC0", "sw"), ctx.sem("qkvBC1", "sw")]

    wcb = [ctx.a2(BF16, 0, 2048), ctx.a2(BF16, 4096, 2048)]
    qkbig = [ctx.a2(BF16, 8192, 4096), ctx.a2(BF16, 16384, 4096)]
    vwb = ctx.a2(BF16, 24576, 8192)
    biascol = ctx.a2(F32, 40960, 4)
    vbias = ctx.a2(F32, 41216, 2048)
    v_sb = [ctx.a2(BF16, 49408, 512), ctx.a2(BF16, 50432, 512)]
    xnT3 = ctx.a1(BF16, 0, NDC * T).rearrange("p (c t) -> p c t", t=T)
    w4 = wqkv.rearrange("(c p) o -> p c o", p=128)

    NQK = n_qk_chunks
    NVC = n_v_cols // 512
    NG_A = NQK * 8

    def f_gpsimd(e):
        if vb is not None:
            e.dma_start(out=vbias[:, 0:n_v_cols], in_=vb[:]).then_inc(sB2, 16)
        with e.Fori(0, NQK // 2) as o2:
            for p in range(2):
                e.wait_ge(sMM, o2 * 16 + p * 8)
                e.dma_start(out=wcb[p].rearrange("q (c r) -> q c r", r=128),
                            in_=w4[:, :, bass.ts(o2 * 2 + p, 128)]).then_inc(sW[p], 16)
                if qkb is not None:
                    e.wait_ge(sEVA, o2 * 16 + p * 8 + 2)
                    e.dma_start(out=biascol[:, p:p + 1],
                                in_=qkb[bass.ds(o2 * 2 + p, 1), :]).then_inc(sBC[p], 16)
        for vc in range(NVC):
            e.wait_ge(sMM, 8 + NG_A + vc * 32)
            e.dma_start(out=vwb.rearrange("q (c r) -> q c r", r=512),
                        in_=w4[:, :, v_col0 + vc * 512: v_col0 + (vc + 1) * 512]
                        ).then_inc(sW[0], 16)

    def f_tensor(e):
        e.nop().then_inc(sMM, 8)
        with e.Fori(0, NQK // 2) as o2:
            for p in range(2):
                e.wait_ge(sW[p], o2 * 16 + 16)
                for ts in range(8):
                    lg = p * 8 + ts
                    if ts % 2 == 0:
                        e.wait_ge(sEVA, o2 * 16 + lg + 2)
                    for dc in range(NDC):
                        ins_ = e.matmul(ctx.pp[ts % 2][:],
                                        wcb[p][:, dc * 128:(dc + 1) * 128],
                                        xnT3[:, dc, bass.ts(ts, 512)],
                                        start=(dc == 0), stop=(dc == NDC - 1))
                    ins_.then_inc(sMM, 1)
        for vc in range(NVC):
            e.wait_ge(sW[0], (NQK // 2) * 16 + (vc + 1) * 16)
            for tt in range(NTT):
                lb = vc * 32 + tt
                if lb >= 2:
                    e.wait_ge(sEVB, lb - 1)
                else:
                    e.wait_ge(sEVA, 2 + NG_A)
                for dc in range(NDC):
                    ins_ = e.matmul(ctx.pp[tt % 2][:],
                                    xnT3[:, dc, bass.ts(tt, 128)],
                                    vwb[:, dc * 512:(dc + 1) * 512],
                                    start=(dc == 0), stop=(dc == NDC - 1))
                ins_.then_inc(sMM, 1)

    def f_scalar(e):
        e.nop().then_inc(sEVA, 2)
        with e.Fori(0, NQK // 2) as o2:
            for p in range(2):
                e.wait_ge(sST[p], o2 * 16)
                if qkb is not None:
                    e.wait_ge(sBC[p], o2 * 16 + 16)
                for ts in range(8):
                    lg = p * 8 + ts
                    if ts % 2 == 0:
                        e.wait_ge(sMM, o2 * 16 + lg + 10)
                    if qkb is not None:
                        ins_ = e.activation(qkbig[p][:, ts * 512:(ts + 1) * 512],
                                            ctx.pp[ts % 2][:], AF.Identity,
                                            bias=biascol[:, p:p + 1])
                    else:
                        ins_ = e.activation(qkbig[p][:, ts * 512:(ts + 1) * 512],
                                            ctx.pp[ts % 2][:], AF.Copy)
                    ins_.then_inc(sEVA, 1)

    def f_vector(e):
        if vb is not None:
            e.wait_ge(sB2, 16)
        for vc in range(NVC):
            for tt in range(NTT):
                lb = vc * 32 + tt
                e.wait_ge(sMM, 8 + NG_A + lb + 1)
                e.wait_ge(sSTB[tt % 2], 16 * (lb // 2))
                if vb is not None:
                    ins_ = e.scalar_tensor_tensor(
                        v_sb[tt % 2], ctx.pp[tt % 2][:], 1.0,
                        vbias[:, vc * 512:(vc + 1) * 512],
                        ALU.mult, ALU.add)
                else:
                    ins_ = e.tensor_copy(v_sb[tt % 2], ctx.pp[tt % 2][:])
                ins_.then_inc(sEVB, 1)

    def f_sync(e):
        qkT4 = qkT.rearrange("(c p) t -> c p t", p=128)
        with e.Fori(0, NQK // 2) as o2:
            for p in range(2):
                e.wait_ge(sEVA, o2 * 16 + p * 8 + 10)
                e.dma_start(out=qkT4[o2 * 2 + p],
                            in_=qkbig[p]).then_inc(sST[p], 16)
        if v is not None:
            v4 = v.rearrange("(n p) o -> n p o", p=128)
            for vc in range(NVC):
                for tt in range(NTT):
                    lb = vc * 32 + tt
                    e.wait_ge(sEVB, lb + 1)
                    e.dma_start(out=v4[tt, :, vc * 512:(vc + 1) * 512],
                                in_=v_sb[tt % 2]).then_inc(sSTB[tt % 2], 16)
        e.wait_ge(sST[0], NQK * 8)
        e.wait_ge(sST[1], NQK * 8)
        if v is not None:
            e.wait_ge(sSTB[0], NVC * 256)
            e.wait_ge(sSTB[1], NVC * 256)

    ctx.stage(sync=f_sync, tensor=f_tensor, vector=f_vector,
              scalar=f_scalar, gpsimd=f_gpsimd)



def emit_attn_core(ctx, qT_name, kT_name, v_name, attnT_name,
                   n_heads=16, kv_heads=16, causal=True, scale=SQRT_DK_INV):
    nc = ctx.nc
    qT = ctx.dram[qT_name]
    kT = ctx.dram[kT_name]
    v = ctx.dram[v_name]
    attnT = ctx.dram[attnT_name]
    GRP = n_heads // kv_heads
    NQ = 8
    KC = lambda Q: (Q + 1) if causal else NQ

    for i in range(4):
        m = np.where(np.arange(512)[None, :] <= np.arange(128)[:, None] + i * 128,
                     0.0, -1e30).astype(np.float32)
        ctx.addconst(f"c_mask{i}", m)
    ctx.addconst("c_zeros", np.zeros((128, 512), np.float32))

    sQL = ctx.sem("atQL", "hw"); sKL = ctx.sem("atKL", "hw"); sVL = ctx.sem("atVL", "hw")
    sMK = ctx.sem("atMK", "hw")
    sSC = ctx.sem("atSC")
    sSE = ctx.sem("atSE")
    sNM = ctx.sem("atNM")
    sEX = ctx.sem("atEX")
    sRC = ctx.sem("atRC")
    sRT = ctx.sem("atRT")
    sRE = ctx.sem("atRE")
    sRB = ctx.sem("atRB")
    sRV = ctx.sem("atRV")
    sPT = ctx.sem("atPT")
    sPE = ctx.sem("atPE")
    sPV = ctx.sem("atPV")
    sAE = ctx.sem("atAE")
    sAS = ctx.sem("atAS", "hw")

    probsT = ctx.a1(BF16, 0, 16384)
    qh = ctx.a1(BF16, 32768, 4096)
    kh = ctx.a1(BF16, 40960, 4096)
    vh = ctx.a1(BF16, 49152, 4096)
    scores = [ctx.a1(F32, 57344 + s * 16384, 4096) for s in range(4)]
    rb = ctx.a1(F32, 122880, 512)
    probs = [ctx.a2(BF16, 0 + s * 8192, 4096) for s in range(4)]
    at_sb = [ctx.a2(BF16, 32768, 512), ctx.a2(BF16, 33792, 512)]
    reciprow = ctx.a2(F32, 34816, 512)
    masks = ctx.a2(F32, 36864, 2560)
    st1 = ctx.st1

    qT4 = qT.rearrange("(c p) t -> c p t", p=128)
    kT4 = kT.rearrange("(c p) t -> c p t", p=128)
    v5 = v.rearrange("(n p) (g q) -> g p n q", p=128, q=128)
    at4 = attnT.rearrange("(c p) t -> c p t", p=128)

    NSC = sum(4 * KC(Q) for Q in range(NQ))
    NPT = sum(4 * KC(Q) for Q in range(NQ))
    NSUB = 4 * NQ
    CPT = [sum(4 * KC(q) for q in range(Q)) for Q in range(NQ + 1)]

    def mk(kc, Q, sub):
        if causal and kc == Q:
            return masks[:, sub * 512:(sub + 1) * 512]
        return masks[:, 2048:2560]

    def heads_loop(e, body):
        if GRP == 1:
            with e.Fori(0, n_heads) as h:
                body(e, h, h, 0, True)
        else:
            with e.Fori(0, kv_heads) as g:
                with e.Fori(0, GRP) as hg:
                    body(e, g * GRP + hg, g, hg, False)

    def f_sync(e):
        for i in range(4):
            e.dma_start(out=masks[:, i * 512:(i + 1) * 512],
                        in_=ctx.dram[f"c_mask{i}"][:]).then_inc(sMK, 16)
        e.dma_start(out=masks[:, 2048:2560], in_=ctx.dram["c_zeros"][:]).then_inc(sMK, 16)
        if GRP == 1:
            with e.Fori(0, n_heads) as h:
                e.wait_ge(sSC, h * NSC)
                e.dma_start(out=kh, in_=kT4[h]).then_inc(sKL, 16)
                e.wait_ge(sPV, h * NQ)
                e.dma_start(out=vh.rearrange("p (n q) -> p n q", q=128),
                            in_=v5[h]).then_inc(sVL, 16)
                e.dma_start(out=qh, in_=qT4[h]).then_inc(sQL, 16)
                for Q in range(NQ):
                    e.wait_ge(sAE, h * NQ + Q + 3)
                    e.dma_start(out=at4[h, :, bass.ts(Q, 512)],
                                in_=at_sb[Q % 2]).then_inc(sAS, 16)
        else:
            with e.Fori(0, kv_heads) as g:
                e.wait_ge(sSC, g * GRP * NSC)
                e.dma_start(out=kh, in_=kT4[g]).then_inc(sKL, 16)
                e.wait_ge(sPV, g * GRP * NQ)
                e.dma_start(out=vh.rearrange("p (n q) -> p n q", q=128),
                            in_=v5[g]).then_inc(sVL, 16)
                with e.Fori(0, GRP) as hg:
                    h = g * GRP + hg
                    e.wait_ge(sSC, h * NSC)
                    e.dma_start(out=qh, in_=qT4[h]).then_inc(sQL, 16)
                    for Q in range(NQ):
                        e.wait_ge(sAE, h * NQ + Q + 3)
                        e.dma_start(out=at4[h, :, bass.ts(Q, 512)],
                                    in_=at_sb[Q % 2]).then_inc(sAS, 16)
        e.wait_ge(sAS, n_heads * NQ * 16)

    def t_body(e, h, g, hg, single):
        e.wait_ge(sQL, h * 16 + 16)
        e.wait_ge(sKL, g * 16 + 16)
        e.wait_ge(sVL, g * 16 + 16)
        gi = 0
        pt = 0
        for Q in range(NQ):
            kcq = KC(Q)
            for sub in range(4):
                for kc in range(kcq):
                    if gi % 2 == 0:
                        e.wait_ge(sSE, h * NSC + gi + 2)
                    e.matmul(ctx.pp[gi % 2][:],
                             qh[:, Q * 512 + sub * 128: Q * 512 + (sub + 1) * 128],
                             kh[:, kc * 512:(kc + 1) * 512],
                             start=True, stop=True).then_inc(sSC, 1)
                    gi += 1
            e.wait_ge(sRC, h * NQ * 4 + Q * 4 + 4)
            e.wait_ge(sRE, h * NQ + Q)
            for sub in range(4):
                ins_ = e.transpose(ctx.pp[7][0:1, sub * 128:(sub + 1) * 128],
                                   st1[:, 12 + sub:13 + sub], ctx.identf[:])
            ins_.then_inc(sRT, 1)
            e.wait_ge(sRE, h * NQ + Q + 1)
            e.wait_ge(sRV, h * NQ + Q)
            e.matmul(ctx.pp[6][:], ctx.onesf[:], reciprow[0:1, :],
                     start=True, stop=True).then_inc(sRB, 1)
            for tkc in range(4 * kcq):
                if pt % 2 == 0:
                    e.wait_ge(sPE, h * NPT + pt + 2)
                for sub in range(4):
                    e.wait_ge(sEX, h * NSUB + Q * 4 + sub + 5)
                    ins_ = e.transpose(
                        ctx.ppb(2 + pt % 2)[:, sub * 128:(sub + 1) * 128],
                        probs[sub][:, tkc * 128:(tkc + 1) * 128],
                        ctx.ident[:])
                ins_.then_inc(sPT, 1)
                pt += 1
            e.wait_ge(sAE, h * NQ + Q + 2)
            for tkc in range(4 * kcq):
                e.wait_ge(sPE, h * NPT + pt - 4 * kcq + tkc + 3)
                ins_ = e.matmul(ctx.pp[4 + Q % 2][:],
                                vh[:, tkc * 128:(tkc + 1) * 128],
                                probsT[:, tkc * 512:(tkc + 1) * 512],
                                start=(tkc == 0), stop=(tkc == 4 * kcq - 1))
            ins_.then_inc(sPV, 1)

    def v_body(e, h, g, hg, single):
        gi = 0
        pt = 0
        for Q in range(NQ):
            kcq = KC(Q)
            for sub in range(4):
                e.wait_ge(sEX, h * NSUB + Q * 4 + sub + 1)
                for kc in range(kcq):
                    e.wait_ge(sSC, h * NSC + gi + 1)
                    e.scalar_tensor_tensor(
                        scores[sub][:, kc * 512:(kc + 1) * 512],
                        ctx.pp[gi % 2][:], 1.0, mk(kc, Q, sub),
                        ALU.mult, ALU.add).then_inc(sSE, 1)
                    gi += 1
                e.wait_ge(sSE, 2 + h * NSC + gi)
                e.reduce_max(st1[:, 2 * sub:2 * sub + 1],
                             scores[sub][:, 0:kcq * 512], AX.X,
                             negate=True).then_inc(sNM, 1)
                e.wait_ge(sNM, h * NSUB * 2 + Q * 8 + sub * 2 + 1)
                e.tensor_scalar(st1[:, 2 * sub + 1:2 * sub + 2],
                                st1[:, 2 * sub:2 * sub + 1], scale, None,
                                ALU.mult).then_inc(sNM, 1)
            for sub in range(4):
                e.wait_ge(sEX, h * NSUB + Q * 4 + sub + 5)
                e.wait_ge(sRT, h * NQ + Q)
                e.reciprocal(st1[:, 12 + sub:13 + sub],
                             st1[:, 8 + sub:9 + sub]).then_inc(sRC, 1)
            e.wait_ge(sRB, h * NQ + Q + 1)
            e.tensor_copy(rb, ctx.pp[6][:]).then_inc(sRV, 1)
            e.wait_ge(sPV, h * NQ + Q)
            for tkc in range(4 * kcq):
                e.wait_ge(sPT, h * NPT + pt + 1)
                e.tensor_copy(probsT[:, tkc * 512:(tkc + 1) * 512],
                              ctx.ppb(2 + pt % 2)[:]).then_inc(sPE, 1)
                pt += 1
            e.wait_ge(sPV, h * NQ + Q + 1)
            e.wait_ge(sAS, (h * NQ + Q) * 16)
            e.wait_ge(sRV, h * NQ + Q + 1)
            e.tensor_tensor(at_sb[Q % 2], ctx.pp[4 + Q % 2][:], rb,
                            ALU.mult).then_inc(sAE, 1)

    def s_body(e, h, g, hg, single):
        for Q in range(NQ):
            kcq = KC(Q)
            for sub in range(4):
                e.wait_ge(sNM, h * NSUB * 2 + Q * 8 + sub * 2 + 2)
                e.wait_ge(sRC, h * NQ * 4 + Q * 4)
                e.wait_ge(sPT, h * NPT + CPT[Q])
                e.activation(probs[sub][:, 0:kcq * 512],
                             scores[sub][:, 0:kcq * 512], AF.Exp,
                             bias=st1[:, 2 * sub + 1:2 * sub + 2], scale=scale,
                             accum_out=st1[:, 8 + sub:9 + sub]).then_inc(sEX, 1)
            e.wait_ge(sRT, h * NQ + Q + 1)
            e.wait_ge(sRB, h * NQ + Q)
            e.activation(reciprow[0:1, :], ctx.pp[7][0:1, 0:512],
                         AF.Copy).then_inc(sRE, 1)

    def f_tensor(e):
        heads_loop(e, t_body)

    def f_vector(e):
        e.nop().then_inc(sSE, 2)
        e.nop().then_inc(sPE, 2)
        e.nop().then_inc(sAE, 2)
        e.wait_ge(sMK, 80)
        heads_loop(e, v_body)

    def f_scalar(e):
        e.nop().then_inc(sEX, 4)
        heads_loop(e, s_body)

    ctx.stage(sync=f_sync, tensor=f_tensor, vector=f_vector, scalar=f_scalar)


def emit_attn_core2(ctx, qT_name, kT_name, v_name, attnT_name,
                    n_heads=16, kv_heads=16, causal=True, scale=SQRT_DK_INV):
    nc = ctx.nc
    qT = ctx.dram[qT_name]
    kT = ctx.dram[kT_name]
    v = ctx.dram[v_name]
    attnT = ctx.dram[attnT_name]
    GRP = n_heads // kv_heads
    NQ = 8
    KC = lambda Q: (Q + 1) if causal else NQ

    for i in range(4):
        m = np.where(np.arange(512)[None, :] <= np.arange(128)[:, None] + i * 128,
                     0.0, -1e30).astype(np.float32)
        ctx.addconst(f"c_mask{i}", m)
    ctx.addconst("c_zeros", np.zeros((128, 512), np.float32))

    sQL = ctx.sem("atQL", "hw"); sKL = ctx.sem("atKL", "hw"); sVL = ctx.sem("atVL", "hw")
    sMK = ctx.sem("atMK", "hw")
    sSC = ctx.sem("atSC")
    sSE = ctx.sem("atSE")
    sEX = ctx.sem("atEX")
    sRC = ctx.sem("atRC")
    sDG = ctx.sem("atDG")
    sPT = ctx.sem("atPT")
    sPE = ctx.sem("atPE")
    sPV = ctx.sem("atPV")
    sAE = ctx.sem("atAE")
    sAS = ctx.sem("atAS", "hw")

    probsT = ctx.a1(BF16, 0, 16384)
    qh = ctx.a1(BF16, 32768, 4096)
    kh = ctx.a1(BF16, 40960, 4096)
    vh = ctx.a1(BF16, 49152, 4096)
    scores = [ctx.a1(F32, 57344 + s * 16384, 4096) for s in range(4)]
    probs = [ctx.a2(BF16, 0 + s * 8192, 4096) for s in range(4)]
    at_sb = [ctx.a2(BF16, 32768, 512), ctx.a2(BF16, 33792, 512)]
    masks = ctx.a2(F32, 36864, 2560)
    diag = [[ctx.a2(BF16, 47104 + (p * 4 + s) * 256, 128) for s in range(4)]
            for p in range(2)]
    st1 = ctx.st1

    qT4 = qT.rearrange("(c p) t -> c p t", p=128)
    kT4 = kT.rearrange("(c p) t -> c p t", p=128)
    v5 = v.rearrange("(n p) (g q) -> g p n q", p=128, q=128)
    at4 = attnT.rearrange("(c p) t -> c p t", p=128)

    NSC = sum(4 * KC(Q) for Q in range(NQ))
    NPT = sum(4 * KC(Q) for Q in range(NQ))
    NSUB = 4 * NQ
    CPT = [sum(4 * KC(q) for q in range(Q)) for Q in range(NQ + 1)]

    def mk(kc, Q, sub):
        if causal and kc == Q:
            return masks[:, sub * 512:(sub + 1) * 512]
        return masks[:, 2048:2560]

    def heads_loop(e, body):
        if GRP == 1:
            with e.Fori(0, n_heads) as h:
                body(e, h, h, 0)
        else:
            with e.Fori(0, kv_heads) as g:
                with e.Fori(0, GRP) as hg:
                    body(e, g * GRP + hg, g, hg)

    def f_sync(e):
        for i in range(4):
            e.dma_start(out=masks[:, i * 512:(i + 1) * 512],
                        in_=ctx.dram[f"c_mask{i}"][:]).then_inc(sMK, 16)
        e.dma_start(out=masks[:, 2048:2560], in_=ctx.dram["c_zeros"][:]).then_inc(sMK, 16)
        if GRP == 1:
            with e.Fori(0, n_heads) as h:
                e.wait_ge(sSC, h * NSC)
                e.dma_start(out=kh, in_=kT4[h]).then_inc(sKL, 16)
                e.wait_ge(sPV, h * NQ)
                e.dma_start(out=vh.rearrange("p (n q) -> p n q", q=128),
                            in_=v5[h]).then_inc(sVL, 16)
                e.dma_start(out=qh, in_=qT4[h]).then_inc(sQL, 16)
                for Q in range(NQ):
                    e.wait_ge(sAE, h * NQ + Q + 3)
                    e.dma_start(out=at4[h, :, bass.ts(Q, 512)],
                                in_=at_sb[Q % 2]).then_inc(sAS, 16)
        else:
            with e.Fori(0, kv_heads) as g:
                e.wait_ge(sSC, g * GRP * NSC)
                e.dma_start(out=kh, in_=kT4[g]).then_inc(sKL, 16)
                e.wait_ge(sPV, g * GRP * NQ)
                e.dma_start(out=vh.rearrange("p (n q) -> p n q", q=128),
                            in_=v5[g]).then_inc(sVL, 16)
                with e.Fori(0, GRP) as hg:
                    h = g * GRP + hg
                    e.wait_ge(sSC, h * NSC)
                    e.dma_start(out=qh, in_=qT4[h]).then_inc(sQL, 16)
                    for Q in range(NQ):
                        e.wait_ge(sAE, h * NQ + Q + 3)
                        e.dma_start(out=at4[h, :, bass.ts(Q, 512)],
                                    in_=at_sb[Q % 2]).then_inc(sAS, 16)
        e.wait_ge(sAS, n_heads * NQ * 16)

    def t_scores(e, h, Q):
        gi = CPT[Q]
        kcq = KC(Q)
        for sub in range(4):
            for kc in range(kcq):
                if gi % 3 == 0:
                    e.wait_ge(sSE, h * NSC + gi + 3)
                e.matmul(ctx.pp[gi % 3][:],
                         qh[:, Q * 512 + sub * 128: Q * 512 + (sub + 1) * 128],
                         kh[:, kc * 512:(kc + 1) * 512],
                         start=True, stop=True).then_inc(sSC, 1)
                gi += 1

    def t_body(e, h, g, hg):
        e.wait_ge(sQL, h * 16 + 16)
        e.wait_ge(sKL, g * 16 + 16)
        e.wait_ge(sVL, g * 16 + 16)
        t_scores(e, h, 0)
        for Q in range(NQ):
            if Q + 1 < NQ:
                t_scores(e, h, Q + 1)
            kcq = KC(Q)
            pt = CPT[Q]
            e.wait_ge(sDG, h * NSUB + Q * 4 + 4)
            for tkc in range(4 * kcq):
                if pt % 2 == 0:
                    e.wait_ge(sPE, h * NPT + pt + 2)
                for sub in range(4):
                    ins_ = e.matmul(
                        ctx.pp[3 + pt % 2][:, sub * 128:(sub + 1) * 128],
                        probs[sub][:, tkc * 128:(tkc + 1) * 128],
                        diag[Q % 2][sub], start=True, stop=True,
                        skip_group_check=True)
                ins_.then_inc(sPT, 1)
                pt += 1
            e.wait_ge(sAE, h * NQ + Q + 2)
            e.wait_ge(sPE, h * NPT + CPT[Q] + 4 * kcq + 2)
            for tkc in range(4 * kcq):
                ins_ = e.matmul(ctx.pp[5 + Q % 2][:],
                                vh[:, tkc * 128:(tkc + 1) * 128],
                                probsT[:, tkc * 512:(tkc + 1) * 512],
                                start=(tkc == 0), stop=(tkc == 4 * kcq - 1))
            ins_.then_inc(sPV, 1)

    def v_em(e, h, Q):
        kcq = KC(Q)
        gi = CPT[Q]
        for sub in range(4):
            e.wait_ge(sEX, h * NSUB + Q * 4 + sub + 1)
            for kc in range(kcq):
                e.wait_ge(sSC, h * NSC + gi + 1)
                e.scalar_tensor_tensor(
                    scores[sub][:, kc * 512:(kc + 1) * 512],
                    ctx.pp[gi % 3][:], 1.0, mk(kc, Q, sub),
                    ALU.mult, ALU.add).then_inc(sSE, 1)
                gi += 1

    def v_rd(e, h, Q):
        for sub in range(4):
            e.wait_ge(sEX, h * NSUB + Q * 4 + sub + 5)
            e.reciprocal(st1[:, 12 + sub:13 + sub],
                         st1[:, 8 + sub:9 + sub]).then_inc(sRC, 1)
            e.wait_ge(sRC, h * NSUB + Q * 4 + sub + 1)
            e.wait_ge(sPT, h * NPT + CPT[Q] + (0 if Q >= 2 else
                      CPT[NQ - 1 + Q] - NPT - CPT[Q]) +
                      (CPT[Q - 1] - CPT[Q] if Q >= 2 else 0) + NPT)
            e.tensor_scalar(diag[Q % 2][sub], ctx.ident[:],
                            st1[:, 12 + sub:13 + sub], None,
                            ALU.mult).then_inc(sDG, 1)

    def v_body(e, h, g, hg):
        v_em(e, h, 0)
        for Q in range(NQ):
            kcq = KC(Q)
            v_rd(e, h, Q)
            if Q + 1 < NQ:
                v_em(e, h, Q + 1)
            pt = CPT[Q]
            for tkc in range(4 * kcq):
                if pt % 2 == 0:
                    e.wait_ge(sPT, h * NPT + pt + 1 + NPT)
                    e.wait_ge(sPV, h * NQ + Q)
                    e.tensor_copy(probsT[:, tkc * 512:(tkc + 1) * 512],
                                  ctx.pp[3 + pt % 2][:]).then_inc(sPE, 1)
                pt += 1

    def s_atsb(e, h, Qp):
        e.wait_ge(sPV, h * NQ + Qp + 1)
        e.wait_ge(sAS, (h * NQ + Qp) * 16)
        e.activation(at_sb[Qp % 2], ctx.pp[5 + Qp % 2][:],
                     AF.Identity).then_inc(sAE, 1)

    def s_body(e, h, g, hg):
        for Q in range(NQ):
            kcq = KC(Q)
            for sub in range(4):
                e.wait_ge(sSE, 3 + h * NSC + CPT[Q] + (sub + 1) * kcq)
                e.wait_ge(sRC, h * NSUB + Q * 4)
                e.wait_ge(sPT, h * NPT + CPT[Q] + NPT)
                e.activation(probs[sub][:, 0:kcq * 512],
                             scores[sub][:, 0:kcq * 512], AF.Exp,
                             bias=0.0, scale=scale,
                             accum_out=st1[:, 8 + sub:9 + sub]).then_inc(sEX, 1)
            if Q >= 1:
                s_atsb(e, h, Q - 1)
            pt = CPT[Q]
            for tkc in range(4 * kcq):
                if pt % 2 == 1:
                    e.wait_ge(sPT, h * NPT + pt + 1 + NPT)
                    e.wait_ge(sPV, h * NQ + Q)
                    e.activation(probsT[:, tkc * 512:(tkc + 1) * 512],
                                 ctx.pp[3 + pt % 2][:],
                                 AF.Copy).then_inc(sPE, 1)
                pt += 1
        s_atsb(e, h, NQ - 1)

    def f_tensor(e):
        e.nop().then_inc(sPT, NPT)
        heads_loop(e, t_body)

    def f_vector(e):
        e.nop().then_inc(sSE, 3)
        e.wait_ge(sMK, 80)
        heads_loop(e, v_body)

    def f_scalar(e):
        e.nop().then_inc(sEX, 4)
        e.nop().then_inc(sPE, 2)
        e.nop().then_inc(sAE, 2)
        heads_loop(e, s_body)

    ctx.stage(sync=f_sync, tensor=f_tensor, vector=f_vector, scalar=f_scalar)


def emit_proj_resid(ctx, lhsT_name, w_name, x_in_name, x_out_name,
                    bias_name=None, n_k=NDC, n_out=D):
    nc = ctx.nc
    lhsT = ctx.dram[lhsT_name]
    w = ctx.dram[w_name]
    x_in = ctx.dram[x_in_name].rearrange("(n p) d -> n p d", p=128)
    x_out = ctx.dram[x_out_name].rearrange("(n p) d -> n p d", p=128)
    bias = ctx.dram[bias_name] if bias_name else None
    NOC = n_out // 512

    sWL = ctx.sem("prWL", "sw")
    sBL = ctx.sem("prBL", "sw")
    sAL = [ctx.sem("prAL0", "hw"), ctx.sem("prAL1", "hw")]
    sXL = [ctx.sem("prXL0", "hw"), ctx.sem("prXL1", "hw")]
    sMM = ctx.sem("prMM")
    sRB = ctx.sem("prRB")
    sEV = ctx.sem("prEV")
    sXS = [ctx.sem("prXS0", "hw"), ctx.sem("prXS1", "hw")]

    wA = ctx.a1(BF16, 0, n_k * n_out)
    biasb = ctx.a2(F32, 0, n_out)
    xnew = [ctx.a2(F32, 8192, D), ctx.a2(F32, 16384, D)]
    at_sb = [ctx.a2(BF16, 24576, n_k * 128), ctx.a2(BF16, 24576 + 4096, n_k * 128)]
    rbt = [ctx.a2(F32, 32768 + 8192 * b, D) for b in range(2)]
    xb = ctx.xb

    w4 = w.rearrange("(c p) o -> p c o", p=128)
    l4 = lhsT.rearrange("(c p) t -> p c t", p=128)

    def f_gpsimd(e):
        e.dma_start(out=wA.rearrange("p (c o) -> p c o", o=n_out),
                    in_=w4).then_inc(sWL, 16)
        if bias is not None:
            e.dma_start(out=biasb, in_=bias[:]).then_inc(sBL, 16)

    def f_sync(e):
        with e.Fori(0, NTT // 2) as i:
            for b in range(2):
                it = i * 2 + b
                e.wait_ge(sMM, it * 4)
                e.dma_start(out=at_sb[b].rearrange("p (c t) -> p c t", t=128),
                            in_=l4[:, :, bass.ts(it, 128)]).then_inc(sAL[b], 16)
                if bias is not None:
                    e.wait_ge(sRB, it)
                else:
                    e.wait_ge(sEV, it * 4 + 2)
                e.dma_start(out=xb[b][:], in_=x_in[it]).then_inc(sXL[b], 16)
            for b in range(2):
                it = i * 2 + b
                e.wait_ge(sEV, it * 4 + 10)
                e.dma_start(out=x_out[it], in_=xnew[b]).then_inc(sXS[b], 16)
        e.wait_ge(sXS[0], NTT * 8)
        e.wait_ge(sXS[1], NTT * 8)

    def f_tensor(e):
        e.wait_ge(sWL, 16)
        with e.Fori(0, NTT // 2) as i:
            for b in range(2):
                it = i * 2 + b
                e.wait_ge(sAL[b], i * 16 + 16)
                for oc in range(NOC):
                    g = it * NOC + oc
                    e.wait_ge(sEV, g + 3)
                    for dc in range(n_k):
                        ins_ = e.matmul(ctx.pp[(oc % 4) if NOC >= 4
                                               else (oc % 2)][:],
                                        at_sb[b][:, dc * 128:(dc + 1) * 128],
                                        wA[:, dc * n_out + oc * 512:
                                           dc * n_out + (oc + 1) * 512],
                                        start=(dc == 0), stop=(dc == n_k - 1))
                    ins_.then_inc(sMM, 1)

    def f_vector(e):
        e.nop().then_inc(sEV, 6)
        if bias is not None:
            e.wait_ge(sBL, 16)
        with e.Fori(0, NTT // 2) as i:
            for b in range(2):
                it = i * 2 + b
                e.wait_ge(sXL[b], i * 16 + 16)
                if bias is not None:
                    e.wait_ge(sEV, it * 4 + 2)
                    e.tensor_tensor(rbt[b], xb[b][:], biasb,
                                    ALU.add).then_inc(sRB, 1)
                src = rbt[b] if bias is not None else xb[b][:]
                for oc in range(NOC):
                    g = it * NOC + oc
                    e.wait_ge(sMM, g + 1)
                    e.wait_ge(sXS[b], i * 16)
                    if bias is not None:
                        e.wait_ge(sRB, it + 1)
                    e.tensor_tensor(xnew[b][:, oc * 512:(oc + 1) * 512],
                                    ctx.pp[(oc % 4) if NOC >= 4
                                           else (oc % 2)][:],
                                    src[:, oc * 512:(oc + 1) * 512],
                                    ALU.add).then_inc(sEV, 1)

    ctx.stage(sync=f_sync, tensor=f_tensor, vector=f_vector, gpsimd=f_gpsimd)


def runhw(nc, in_maps, core_ids, **kw):
    if not nc.is_finalized():
        nc.finalize()
    return run_bass_kernel_spmd(nc, in_maps, core_ids=core_ids, **kw)


def emit_wtrans(ctx, w_name, wT_name, O, I, scale_name=None, quant=False):
    nc = ctx.nc
    w = ctx.dram[w_name]
    wT = ctx.dram[wT_name]
    scale = ctx.dram[scale_name] if scale_name else None
    assert O % 512 == 0
    NO4 = O // 512
    I_s = 2048 if quant else min(I, 4096)
    assert I % I_s == 0
    NIH = I // I_s
    NIC = I_s // 128
    NG = I_s // 64

    if quant:
        wb = [ctx.a2(BF16, 0, I_s), ctx.a2(BF16, 45056, I_s)]
        WQ = [ctx.a2(BF16, 4096 + j * 2 * I_s, I_s) for j in range(4)]
        AB = ctx.a2(F32, 20480, I_s)
        SGN = ctx.a2(BF16, 28672, I_s)
        MASK = ctx.a2(BF16, 32768, I_s)
        SSC = ctx.a2(BF16, 36864, I_s)
        EE = ctx.a2(BF16, 49152, I_s)
        T1 = ctx.a2(BF16, 53248, I_s)
        SUMS = ctx.a2(F32, 40960, NG)
        SCALE = ctx.a2(BF16, 41216, NG)
        THR = ctx.a2(F32, 41344, NG)
        stg = [ctx.a2(BF16, 41984, 512), ctx.a2(BF16, 43008, 512)]
        scol = [ctx.a2(F32, 44032, 1), ctx.a2(F32, 44036, 1)]
        srcs = WQ
    else:
        wb = ctx.a2(BF16, 0, 4 * I_s)
        stg = [ctx.a2(BF16, 32768, 512), ctx.a2(BF16, 33792, 512)]
        scol = [ctx.a2(F32, 34816, 1), ctx.a2(F32, 34820, 1)]
        srcs = None

    w3 = w.rearrange("(oj p) i -> p oj i", p=128)
    wT4 = wT.rearrange("(c p) o -> c p o", p=128)

    for ih in range(NIH):
        isl = slice(ih * I_s, (ih + 1) * I_s)
        icg0 = ih * NIC
        kbase = 0
        sWB = [ctx.sem("wtWB0", "sw"), ctx.sem("wtWB1", "sw")] if quant \
            else ctx.sem("wtWB", "sw")
        sQS = ctx.sem("wtQS")
        sQV = ctx.sem("wtQV")
        sTR = ctx.sem("wtTR")
        sEV = ctx.sem("wtEV")
        sST = [ctx.sem("wtST0", "hw"), ctx.sem("wtST1", "hw")]
        sSC = [ctx.sem("wtSC0", "sw"), ctx.sem("wtSC1", "sw")]

        def f_gpsimd(e, ih=ih, isl=isl, icg0=icg0, kbase=kbase, sWB=sWB, sQS=sQS, sQV=sQV, sTR=sTR, sEV=sEV, sST=sST, sSC=sSC):
            if quant:
                def qload(e, k, j, op4):
                    m = k * 4 + j
                    if not (isinstance(m, int) and m < 2):
                        e.wait_ge(sQV, (m - 1) * 8)
                    e.dma_start(out=wb[j % 2],
                                in_=w3[:, bass.ds(op4 * 4 + j, 1), isl][:, 0, :]
                                ).then_inc(sWB[j % 2], 16)
                for j in range(4):
                    qload(e, kbase, j, 0)
                with e.Fori(1, NO4) as op4:
                    k = kbase + op4
                    for j in range(4):
                        qload(e, k, j, op4)
                return
            with e.Fori(0, NO4) as op4:
                k = kbase + op4
                if False:
                    pass
                else:
                    e.wait_ge(sTR, k * NIC)
                    e.dma_start(out=wb.rearrange("q (j i) -> q j i", i=I_s),
                                in_=w3[:, bass.ds(op4 * 4, 4), isl]).then_inc(sWB, 16)
                if scale is not None:
                    for ic in range(NIC):
                        e.wait_ge(sEV, k * NIC + ic + 1)
                        e.dma_start(out=scol[ic % 2],
                                    in_=scale[bass.ds((icg0 + ic) * 128, 128)]
                                    ).then_inc(sSC[ic % 2], 16)

        def f_scalar(e, kbase=kbase, sWB=sWB, sQS=sQS, sQV=sQV):
            if not quant:
                return None
            with e.Fori(0, NO4) as op4:
                k = kbase + op4
                for j in range(4):
                    e.wait_ge(sWB[j % 2], (k * 2 + j // 2) * 16 + 16)
                    e.wait_ge(sQV, (k * 4 + j) * 8)
                    e.activation(AB, wb[j % 2], AF.Abs).then_inc(sQS, 1)
                    e.activation(SGN, wb[j % 2], AF.Sign).then_inc(sQS, 1)

        def f_vector(e, kbase=kbase, ih=ih, sQS=sQS, sQV=sQV, sTR=sTR, sEV=sEV, sST=sST, sSC=sSC):
            e.nop().then_inc(sEV, 2)
            with e.Fori(0, NO4) as op4:
                k = kbase + op4
                if quant:
                    for j in range(4):
                        q8 = (k * 4 + j) * 8
                        e.wait_ge(sQS, (k * 4 + j) * 2 + 1)
                        e.tensor_reduce(SUMS.unsqueeze(2),
                                        AB.rearrange("q (g u) -> q g u", u=64),
                                        AX.X, ALU.add).then_inc(sQV, 1)
                        e.wait_ge(sQV, q8 + 1)
                        e.tensor_scalar(SCALE, SUMS, 1.0 / 64, 1e-8, ALU.mult,
                                        ALU.max).then_inc(sQV, 1)
                        e.wait_ge(sQV, q8 + 2)
                        e.tensor_scalar(THR, SCALE, 0.5009765625, None,
                                        ALU.mult).then_inc(sQV, 1)
                        e.wait_ge(sQV, q8 + 3)
                        e.tensor_tensor(MASK.rearrange("q (g u) -> q g u", u=64),
                                        AB.rearrange("q (g u) -> q g u", u=64),
                                        THR.unsqueeze(2).broadcast_to([128, NG, 64]),
                                        ALU.is_gt).then_inc(sQV, 1)
                        e.wait_ge(sQS, (k * 4 + j) * 2 + 2)
                        e.wait_ge(sQV, q8 + 4)
                        e.tensor_tensor(SSC.rearrange("q (g u) -> q g u", u=64),
                                        SGN.rearrange("q (g u) -> q g u", u=64),
                                        SCALE.unsqueeze(2).broadcast_to([128, NG, 64]),
                                        ALU.mult).then_inc(sQV, 1)
                        e.wait_ge(sQV, q8 + 5)
                        e.tensor_tensor(EE, SSC, wb[j % 2],
                                        ALU.subtract).then_inc(sQV, 1)
                        e.wait_ge(sQV, q8 + 6)
                        e.tensor_tensor(T1, wb[j % 2], EE,
                                        ALU.add).then_inc(sQV, 1)
                        e.wait_ge(sQV, q8 + 7)
                        e.wait_ge(sTR, k * NIC)
                        e.tensor_tensor(WQ[j], T1, MASK, ALU.mult).then_inc(sQV, 1)
                for ic in range(NIC):
                    e.wait_ge(sTR, k * NIC + ic + 1)
                    e.wait_ge(sST[ic % 2], (k * (NIC // 2) + ic // 2) * 16)
                    if scale is not None:
                        e.wait_ge(sSC[ic % 2], (k * (NIC // 2) + ic // 2) * 16 + 16)
                        ins_ = e.tensor_scalar(stg[ic % 2], ctx.ppb(ic % 2)[:],
                                               scol[ic % 2], None, ALU.mult)
                    else:
                        ins_ = e.tensor_copy(stg[ic % 2], ctx.ppb(ic % 2)[:])
                    ins_.then_inc(sEV, 1)

        def f_tensor(e, kbase=kbase, sWB=sWB, sQV=sQV, sEV=sEV, sTR=sTR):
            with e.Fori(0, NO4) as op4:
                k = kbase + op4
                if quant:
                    e.wait_ge(sQV, k * 32 + 32)
                else:
                    e.wait_ge(sWB, k * 16 + 16)
                for ic in range(NIC):
                    e.wait_ge(sEV, k * NIC + ic + 1)
                    for j in range(4):
                        src = srcs[j][:, ic * 128:(ic + 1) * 128] if quant \
                            else wb[:, j * I_s + ic * 128: j * I_s + (ic + 1) * 128]
                        ins_ = e.transpose(
                            ctx.ppb(ic % 2)[:, j * 128:(j + 1) * 128],
                            src, ctx.ident[:])
                    ins_.then_inc(sTR, 1)

        def f_sync(e, icg0=icg0, kbase=kbase, sEV=sEV, sST=sST):
            with e.Fori(0, NO4) as op4:
                k = kbase + op4
                for ic in range(NIC):
                    e.wait_ge(sEV, 2 + k * NIC + ic + 1)
                    e.dma_start(out=wT4[icg0 + ic, :, bass.ts(op4, 512)],
                                in_=stg[ic % 2]).then_inc(sST[ic % 2], 16)
            e.wait_ge(sST[0], (kbase + NO4) * (NIC // 2) * 16)
            e.wait_ge(sST[1], (kbase + NO4) * (NIC // 2) * 16)

        ctx.stage(sync=f_sync, tensor=f_tensor, vector=f_vector,
                  gpsimd=f_gpsimd,
                  **({"scalar": f_scalar} if quant else {}))


def emit_l1a_core(ctx, wT_names, fb_name, gated_name):
    nc = ctx.nc
    wT = [ctx.dram[n] for n in wT_names]
    fb = ctx.dram[fb_name]
    gated = ctx.dram[gated_name]
    NSC = 16

    sWL = ctx.sem("laWL", "sw")
    sBC = ctx.sem("laBC", "sw")
    sMM = ctx.sem("laMM")
    sAC = ctx.sem("laAC")
    sVE = ctx.sem("laVE")
    sSN = ctx.sem("laSN")
    sGT = ctx.sem("laGT")
    sGS = ctx.sem("laGS", "hw")

    fbuf = ctx.a2(BF16, 0, T)
    zbuf = ctx.a2(BF16, 8192, T)
    og = ctx.a2(BF16, 16384, T)
    hbuf = ctx.a2(BF16, 24576, T)
    SG = ctx.a2(F32, 32768, 512)
    GA = ctx.a2(F32, 34816, 512)
    wcb = [ctx.a2(BF16, 36864 + wi * 4096, 2048) for wi in range(5)]
    biascol = ctx.st1[:, 0:1]
    xnT3 = ctx.a1(BF16, 0, NDC * T).rearrange("p (c t) -> p c t", t=T)
    g4 = gated.rearrange("(c p) t -> c p t", p=128)
    wT4 = [x.rearrange("(c p) s -> p c s", p=128) for x in wT]

    def f_gpsimd(e):
        with e.Fori(0, NSC) as sc:
            e.wait_ge(sMM, sc * 40)
            for wi in range(5):
                e.dma_start(out=wcb[wi].rearrange("p (c r) -> p c r", r=128),
                            in_=wT4[wi][:, :, bass.ts(sc, 128)]).then_inc(sWL, 16)
            e.wait_ge(sAC, sc * 24)
            e.dma_start(out=biascol, in_=fb[bass.ds(sc, 1), :]).then_inc(sBC, 16)

    def f_tensor(e):
        with e.Fori(0, NSC) as sc:
            e.wait_ge(sWL, sc * 80 + 80)
            for ts in range(8):
                w = None
                e.wait_ge(sAC, (sc * 8 + ts) * 3)
                e.wait_ge(sVE, (sc * 8 + ts) * 2)
                for wi in range(5):
                    for dc in range(NDC):
                        ins_ = e.matmul(ctx.pp[wi][:],
                                        wcb[wi][:, dc * 128:(dc + 1) * 128],
                                        xnT3[:, dc, bass.ts(ts, 512)],
                                        start=(dc == 0), stop=(dc == NDC - 1))
                    ins_.then_inc(sMM, 1)

    def f_scalar(e):
        with e.Fori(0, NSC) as sc:
            e.wait_ge(sBC, sc * 16 + 16)
            e.wait_ge(sSN, sc)
            for ts in range(8):
                wv = sc * 8 + ts
                e.wait_ge(sMM, wv * 5 + 1)
                e.activation(fbuf[:, ts * 512:(ts + 1) * 512], ctx.pp[0][:],
                             AF.Sigmoid, bias=biascol).then_inc(sAC, 1)
                e.wait_ge(sMM, wv * 5 + 2)
                e.wait_ge(sVE, wv * 2)
                e.activation(SG, ctx.pp[1][:], AF.Sigmoid).then_inc(sAC, 1)
                e.wait_ge(sMM, wv * 5 + 4)
                e.activation(GA, ctx.pp[3][:], AF.Silu).then_inc(sAC, 1)

    def f_vector(e):
        with e.Fori(0, NSC) as sc:
            e.wait_ge(sSN, sc)
            e.wait_ge(sGT, sc)
            e.wait_ge(sGS, sc * 16)
            for ts in range(8):
                wv = sc * 8 + ts
                e.wait_ge(sMM, wv * 5 + 3)
                e.wait_ge(sAC, wv * 3 + 2)
                e.tensor_tensor(zbuf[:, ts * 512:(ts + 1) * 512], ctx.pp[2][:],
                                SG, ALU.mult).then_inc(sVE, 1)
                e.wait_ge(sMM, wv * 5 + 5)
                e.wait_ge(sAC, wv * 3 + 3)
                e.tensor_tensor(og[:, ts * 512:(ts + 1) * 512], ctx.pp[4][:],
                                GA, ALU.mult).then_inc(sVE, 1)
            e.wait_ge(sVE, sc * 16 + 16)
            e.wait_ge(sAC, sc * 24 + 24)
            e.tensor_tensor_scan(hbuf, fbuf, zbuf, 0.0, ALU.mult,
                                 ALU.add).then_inc(sSN, 1)
            e.wait_ge(sSN, sc + 1)
            e.tensor_tensor(zbuf, og, hbuf, ALU.mult).then_inc(sGT, 1)

    def f_sync(e):
        with e.Fori(0, NSC) as sc:
            e.wait_ge(sGT, sc + 1)
            e.dma_start(out=g4[sc], in_=zbuf).then_inc(sGS, 16)
        e.wait_ge(sGS, NSC * 16)

    ctx.stage(sync=f_sync, tensor=f_tensor, vector=f_vector,
              scalar=f_scalar, gpsimd=f_gpsimd)


def emit_ffn_gu(ctx, wguT_name, hT_name, rstdT_name):
    nc = ctx.nc
    wguT = ctx.dram[wguT_name]
    hT = ctx.dram[hT_name]
    rstdT = ctx.dram[rstdT_name]
    NHC = 64
    NW = NHC * 8

    sWG = [[ctx.sem(f"fgWG{b}{h}", "sw") for h in range(2)] for b in range(2)]
    sMM = ctx.sem("fgMM")
    sSI = ctx.sem("fgSI")
    sHB = ctx.sem("fgHB")
    sSQ = ctx.sem("fgSQ")
    sSM = ctx.sem("fgSM")
    sHS = ctx.sem("fgHS", "hw")
    sRD = ctx.sem("fgRD")

    hbuf = ctx.a2(BF16, 0, T)
    GS = ctx.a2(BF16, 16384, 512)
    h2 = [ctx.a2(BF16, 17408, 512), ctx.a2(BF16, 18432, 512)]
    wgb = [ctx.a2(BF16, 20480, 4096), ctx.a2(BF16, 28672, 4096)]
    meanb = ctx.a2(F32, 40960, 32)
    lnvb = ctx.a2(F32, 41088, 32)
    rstdb = ctx.a2(F32, 41216, 32)
    rstdT_sb = ctx.a2(F32, 41472, 128)
    xnT3 = ctx.a1(BF16, 0, NDC * T).rearrange("p (c t) -> p c t", t=T)
    wg4 = wguT.rearrange("(c p) o -> p c o", p=128)
    hT4 = hT.rearrange("(c p) t -> c p t", p=128)

    def t_sum(e, wvp, tsp):
        e.wait_ge(sSQ, wvp + 1)
        for tsub in range(4):
            ins_ = e.matmul(ctx.pp[4][:, tsp * 4 + tsub: tsp * 4 + tsub + 1],
                            h2[tsp % 2][:, tsub * 128:(tsub + 1) * 128],
                            ctx.onescol[:],
                            start=(wvp == 0 if isinstance(wvp, int) else False),
                            stop=False, skip_group_check=True)
        ins_.then_inc(sSM, 1)

    def t_wave(e, hc, ts, wb):
        wv = hc * 8 + ts
        e.wait_ge(sSI, wv)
        e.wait_ge(sHB, wv)
        for dc in range(NDC):
            ins_ = e.matmul(ctx.pp[ts % 2][:],
                            wb[:, dc * 128:(dc + 1) * 128],
                            xnT3[:, dc, bass.ts(ts, 512)],
                            start=(dc == 0), stop=(dc == NDC - 1))
        ins_.then_inc(sMM, 1)
        for dc in range(NDC):
            ins_ = e.matmul(ctx.pp[2 + ts % 2][:],
                            wb[:, 2048 + dc * 128: 2048 + (dc + 1) * 128],
                            xnT3[:, dc, bass.ts(ts, 512)],
                            start=(dc == 0), stop=(dc == NDC - 1))
        ins_.then_inc(sMM, 1)
        if isinstance(wv, int) and wv == 0:
            return
        t_sum(e, wv - 1, (ts - 1) % 8)

    def f_tensor(e):
        e.nop().then_inc(sSM, 1)
        e.wait_ge(sWG[0][0], 16)
        e.wait_ge(sWG[0][1], 16)
        for ts in range(8):
            t_wave(e, 0, ts, wgb[0])
        e.wait_ge(sWG[1][0], 16)
        e.wait_ge(sWG[1][1], 16)
        for ts in range(8):
            t_wave(e, 1, ts, wgb[1])
        with e.Fori(1, NHC // 2) as hc2:
            e.wait_ge(sWG[0][0], hc2 * 16 + 16)
            e.wait_ge(sWG[0][1], hc2 * 16 + 16)
            for ts in range(8):
                t_wave(e, hc2 * 2, ts, wgb[0])
            e.wait_ge(sWG[1][0], hc2 * 16 + 16)
            e.wait_ge(sWG[1][1], hc2 * 16 + 16)
            for ts in range(8):
                t_wave(e, hc2 * 2 + 1, ts, wgb[1])
        t_sum(e, NW - 1, 7)
        e.wait_ge(sRD, 3)
        e.transpose(ctx.pp[5][0:32, 0:128], rstdb, ctx.identf[:]).then_inc(sRD, 1)

    def f_gpsimd(e):
        def load(e, hc, b, gate_val):
            if gate_val is not None:
                e.wait_ge(sMM, gate_val)
            e.dma_start(out=wgb[b][:, 0:2048].rearrange("p (c r) -> p c r", r=128),
                        in_=wg4[:, :, bass.ds(hc * 128, 128)]).then_inc(sWG[b][0], 16)
            e.dma_start(out=wgb[b][:, 2048:4096].rearrange("p (c r) -> p c r", r=128),
                        in_=wg4[:, :, bass.ds(8192 + hc * 128, 128)]).then_inc(sWG[b][1], 16)
        load(e, 0, 0, None)
        load(e, 1, 1, None)
        with e.Fori(1, NHC // 2) as hc2:
            load(e, hc2 * 2, 0, (hc2 * 2 - 1) * 16)
            load(e, hc2 * 2 + 1, 1, hc2 * 2 * 16)

    def f_scalar(e):
        with e.Fori(0, NHC) as hc:
            for ts in range(8):
                wv = hc * 8 + ts
                e.wait_ge(sMM, wv * 2 + 1)
                e.wait_ge(sHB, wv)
                e.activation(GS, ctx.pp[ts % 2][:], AF.Silu).then_inc(sSI, 1)
                e.wait_ge(sHB, wv + 1)
                e.wait_ge(sSM, wv)
                e.activation(h2[ts % 2], hbuf[:, ts * 512:(ts + 1) * 512],
                             AF.Square).then_inc(sSQ, 1)
        e.wait_ge(sRD, 1)
        e.activation(lnvb, meanb, AF.Ln, bias=ctx.ceps2[:]).then_inc(sRD, 1)
        e.wait_ge(sRD, 2)
        e.activation(rstdb, lnvb, AF.Exp, scale=-0.5).then_inc(sRD, 1)
        e.wait_ge(sRD, 4)
        e.activation(rstdT_sb[0:32, :], ctx.pp[5][0:32, 0:128],
                     AF.Copy).then_inc(sRD, 1)

    def f_vector(e):
        with e.Fori(0, NHC) as hc:
            for ts in range(8):
                wv = hc * 8 + ts
                e.wait_ge(sMM, wv * 2 + 2)
                e.wait_ge(sSI, wv + 1)
                e.wait_ge(sHS, hc * 16)
                e.tensor_tensor(hbuf[:, ts * 512:(ts + 1) * 512],
                                ctx.pp[2 + ts % 2][:], GS,
                                ALU.mult).then_inc(sHB, 1)
        e.wait_ge(sSM, NW + 1)
        e.tensor_scalar(meanb, ctx.pp[4][:, 0:32], 1.0 / 8192.0, None,
                        ALU.mult).then_inc(sRD, 1)

    def f_sync(e):
        with e.Fori(0, NHC) as hc:
            e.wait_ge(sSQ, hc * 8 + 8)
            e.dma_start(out=hT4[hc], in_=hbuf).then_inc(sHS, 16)
        e.wait_ge(sHS, NHC * 16)
        e.wait_ge(sRD, 5)
        e.dma_start(out=rstdT[:], in_=rstdT_sb[0:32, :]).then_inc(sHS, 16)
        e.wait_ge(sHS, NHC * 16 + 16)

    ctx.stage(sync=f_sync, tensor=f_tensor, vector=f_vector,
              scalar=f_scalar, gpsimd=f_gpsimd)


def emit_ffn_down(ctx, hT_name, wpT_name, rstdT_name, x_in_name, x_out_name):
    nc = ctx.nc
    hT = ctx.dram[hT_name]
    wpT = ctx.dram[wpT_name]
    rstdT = ctx.dram[rstdT_name]
    x_in = ctx.dram[x_in_name].rearrange("(n p) d -> n p d", p=128)
    x_out = ctx.dram[x_out_name].rearrange("(n p) d -> n p d", p=128)
    xpart = ctx.scratch(x_out_name + "_part", [T, D], F32) \
        .rearrange("(n p) d -> n p d", p=128)

    sWL = ctx.sem("fdWL", "sw")
    sAL = [ctx.sem("fdAL0", "hw"), ctx.sem("fdAL1", "hw")]
    sXL = [ctx.sem("fdXL0", "hw"), ctx.sem("fdXL1", "hw")]
    sRL = [ctx.sem("fdRL0", "hw"), ctx.sem("fdRL1", "hw")]
    sPL = [ctx.sem("fdPL0", "hw"), ctx.sem("fdPL1", "hw")]
    sPS = [ctx.sem("fdPS0", "hw"), ctx.sem("fdPS1", "hw")]
    sMM = ctx.sem("fdMM")
    sEV = ctx.sem("fdEV")
    sVX = ctx.sem("fdVX")
    sXS = [ctx.sem("fdXS0", "hw"), ctx.sem("fdXS1", "hw")]

    wA = ctx.a1(BF16, 0, 32 * D)
    at_sb = [ctx.a2(BF16, 0, 32 * 128), ctx.a2(BF16, 8192, 32 * 128)]
    xacc = [ctx.a2(F32, 16384, D), ctx.a2(F32, 24576, D)]
    xnew = [ctx.a2(F32, 32768, D), ctx.a2(F32, 40960, D)]
    rsl = [ctx.a2(F32, 49216, 1), ctx.a2(F32, 49220, 1)]
    xb = ctx.xb
    w4 = wpT.rearrange("(c p) o -> p c o", p=128)
    l4 = hT.rearrange("(c p) t -> p c t", p=128)

    def f_gpsimd(e):
        for kh in range(2):
            if kh == 1:
                e.wait_ge(sMM, NTT * 4)
            e.dma_start(out=wA.rearrange("p (c o) -> p c o", o=D),
                        in_=w4[:, kh * 32:(kh + 1) * 32, :]).then_inc(sWL, 16)

    def f_sync(e):
        with e.Fori(0, NTT // 2) as i:
            for b in range(2):
                it = i * 2 + b
                e.wait_ge(sMM, it * 4)
                e.dma_start(out=at_sb[b].rearrange("p (c t) -> p c t", t=128),
                            in_=l4[:, 0:32, bass.ts(it, 128)]).then_inc(sAL[b], 16)
            for b in range(2):
                it = i * 2 + b
                e.wait_ge(sEV, it * 4 + 10)
                e.dma_start(out=xpart[it], in_=xacc[b]).then_inc(sPS[b], 16)
        with e.Fori(0, NTT // 2) as i:
            for b in range(2):
                it = i * 2 + b
                e.wait_ge(sMM, NTT * 4 + it * 4)
                e.dma_start(out=at_sb[b].rearrange("p (c t) -> p c t", t=128),
                            in_=l4[:, 32:64, bass.ts(it, 128)]).then_inc(sAL[b], 16)
                e.wait_ge(sPS[b], (NTT // 2) * 16)
                e.wait_ge(sVX, it * 4)
                e.dma_start(out=xacc[b][:], in_=xpart[it]).then_inc(sPL[b], 16)
                e.wait_ge(sEV, NTT * 4 + it * 4 + 2)
                e.dma_start(out=xb[b][:], in_=x_in[it]).then_inc(sXL[b], 16)
                e.dma_start(out=rsl[b], in_=rstdT[bass.ds(it, 1), :]).then_inc(sRL[b], 16)
            for b in range(2):
                it = i * 2 + b
                e.wait_ge(sEV, NTT * 4 + it * 4 + 10)
                e.dma_start(out=x_out[it], in_=xnew[b]).then_inc(sXS[b], 16)
        e.wait_ge(sXS[0], NTT * 8)
        e.wait_ge(sXS[1], NTT * 8)

    def f_tensor(e):
        for kh in range(2):
            e.wait_ge(sWL, kh * 16 + 16)
            with e.Fori(0, NTT // 2) as i:
                for b in range(2):
                    it = i * 2 + b
                    e.wait_ge(sAL[b], kh * NTT * 8 + i * 16 + 16)
                    for oc in range(4):
                        g = kh * NTT * 4 + it * 4 + oc
                        if oc % 2 == 0:
                            e.wait_ge(sEV, g + 6)
                        for dc in range(32):
                            ins_ = e.matmul(ctx.pp[oc % 2][:],
                                            at_sb[b][:, dc * 128:(dc + 1) * 128],
                                            wA[:, dc * D + oc * 512:
                                               dc * D + (oc + 1) * 512],
                                            start=(dc == 0), stop=(dc == 31))
                        ins_.then_inc(sMM, 1)

    def f_vector(e):
        e.nop().then_inc(sEV, 6)
        with e.Fori(0, NTT // 2) as i:
            for b in range(2):
                it = i * 2 + b
                e.wait_ge(sPS[b], i * 16)
                for oc in range(4):
                    e.wait_ge(sMM, it * 4 + oc + 1)
                    e.tensor_copy(xacc[b][:, oc * 512:(oc + 1) * 512],
                                  ctx.pp[oc % 2][:]).then_inc(sEV, 1)
        with e.Fori(0, NTT // 2) as i:
            for b in range(2):
                it = i * 2 + b
                e.wait_ge(sXL[b], i * 16 + 16)
                e.wait_ge(sRL[b], i * 16 + 16)
                e.wait_ge(sPL[b], i * 16 + 16)
                for oc in range(4):
                    g = NTT * 4 + it * 4 + oc
                    e.wait_ge(sMM, g + 1)
                    e.wait_ge(sXS[b], i * 16)
                    e.tensor_tensor(xnew[b][:, oc * 512:(oc + 1) * 512],
                                    ctx.pp[oc % 2][:],
                                    xacc[b][:, oc * 512:(oc + 1) * 512],
                                    ALU.add).then_inc(sVX, 1)
                    e.wait_ge(sVX, it * 4 + oc + 1)
                    e.scalar_tensor_tensor(
                        xnew[b][:, oc * 512:(oc + 1) * 512],
                        xnew[b][:, oc * 512:(oc + 1) * 512],
                        rsl[b], xb[b][:, oc * 512:(oc + 1) * 512],
                        ALU.mult, ALU.add).then_inc(sEV, 1)

    ctx.stage(sync=f_sync, tensor=f_tensor, vector=f_vector, gpsimd=f_gpsimd)


def emit_l1b_rmsrope(ctx, qkraw_name, qT_name, kT_name):
    nc = ctx.nc
    qkraw = ctx.dram[qkraw_name]
    qT = ctx.dram[qT_name]
    kT = ctx.dram[kT_name]

    sTB = ctx.sem("rrTB", "hw")
    sLQ = ctx.sem("rrLQ", "hw")
    sSQ = ctx.sem("rrSQ")
    sSM = ctx.sem("rrSM")
    sMEAN = ctx.sem("rrMEAN")
    sLN = ctx.sem("rrLN")
    sTRr = ctx.sem("rrTRr")
    sRRE = ctx.sem("rrRRE")
    sBC = ctx.sem("rrBC")
    sRBE = ctx.sem("rrRBE")
    sVq = ctx.sem("rrVq")
    sROT = ctx.sem("rrROT")
    sTC = ctx.sem("rrTC")
    sST = ctx.sem("rrST", "hw")

    qcos = ctx.a1(F32, 0, T)
    qsin = ctx.a1(F32, 16384, T)
    kcos = ctx.a1(F32, 32768, T)
    ksin = ctx.a1(F32, 49152, T)
    qsl = ctx.a1(BF16, 65536, 16 * 512)
    ksl = ctx.a1(BF16, 81920, 4 * 512)
    rbq = ctx.a1(F32, 90112, 512)
    rbk = ctx.a1(F32, 92160, 512)
    sq = ctx.a2(BF16, 0, 512)
    qn = ctx.a2(BF16, 1024, 512)
    t1 = ctx.a2(F32, 3072, 512)
    t2 = ctx.a2(F32, 5120, 512)
    ohat = ctx.a2(BF16, 7168, 512)
    rrow = ctx.a2(F32, 8192, 512)
    costb = ctx.a2(F32, 10240, 512)
    sintb = ctx.a2(F32, 12288, 512)
    st1 = ctx.st1
    qk4 = qkraw.rearrange("(c p) t -> p c t", p=128)
    qT4 = qT.rearrange("(c p) t -> c p t", p=128)
    kT4 = kT.rearrange("(c p) t -> c p t", p=128)
    NCH = 20

    def chsrc(c):
        return qsl[:, c * 512:(c + 1) * 512] if c < 16 else \
            ksl[:, (c - 16) * 512:(c - 16) * 512 + 512]

    def f_sync(e):
        for nm, buf in (("l1b_qcos", qcos), ("l1b_qsin", qsin),
                        ("l1b_kcos", kcos), ("l1b_ksin", ksin)):
            e.dma_start(out=buf, in_=ctx.dram[nm][:]).then_inc(sTB, 16)
        with e.Fori(0, 8) as ts:
            e.wait_ge(sVq, ts * 80)
            e.dma_start(out=qsl.rearrange("p (c u) -> p c u", u=512),
                        in_=qk4[:, 0:16, bass.ts(ts, 512)]).then_inc(sLQ, 16)
            e.dma_start(out=ksl.rearrange("p (c u) -> p c u", u=512),
                        in_=qk4[:, 16:20, bass.ts(ts, 512)]).then_inc(sLQ, 16)
            for c in range(NCH):
                e.wait_ge(sVq, ts * 80 + c * 4 + 4)
                if c < 16:
                    e.dma_start(out=qT4[c, :, bass.ts(ts, 512)],
                                in_=ohat).then_inc(sST, 16)
                else:
                    e.dma_start(out=kT4[c - 16, :, bass.ts(ts, 512)],
                                in_=ohat).then_inc(sST, 16)
        e.wait_ge(sST, 8 * NCH * 16)

    def f_scalar(e):
        with e.Fori(0, 8) as ts:
            e.wait_ge(sLQ, ts * 32 + 32)
            for c in range(NCH):
                e.wait_ge(sSM, ts * NCH + c)
                e.activation(sq, chsrc(c), AF.Square).then_inc(sSQ, 1)
            for side in range(2):
                b = side * 16
                e.wait_ge(sMEAN, ts * 2 + side + 1)
                e.activation(st1[:, 4 + b:8 + b], st1[:, 0 + b:4 + b], AF.Ln,
                             bias=ctx.ceps3[:]).then_inc(sLN, 1)
                e.wait_ge(sLN, ts * 4 + side * 2 + 1)
                e.activation(st1[:, 8 + b:12 + b], st1[:, 4 + b:8 + b], AF.Exp,
                             scale=-0.5).then_inc(sLN, 1)
                e.wait_ge(sTRr, ts * 2 + side + 1)
                e.wait_ge(sBC, ts * 2 + side)
                e.activation(rrow[0:1, :], ctx.pp[7][0:1, 0:512],
                             AF.Copy).then_inc(sRRE, 1)

    def f_tensor(e):
        with e.Fori(0, 8) as ts:
            for c in range(NCH):
                e.wait_ge(sSQ, ts * NCH + c + 1)
                bank = 0 if c < 16 else 1
                col0 = (c == 0) or (c == 16)
                if col0:
                    e.wait_ge(sMEAN, ts * 2)
                for tsub in range(4):
                    ins_ = e.matmul(ctx.pp[bank][:, tsub: tsub + 1],
                                    sq[:, tsub * 128:(tsub + 1) * 128],
                                    ctx.onescol[:],
                                    start=col0, stop=False,
                                    skip_group_check=True)
                ins_.then_inc(sSM, 1)
            for side in range(2):
                b = side * 16
                e.wait_ge(sLN, ts * 4 + side * 2 + 2)
                e.wait_ge(sRRE, ts * 2 + side)
                for tsub in range(4):
                    ins_ = e.transpose(
                        ctx.pp[7][0:1, tsub * 128:(tsub + 1) * 128],
                        st1[:, 8 + b + tsub: 9 + b + tsub],
                        ctx.identf[:])
                ins_.then_inc(sTRr, 1)
                e.wait_ge(sRRE, ts * 2 + side + 1)
                e.wait_ge(sRBE, ts * 2 + side)
                e.matmul(ctx.pp[6][:], ctx.onesf[:], rrow[0:1, :],
                         start=True, stop=True).then_inc(sBC, 1)
            for c in range(NCH):
                base = ts * 80 + c * 4
                e.wait_ge(sVq, base + 1)
                e.matmul(ctx.pp[4 + c % 2][:], ctx.roper[:], qn,
                         start=True, stop=True).then_inc(sROT, 1)


    def f_vector(e):
        e.wait_ge(sTB, 64)
        with e.Fori(0, 8) as ts:
            for side in range(2):
                e.wait_ge(sSM, ts * NCH + (16 if side == 0 else 20))
                e.wait_ge(sLN, ts * 4 + side * 2)
                e.tensor_scalar(st1[:, side * 16: side * 16 + 4],
                                ctx.pp[side][:, 0:4],
                                1.0 / (2048.0 if side == 0 else 512.0), None,
                                ALU.mult).then_inc(sMEAN, 1)
            for side in range(2):
                e.wait_ge(sBC, ts * 2 + side + 1)
                e.wait_ge(sVq, ts * 80)
                e.tensor_copy(rbq if side == 0 else rbk,
                              ctx.pp[6][:]).then_inc(sRBE, 1)
            for c in range(NCH):
                base = ts * 80 + c * 4
                rb = rbq if c < 16 else rbk
                if c in (0, 16):
                    cosb = qcos if c < 16 else kcos
                    sinb = qsin if c < 16 else ksin
                    e.wait_ge(sVq, ts * 80 + (0 if c == 0 else 64))
                    e.tensor_copy(costb, cosb[:, bass.ts(ts, 512)]).then_inc(sTC, 1)
                    e.tensor_copy(sintb, sinb[:, bass.ts(ts, 512)]).then_inc(sTC, 1)
                e.wait_ge(sTC, ts * 4 + (2 if c < 16 else 4))
                e.wait_ge(sRBE, ts * 2 + (1 if c < 16 else 2))
                e.wait_ge(sROT, ts * NCH + c)
                e.tensor_tensor(qn, chsrc(c), rb, ALU.mult).then_inc(sVq, 1)
                e.wait_ge(sVq, base + 1)
                e.tensor_tensor(t1, qn, costb, ALU.mult).then_inc(sVq, 1)
                e.wait_ge(sROT, ts * NCH + c + 1)
                e.tensor_tensor(t2, ctx.pp[4 + c % 2][:], sintb,
                                ALU.mult).then_inc(sVq, 1)
                e.wait_ge(sVq, base + 3)
                e.wait_ge(sST, (ts * NCH + c) * 16)
                e.tensor_tensor(ohat, t1, t2, ALU.add).then_inc(sVq, 1)

    ctx.stage(sync=f_sync, tensor=f_tensor, vector=f_vector, scalar=f_scalar)


def emit_router_prep(ctx, router_name, routerT_name):
    nc = ctx.nc
    router = ctx.dram[router_name]
    routerT = ctx.dram[routerT_name].rearrange("(c p) u -> c p u", p=128)
    sL = ctx.sem("rpL", "sw"); sT = ctx.sem("rpT"); sE = ctx.sem("rpE")
    sS = ctx.sem("rpS", "hw")
    rt_sb = ctx.a2(BF16, 0, 2048)
    ot_sb = ctx.a2(BF16, 4096, 128)

    def f_gpsimd(e):
        e.dma_start(out=rt_sb[0:8, :], in_=router[:]).then_inc(sL, 16)

    def f_tensor(e):
        e.wait_ge(sL, 16)
        for ic in range(16):
            ins_ = e.transpose(ctx.ppb(0)[:, ic * 8:(ic + 1) * 8],
                               rt_sb[0:8, ic * 128:(ic + 1) * 128],
                               ctx.ident[0:8, 0:8])
        ins_.then_inc(sT, 1)

    def f_vector(e):
        e.wait_ge(sT, 1)
        e.tensor_copy(ot_sb, ctx.ppb(0)[:, 0:128]).then_inc(sE, 1)

    def f_sync(e):
        e.wait_ge(sE, 1)
        for ic in range(16):
            e.dma_start(out=routerT[ic],
                        in_=ot_sb[:, ic * 8:(ic + 1) * 8]).then_inc(sS, 16)
        e.wait_ge(sS, 256)

    ctx.stage(sync=f_sync, tensor=f_tensor, vector=f_vector, gpsimd=f_gpsimd)


def emit_moe_router(ctx, routerT_name, moe_wT_name):
    nc = ctx.nc
    routerT = ctx.dram[routerT_name]
    moe_wT = ctx.dram[moe_wT_name]
    sRL = ctx.sem("mrRL", "sw")
    sMM = ctx.sem("mrMM")
    sV = ctx.sem("mrV")
    sP = ctx.sem("mrP")
    sTR = ctx.sem("mrTR")
    sE = ctx.sem("mrE")
    sS = ctx.sem("mrS", "hw")

    rb_sb = ctx.a2(BF16, 0, 128)
    sc = ctx.a2(F32, 512, 8)
    eq1 = ctx.a2(F32, 1024, 8)
    sm = ctx.a2(F32, 1536, 8)
    eq2 = ctx.a2(F32, 2048, 8)
    w8 = ctx.a2(F32, 2560, 8)
    wT_sb = ctx.a2(F32, 3072, 128)
    st1 = ctx.st1
    xnT3 = ctx.a1(BF16, 0, NDC * T).rearrange("p (c t) -> p c t", t=T)
    rT4 = routerT.rearrange("(c p) u -> p c u", p=128)

    def f_gpsimd(e):
        e.dma_start(out=rb_sb.rearrange("p (c u) -> p c u", u=8),
                    in_=rT4).then_inc(sRL, 16)

    def f_tensor(e):
        e.wait_ge(sRL, 16)
        for tt in range(NTT):
            e.wait_ge(sV, tt * 9)
            for dc in range(NDC):
                ins_ = e.matmul(ctx.pp[0][:, 0:8],
                                xnT3[:, dc, bass.ts(tt, 128)],
                                rb_sb[:, dc * 8:(dc + 1) * 8],
                                start=(dc == 0), stop=(dc == NDC - 1))
            ins_.then_inc(sMM, 1)
            e.wait_ge(sV, tt * 9 + 17)
            e.wait_ge(sE, tt)
            e.transpose(ctx.pp[1][0:8, 0:128], w8, ctx.identf[:]).then_inc(sTR, 1)

    def f_vector(e):
        e.nop().then_inc(sV, 8)
        for tt in range(NTT):
            base = tt * 9 + 8
            e.wait_ge(sMM, tt + 1)
            e.tensor_copy(sc, ctx.pp[0][:, 0:8]).then_inc(sV, 1)
            e.wait_ge(sV, base + 1)
            e.reduce_max(st1[:, 0:1], sc, AX.X).then_inc(sV, 1)
            e.wait_ge(sV, base + 2)
            e.tensor_single_scalar(eq1, sc, st1[:, 0:1], ALU.is_equal).then_inc(sV, 1)
            e.wait_ge(sV, base + 3)
            e.scalar_tensor_tensor(sm, eq1, -1e30, sc, ALU.mult,
                                   ALU.add).then_inc(sV, 1)
            e.wait_ge(sV, base + 4)
            e.reduce_max(st1[:, 1:2], sm, AX.X).then_inc(sV, 1)
            e.wait_ge(sV, base + 5)
            e.tensor_single_scalar(eq2, sm, st1[:, 1:2], ALU.is_equal).then_inc(sV, 1)
            e.wait_ge(sV, base + 6)
            e.wait_ge(sP, tt * 2)
            e.tensor_tensor(st1[:, 4:5], st1[:, 0:1], st1[:, 1:2],
                            ALU.subtract).then_inc(sV, 1)
            e.wait_ge(sP, tt * 2 + 2)
            e.wait_ge(sTR, tt)
            e.tensor_scalar(w8, eq2, st1[:, 3:4], None, ALU.mult).then_inc(sV, 1)
            e.wait_ge(sV, base + 8)
            e.scalar_tensor_tensor(w8, eq1, st1[:, 2:3], w8, ALU.mult,
                                   ALU.add).then_inc(sV, 1)

    def f_scalar(e):
        for tt in range(NTT):
            e.wait_ge(sV, tt * 9 + 15)
            e.activation(st1[:, 2:3], st1[:, 4:5], AF.Sigmoid).then_inc(sP, 1)
            e.activation(st1[:, 3:4], st1[:, 4:5], AF.Sigmoid,
                         scale=-1.0).then_inc(sP, 1)
            e.wait_ge(sTR, tt + 1)
            e.wait_ge(sS, tt * 16)
            e.activation(wT_sb[0:8, :], ctx.pp[1][0:8, 0:128],
                         AF.Copy).then_inc(sE, 1)

    def f_sync(e):
        for tt in range(NTT):
            e.wait_ge(sE, tt + 1)
            e.dma_start(out=moe_wT[:, bass.ts(tt, 128)],
                        in_=wT_sb[0:8, :]).then_inc(sS, 16)
        e.wait_ge(sS, NTT * 16)

    ctx.stage(sync=f_sync, tensor=f_tensor, vector=f_vector,
              scalar=f_scalar, gpsimd=f_gpsimd)


def emit_moe_gu(ctx, wgT_name, wuT_name, hT_name):
    nc = ctx.nc
    wgT = ctx.dram[wgT_name]
    wuT = ctx.dram[wuT_name]
    hT = ctx.dram[hT_name]
    NJ = 48

    sWL = [[ctx.sem(f"mgWL{b}{h}", "sw") for h in range(2)] for b in range(2)]
    sMM = ctx.sem("mgMM")
    sSI = ctx.sem("mgSI")
    sHB = ctx.sem("mgHB")
    sST = [ctx.sem("mgST0", "hw"), ctx.sem("mgST1", "hw")]

    wgb = [ctx.a2(BF16, 0, 2048), ctx.a2(BF16, 4096, 2048)]
    wub = [ctx.a2(BF16, 8192, 2048), ctx.a2(BF16, 12288, 2048)]
    GT = ctx.a2(BF16, 16384, 512)
    hst = [ctx.a2(BF16, 17408, 512), ctx.a2(BF16, 18432, 512)]
    xnT3 = ctx.a1(BF16, 0, NDC * T).rearrange("p (c t) -> p c t", t=T)
    wg4 = wgT.rearrange("(c p) o -> p c o", p=128)
    wu4 = wuT.rearrange("(c p) o -> p c o", p=128)
    hT4 = hT.rearrange("(c p) t -> c p t", p=128)

    def f_gpsimd(e):
        def load(e, j, b, gate_val):
            if gate_val is not None:
                e.wait_ge(sMM, gate_val)
            e.dma_start(out=wgb[b].rearrange("p (c r) -> p c r", r=128),
                        in_=wg4[:, :, bass.ds(j * 128, 128)]).then_inc(sWL[b][0], 16)
            e.dma_start(out=wub[b].rearrange("p (c r) -> p c r", r=128),
                        in_=wu4[:, :, bass.ds(j * 128, 128)]).then_inc(sWL[b][1], 16)
        load(e, 0, 0, None)
        load(e, 1, 1, None)
        with e.Fori(1, NJ // 2) as j2:
            load(e, j2 * 2, 0, (j2 * 2 - 1) * 16)
            load(e, j2 * 2 + 1, 1, j2 * 2 * 16)

    def t_jwave(e, j, b):
        for ts in range(8):
            wv = j * 8 + ts
            e.wait_ge(sSI, wv)
            e.wait_ge(sHB, wv)
            for dc in range(NDC):
                ins_ = e.matmul(ctx.pp[ts % 2][:],
                                wgb[b][:, dc * 128:(dc + 1) * 128],
                                xnT3[:, dc, bass.ts(ts, 512)],
                                start=(dc == 0), stop=(dc == NDC - 1))
            ins_.then_inc(sMM, 1)
            for dc in range(NDC):
                ins_ = e.matmul(ctx.pp[2 + ts % 2][:],
                                wub[b][:, dc * 128:(dc + 1) * 128],
                                xnT3[:, dc, bass.ts(ts, 512)],
                                start=(dc == 0), stop=(dc == NDC - 1))
            ins_.then_inc(sMM, 1)

    def f_tensor(e):
        e.wait_ge(sWL[0][0], 16)
        e.wait_ge(sWL[0][1], 16)
        t_jwave(e, 0, 0)
        e.wait_ge(sWL[1][0], 16)
        e.wait_ge(sWL[1][1], 16)
        t_jwave(e, 1, 1)
        with e.Fori(1, NJ // 2) as j2:
            e.wait_ge(sWL[0][0], j2 * 16 + 16)
            e.wait_ge(sWL[0][1], j2 * 16 + 16)
            t_jwave(e, j2 * 2, 0)
            e.wait_ge(sWL[1][0], j2 * 16 + 16)
            e.wait_ge(sWL[1][1], j2 * 16 + 16)
            t_jwave(e, j2 * 2 + 1, 1)

    def f_scalar(e):
        with e.Fori(0, NJ) as j:
            for ts in range(8):
                wv = j * 8 + ts
                e.wait_ge(sMM, wv * 2 + 1)
                e.wait_ge(sHB, wv)
                e.activation(GT, ctx.pp[ts % 2][:], AF.Silu).then_inc(sSI, 1)

    def f_vector(e):
        with e.Fori(0, NJ) as j:
            for ts in range(8):
                wv = j * 8 + ts
                e.wait_ge(sMM, wv * 2 + 2)
                e.wait_ge(sSI, wv + 1)
                e.wait_ge(sST[ts % 2], (j * 4 + ts // 2) * 16)
                e.tensor_tensor(hst[ts % 2], ctx.pp[2 + ts % 2][:], GT,
                                ALU.mult).then_inc(sHB, 1)

    def f_sync(e):
        with e.Fori(0, NJ) as j:
            for ts in range(8):
                wv = j * 8 + ts
                e.wait_ge(sHB, wv + 1)
                e.dma_start(out=hT4[j][:, bass.ts(ts, 512)],
                            in_=hst[ts % 2]).then_inc(sST[ts % 2], 16)
        e.wait_ge(sST[0], NJ * 4 * 16)
        e.wait_ge(sST[1], NJ * 4 * 16)

    ctx.stage(sync=f_sync, tensor=f_tensor, vector=f_vector,
              scalar=f_scalar, gpsimd=f_gpsimd)


def emit_moe_scale(ctx, hT_name, moe_wT_name):
    nc = ctx.nc
    hT = ctx.dram[hT_name]
    moe_wT = ctx.dram[moe_wT_name]
    sWL = [ctx.sem("msWL0", "hw"), ctx.sem("msWL1", "hw")]
    sBC = ctx.sem("msBC")
    sBE = ctx.sem("msBE")
    sHL = [ctx.sem("msHL0", "hw"), ctx.sem("msHL1", "hw")]
    sSC = ctx.sem("msSC")
    sST = [ctx.sem("msST0", "sw"), ctx.sem("msST1", "sw")]

    wrow = ctx.a2(F32, 0, 512)
    web = ctx.a2(F32, 2048, 512)
    hbuf = [ctx.a2(BF16, 4096 + p * 6144, 6 * 512) for p in range(2)]
    obuf = [ctx.a2(BF16, 16384 + p * 6144, 6 * 512) for p in range(2)]

    def f_sync(e):
        with e.Fori(0, 8) as ex:
            for ts in range(8):
                i = ex * 8 + ts
                e.wait_ge(sBC, i)
                e.dma_start(out=wrow[0:1, :],
                            in_=moe_wT[bass.ds(ex, 1), bass.ts(ts, 512)]
                            ).then_inc(sWL[0], 16)
                e.wait_ge(sSC, i * 6)
                e.dma_start(out=hbuf[ts % 2].rearrange("p (c u) -> p c u", u=512),
                            in_=hT.rearrange("(c p) t -> p c t", p=128)
                            [:, bass.ds(ex * 6, 6), bass.ts(ts, 512)]
                            ).then_inc(sHL[ts % 2], 16)

    def f_gpsimd(e):
        with e.Fori(0, 8) as ex:
            for ts in range(8):
                i = ex * 8 + ts
                e.wait_ge(sSC, i * 6 + 6)
                e.dma_start(out=hT.rearrange("(c p) t -> p c t", p=128)
                            [:, bass.ds(ex * 6, 6), bass.ts(ts, 512)],
                            in_=obuf[ts % 2].rearrange("p (c u) -> p c u", u=512)
                            ).then_inc(sST[ts % 2], 16)
        e.wait_ge(sST[0], 8 * 4 * 16)
        e.wait_ge(sST[1], 8 * 4 * 16)

    def f_tensor(e):
        with e.Fori(0, 8) as ex:
            for ts in range(8):
                i = ex * 8 + ts
                e.wait_ge(sWL[0], i * 16 + 16)
                e.wait_ge(sBE, i)
                e.matmul(ctx.pp[6][:], ctx.onesf[:], wrow[0:1, :],
                         start=True, stop=True).then_inc(sBC, 1)

    def f_vector(e):
        with e.Fori(0, 8) as ex:
            for ts in range(8):
                i = ex * 8 + ts
                e.wait_ge(sBC, i + 1)
                e.wait_ge(sSC, i * 6)
                e.tensor_copy(web, ctx.pp[6][:]).then_inc(sBE, 1)
                e.wait_ge(sBE, i + 1)
                e.wait_ge(sHL[ts % 2], (ex * 4 + ts // 2) * 16 + 16)
                e.wait_ge(sST[ts % 2], (ex * 4 + ts // 2) * 16)
                for fc in range(6):
                    e.tensor_tensor(obuf[ts % 2][:, fc * 512:(fc + 1) * 512],
                                    hbuf[ts % 2][:, fc * 512:(fc + 1) * 512],
                                    web, ALU.mult).then_inc(sSC, 1)

    ctx.stage(sync=f_sync, tensor=f_tensor, vector=f_vector, gpsimd=f_gpsimd)


def emit_moe_down(ctx, hT_name, wdT_name, x_in_name, x_out_name):
    nc = ctx.nc
    hT = ctx.dram[hT_name]
    wdT = ctx.dram[wdT_name]
    x_in = ctx.dram[x_in_name].rearrange("(n p) d -> n p d", p=128)
    x_out = ctx.dram[x_out_name].rearrange("(n p) d -> n p d", p=128)

    sWL = ctx.sem("mdWL", "sw")
    sAL = [ctx.sem("mdAL0", "sw"), ctx.sem("mdAL1", "sw")]
    sXL = [ctx.sem("mdXL0", "hw"), ctx.sem("mdXL1", "hw")]
    sMM = ctx.sem("mdMM")
    sEV = ctx.sem("mdEV")
    sXS = [ctx.sem("mdXS0", "hw"), ctx.sem("mdXS1", "hw")]

    wA = ctx.a1(BF16, 0, 49152).rearrange("p (c e o) -> p c e o", e=8, o=1024)
    at_sb = [ctx.a2(BF16, 0, 6144), ctx.a2(BF16, 12288, 6144)]
    xnew = [ctx.a2(F32, 24576, 1024), ctx.a2(F32, 28672, 1024)]
    xb = ctx.xb
    w4 = wdT.rearrange("(c p) o -> p c o", p=128)
    l4 = hT.rearrange("(c p) t -> p c t", p=128)

    def f_gpsimd(e):
        for oh in range(2):
            if oh == 1:
                e.wait_ge(sMM, 64)
            for ex in range(8):
                e.dma_start(out=wA[:, :, ex, :],
                            in_=w4[:, :, ex * 2048 + oh * 1024:
                                   ex * 2048 + (oh + 1) * 1024]
                            ).then_inc(sWL, 16)
            with e.Fori(0, NTT // 2) as i:
                for b in range(2):
                    it = i * 2 + b
                    e.wait_ge(sMM, oh * 64 + it * 2)
                    e.dma_start(out=at_sb[b].rearrange("p (c t) -> p c t", t=128),
                                in_=l4[:, :, bass.ts(it, 128)]).then_inc(sAL[b], 16)

    def f_sync(e):
        for oh in range(2):
            with e.Fori(0, NTT // 2) as i:
                for b in range(2):
                    it = i * 2 + b
                    e.wait_ge(sEV, oh * 64 + it * 2 + 4)
                    e.dma_start(out=xb[b][:, 0:1024],
                                in_=x_in[it][:, oh * 1024:(oh + 1) * 1024]
                                ).then_inc(sXL[b], 16)
                for b in range(2):
                    it = i * 2 + b
                    e.wait_ge(sEV, oh * 64 + it * 2 + 8)
                    e.dma_start(out=x_out[it][:, oh * 1024:(oh + 1) * 1024],
                                in_=xnew[b]).then_inc(sXS[b], 16)
        e.wait_ge(sXS[0], NTT * 16)
        e.wait_ge(sXS[1], NTT * 16)

    def f_tensor(e):
        for oh in range(2):
            e.wait_ge(sWL, oh * 128 + 128)
            with e.Fori(0, NTT // 2) as i:
                for b in range(2):
                    it = i * 2 + b
                    e.wait_ge(sAL[b], oh * NTT * 8 + i * 16 + 16)
                    for oc in range(2):
                        g = oh * 64 + it * 2 + oc
                        if oc == 0:
                            e.wait_ge(sEV, g + 6)
                        kk = 0
                        for ex in range(8):
                            for fc in range(6):
                                j = ex * 6 + fc
                                ins_ = e.matmul(
                                    ctx.pp[oc % 2][:],
                                    at_sb[b][:, j * 128:(j + 1) * 128],
                                    wA[:, fc, ex, oc * 512:(oc + 1) * 512],
                                    start=(kk == 0), stop=(kk == 47))
                                kk += 1
                        ins_.then_inc(sMM, 1)

    def f_vector(e):
        e.nop().then_inc(sEV, 6)
        for oh in range(2):
            with e.Fori(0, NTT // 2) as i:
                for b in range(2):
                    it = i * 2 + b
                    e.wait_ge(sXL[b], oh * NTT * 8 + i * 16 + 16)
                    for oc in range(2):
                        g = oh * 64 + it * 2 + oc
                        e.wait_ge(sMM, g + 1)
                        e.wait_ge(sXS[b], oh * NTT * 8 + i * 16)
                        e.tensor_tensor(xnew[b][:, oc * 512:(oc + 1) * 512],
                                        ctx.pp[oc % 2][:],
                                        xb[b][:, oc * 512:(oc + 1) * 512],
                                        ALU.add).then_inc(sEV, 1)

    ctx.stage(sync=f_sync, tensor=f_tensor, vector=f_vector, gpsimd=f_gpsimd)



def emit_mla_cqkv(ctx, qdwT_name, kvdwT_name, bq_name, bkv_name,
                  xnTd_name, cqT_name, ckvT_name):
    nc = ctx.nc
    cqT = ctx.dram[cqT_name]
    ckvT = ctx.dram[ckvT_name]
    xd4 = ctx.dram[xnTd_name].rearrange("(c p) t -> p c t", p=128)

    sWL = ctx.sem("mcWL", "sw")
    sXS = ctx.sem("mcXS", "sw")
    sMM = ctx.sem("mcMM")
    sEV = ctx.sem("mcEV")
    sRS = ctx.sem("mcRS")
    sAP = ctx.sem("mcAP")
    sTR = ctx.sem("mcTR")
    sTE = ctx.sem("mcTE")
    sST = ctx.sem("mcST", "hw")

    wqA = ctx.a1(BF16, 0, 16 * 1024)
    wkA = ctx.a1(BF16, 32768, 16 * 512)
    cstg = ctx.a2(BF16, 0, 2048)
    cqsb = ctx.a2(F32, 4096, 1024)
    cqn = ctx.a2(BF16, 8192, 1024)
    ctstg = ctx.a2(BF16, 10240, 1024)
    kvsb = ctx.a2(F32, 12288, 512)
    kvn = ctx.a2(BF16, 14336, 512)
    ktstg = ctx.a2(BF16, 15360, 512)
    brow = ctx.a2(BF16, 16384, 1536)
    dumpq = ctx.a2(BF16, 19456, 1024)
    dumpkv = ctx.a2(BF16, 21504, 512)
    st1 = ctx.st1
    cqT4 = cqT.rearrange("(c p) t -> p c t", p=128)
    ckvT4 = ckvT.rearrange("(c p) t -> p c t", p=128)
    qdwT = ctx.dram[qdwT_name]
    kvdwT = ctx.dram[kvdwT_name]

    def f_gpsimd(e):
        e.dma_start(out=wqA.rearrange("p (c o) -> p c o", o=1024),
                    in_=qdwT.rearrange("(c p) o -> p c o", p=128)).then_inc(sWL, 16)
        e.dma_start(out=wkA.rearrange("p (c o) -> p c o", o=512),
                    in_=kvdwT.rearrange("(c p) o -> p c o", p=128)).then_inc(sWL, 16)
        e.dma_start(out=brow[0:1, 0:1024], in_=ctx.dram[bq_name][:]).then_inc(sWL, 16)
        e.dma_start(out=brow[0:1, 1024:1536], in_=ctx.dram[bkv_name][:]).then_inc(sWL, 16)
        with e.Fori(0, NTT) as tt:
            e.wait_ge(sMM, tt * 3)
            e.dma_start(out=cstg.rearrange("p (c q) -> p c q", q=128),
                        in_=xd4[:, :, bass.ds(tt * 128, 128)]).then_inc(sXS, 16)

    def f_tensor(e):
        e.wait_ge(sWL, 64)
        with e.Fori(0, NTT) as tt:
            e.wait_ge(sXS, tt * 16 + 16)
            for grp in range(3):
                bofs = [0, 512, 1024][grp]
                wsrc = wqA if grp < 2 else wkA
                ostride = 1024 if grp < 2 else 512
                obase = [0, 512, 0][grp]
                e.wait_ge(sEV, tt * 3 + grp)
                for dc in range(NDC):
                    e.matmul(ctx.pp[grp][:],
                             cstg[:, dc * 128:(dc + 1) * 128],
                             wsrc[:, dc * ostride + obase: dc * ostride + obase + 512],
                             start=(dc == 0), stop=False)
                ins_ = e.matmul(ctx.pp[grp][:], ctx.onesb[:],
                                brow[0:1, bofs: bofs + 512],
                                start=False, stop=True)
                ins_.then_inc(sMM, 1)
            for grp in range(3):
                src = cqn if grp < 2 else kvn
                sof = [0, 512, 0][grp]
                e.wait_ge(sAP, tt * 2 + (1 if grp < 2 else 2))
                e.wait_ge(sTE, tt * 3 + grp)
                for q4 in range(4):
                    ins_ = e.transpose(
                        ctx.ppb(4 + grp % 2)[:, q4 * 128:(q4 + 1) * 128],
                        src[:, sof + q4 * 128: sof + (q4 + 1) * 128],
                        ctx.ident[:])
                ins_.then_inc(sTR, 1)

    def f_vector(e):
        with e.Fori(0, NTT) as tt:
            e.wait_ge(sAP, tt * 2)
            e.wait_ge(sRS, tt * 6)
            for grp in range(3):
                e.wait_ge(sMM, tt * 3 + grp + 1)
                dst = cqsb[:, 0:512] if grp == 0 else \
                    cqsb[:, 512:1024] if grp == 1 else kvsb
                e.tensor_copy(dst, ctx.pp[grp][:]).then_inc(sEV, 1)
            e.wait_ge(sRS, tt * 6 + 3)
            e.wait_ge(sTR, tt * 3)
            e.tensor_scalar(cqn, cqsb, st1[:, 6:7], None,
                            ALU.mult).then_inc(sAP, 1)
            e.wait_ge(sRS, tt * 6 + 6)
            e.tensor_scalar(kvn, kvsb, st1[:, 9:10], None,
                            ALU.mult).then_inc(sAP, 1)

    def f_scalar(e):
        with e.Fori(0, NTT) as tt:
            e.wait_ge(sEV, tt * 3 + 2)
            e.wait_ge(sAP, tt * 2)
            e.activation(dumpq, cqsb, AF.Square,
                         accum_out=st1[:, 5:6]).then_inc(sRS, 1)
            e.wait_ge(sRS, tt * 6 + 1)
            e.activation(st1[:, 7:8], st1[:, 5:6], AF.Ln, bias=ctx.ceps2[:],
                         scale=1.0 / 1024.0).then_inc(sRS, 1)
            e.wait_ge(sRS, tt * 6 + 2)
            e.activation(st1[:, 6:7], st1[:, 7:8], AF.Exp,
                         scale=-0.5).then_inc(sRS, 1)
            e.wait_ge(sEV, tt * 3 + 3)
            e.activation(dumpkv, kvsb, AF.Square,
                         accum_out=st1[:, 8:9]).then_inc(sRS, 1)
            e.wait_ge(sRS, tt * 6 + 4)
            e.activation(st1[:, 7:8], st1[:, 8:9], AF.Ln, bias=ctx.ceps2[:],
                         scale=1.0 / 512.0).then_inc(sRS, 1)
            e.wait_ge(sRS, tt * 6 + 5)
            e.activation(st1[:, 9:10], st1[:, 7:8], AF.Exp,
                         scale=-0.5).then_inc(sRS, 1)
            for grp in range(3):
                dst = ctstg[:, 0:512] if grp == 0 else \
                    ctstg[:, 512:1024] if grp == 1 else ktstg
                e.wait_ge(sTR, tt * 3 + grp + 1)
                e.wait_ge(sST, tt * 32)
                e.activation(dst, ctx.ppb(4 + grp % 2)[:], AF.Copy).then_inc(sTE, 1)

    def f_sync(e):
        with e.Fori(0, NTT) as tt:
            e.wait_ge(sTE, tt * 3 + 2)
            e.dma_start(out=cqT4[:, :, bass.ds(tt * 128, 128)],
                        in_=ctstg.rearrange("p (c q) -> p c q", q=128)
                        ).then_inc(sST, 16)
            e.wait_ge(sTE, tt * 3 + 3)
            e.dma_start(out=ckvT4[:, :, bass.ds(tt * 128, 128)],
                        in_=ktstg.rearrange("p (c q) -> p c q", q=128)
                        ).then_inc(sST, 16)
        e.wait_ge(sST, NTT * 32)

    ctx.stage(sync=f_sync, tensor=f_tensor, vector=f_vector,
              scalar=f_scalar, gpsimd=f_gpsimd)


def emit_fproj(ctx, wT_name, I, O, rhs_name, out_name, biasrows_name=None):
    nc = ctx.nc
    wT = ctx.dram[wT_name]
    out = ctx.dram[out_name]
    rhs_d = ctx.dram[rhs_name]
    br = ctx.dram[biasrows_name] if biasrows_name else None
    NK = I // 128
    NOC = O // 128
    WB = NK * O * 2
    NB = 2 if WB + NK * 2048 + 4096 <= 57344 - 2560 else 1
    assert WB + NB * NK * 1024 + 4096 <= 57344 - 2560 + 4096
    assert NOC % 2 == 0

    sWL = ctx.sem("fpWL", "sw")
    sBL = ctx.sem("fpBL", "sw")
    sRL = [ctx.sem("fpRL0", "sw"), ctx.sem("fpRL1", "sw")]
    sMM = ctx.sem("fpMM")
    sEV = ctx.sem("fpEV")
    sST = [ctx.sem("fpST0", "hw"), ctx.sem("fpST1", "hw")]

    wA = ctx.a2(BF16, 0, NK * O)
    rslc = [ctx.a2(BF16, WB + b * NK * 1024, NK * 512) for b in range(NB)]
    ostg = [ctx.a2(BF16, WB + NB * NK * 1024, 512),
            ctx.a2(BF16, WB + NB * NK * 1024 + 1024, 512)]
    bias_sb = ctx.a2(F32, WB + NB * NK * 1024 + 2048, 128)
    o4 = out.rearrange("(c p) t -> c p t", p=128)
    r4 = rhs_d.rearrange("(c p) t -> p c t", p=128)

    def f_gpsimd(e):
        e.dma_start(out=wA.rearrange("p (c o) -> p c o", o=O),
                    in_=wT.rearrange("(c p) o -> p c o", p=128)).then_inc(sWL, 16)
        if br is not None:
            e.dma_start(out=bias_sb[:, 0:NOC],
                        in_=br.rearrange("c p -> p c"),
                        allow_slow_non_contiguous=True).then_inc(sBL, 16)
        with e.Fori(0, 4) as i:
            for b in range(2):
                e.wait_ge(sMM, (i * 2 + b) * NOC + (NOC if NB == 1 else 0))
                e.dma_start(out=rslc[b % NB].rearrange("p (c u) -> p c u", u=512),
                            in_=r4[:, :, bass.ds((i * 2 + b) * 512, 512)]
                            ).then_inc(sRL[b % NB], 16)

    def f_tensor(e):
        e.nop().then_inc(sMM, NOC)
        e.wait_ge(sWL, 16)
        with e.Fori(0, 4) as i:
            for b in range(2):
                if NB == 2:
                    e.wait_ge(sRL[b], i * 16 + 16)
                else:
                    e.wait_ge(sRL[0], i * 32 + b * 16 + 16)
                for oc in range(NOC):
                    g = (i * 2 + b) * NOC + oc
                    if oc % 2 == 0:
                        e.wait_ge(sEV, g + 2)
                    for kc in range(NK):
                        ins_ = e.matmul(ctx.pp[oc % 2][:],
                                        wA[:, kc * O + oc * 128:
                                           kc * O + (oc + 1) * 128],
                                        rslc[b % NB][:, kc * 512:(kc + 1) * 512],
                                        start=(kc == 0), stop=(kc == NK - 1))
                    ins_.then_inc(sMM, 1)

    def f_scalar(e):
        e.nop().then_inc(sEV, 2)
        if br is not None:
            e.wait_ge(sBL, 16)
        with e.Fori(0, 4) as i:
            for b in range(2):
                for oc in range(NOC):
                    g = (i * 2 + b) * NOC + oc
                    e.wait_ge(sMM, NOC + g + 1)
                    e.wait_ge(sST[oc % 2],
                              ((i * 2 + b) * (NOC // 2) + oc // 2) * 16)
                    if br is not None:
                        ins_ = e.activation(ostg[oc % 2], ctx.pp[oc % 2][:],
                                            AF.Identity,
                                            bias=bias_sb[:, oc: oc + 1])
                    else:
                        ins_ = e.activation(ostg[oc % 2], ctx.pp[oc % 2][:],
                                            AF.Copy)
                    ins_.then_inc(sEV, 1)

    def f_sync(e):
        with e.Fori(0, 4) as i:
            for b in range(2):
                for oc in range(NOC):
                    g = (i * 2 + b) * NOC + oc
                    e.wait_ge(sEV, g + 3)
                    e.dma_start(out=o4[oc][:, bass.ds((i * 2 + b) * 512, 512)],
                                in_=ostg[oc % 2]).then_inc(sST[oc % 2], 16)
        e.wait_ge(sST[0], 8 * (NOC // 2) * 16)
        e.wait_ge(sST[1], 8 * (NOC // 2) * 16)

    ctx.stage(sync=f_sync, tensor=f_tensor, scalar=f_scalar, gpsimd=f_gpsimd)


def emit_mla_assemble(ctx, kvT_name, kropeO_name, kT_name, v_name):
    nc = ctx.nc
    kvT = ctx.dram[kvT_name]
    krope = ctx.dram[kropeO_name]
    kT = ctx.dram[kT_name]
    v = ctx.dram[v_name]

    sCP = ctx.sem("maCP", "sw")
    sVL = ctx.sem("maVL", "sw")
    sTR = ctx.sem("maTR")
    sTE = ctx.sem("maTE")
    sVS = ctx.sem("maVS", "hw")

    ksb = ctx.a2(BF16, 0, 4096)
    vstg = ctx.a2(BF16, 8192, 4096)

    def f_gpsimd(e):
        with e.Fori(0, 16) as h:
            e.dma_start(out=kT[bass.ds(h * 128, 64), :],
                        in_=kvT[bass.ds(h * 192, 64), :]).then_inc(sCP, 16)
            e.dma_start(out=kT[bass.ds(h * 128 + 64, 64), :],
                        in_=krope[bass.ds(h * 64, 64), :]).then_inc(sCP, 16)
            e.wait_ge(sTR, h * 8)
            e.dma_start(out=ksb, in_=kvT[bass.ds(h * 192 + 64, 128), :]
                        ).then_inc(sVL, 16)
        e.wait_ge(sCP, 16 * 32)

    def f_tensor(e):
        with e.Fori(0, 16) as h:
            e.wait_ge(sVL, h * 16 + 16)
            for g in range(8):
                e.wait_ge(sTE, h * 8 + g)
                for q4 in range(4):
                    tb = g * 4 + q4
                    ins_ = e.transpose(ctx.ppb(g % 2)[:, q4 * 128:(q4 + 1) * 128],
                                       ksb[:, tb * 128:(tb + 1) * 128],
                                       ctx.ident[:])
                ins_.then_inc(sTR, 1)

    def f_vector(e):
        with e.Fori(0, 16) as h:
            e.wait_ge(sVS, h * 16)
            for g in range(8):
                e.wait_ge(sTR, h * 8 + g + 1)
                e.tensor_copy(vstg[:, g * 512:(g + 1) * 512],
                              ctx.ppb(g % 2)[:]).then_inc(sTE, 1)

    def f_sync(e):
        v5 = v.rearrange("(n p) (g q) -> g p n q", p=128, q=128)
        with e.Fori(0, 16) as h:
            e.wait_ge(sTE, h * 8 + 8)
            e.dma_start(out=v5[bass.ds(h, 1)][0],
                        in_=vstg.rearrange("p (n q) -> p n q", q=128)
                        ).then_inc(sVS, 16)
        e.wait_ge(sVS, 16 * 16)

    ctx.stage(sync=f_sync, tensor=f_tensor, vector=f_vector, gpsimd=f_gpsimd)


PREP_OUT = {
    "l0a": [("l0a_qdwT", [2048, 1024]), ("l0a_quwT", [1024, 2048]),
            ("l0a_kdwT", [2048, 512]), ("l0a_kuwT", [512, 3072]),
            ("l0a_krwT", [2048, 1024]), ("l0a_owT", [2048, 2048])],
    "l0f": [("l0f_wguT", [2048, 16384]), ("l0f_wpT", [8192, 2048])],
    "l1a": [("l1a_wfT", [2048, 2048]), ("l1a_wzgT", [2048, 2048]),
            ("l1a_wzT", [2048, 2048]), ("l1a_wgaT", [2048, 2048]),
            ("l1a_wgbT", [2048, 2048]), ("l1a_woutT", [2048, 2048])],
    "l1f": [("l1f_rT", [2048, 8]), ("l1f_wgT", [2048, 6144]),
            ("l1f_wuT", [2048, 6144]), ("l1f_wdT", [768, 16384])],
}
PREP_RAW = {
    "l0a": [("l0a_qdw", [1024, 2048]), ("l0a_qnw", [1024]),
            ("l0a_quw", [2048, 1024]), ("l0a_kdw", [512, 2048]),
            ("l0a_knw", [512]), ("l0a_kuw", [3072, 512]),
            ("l0a_krw", [1024, 2048]), ("l0a_ow", [2048, 2048])],
    "l0f": [("l0f_wgu", [16384, 2048]), ("l0f_wp", [2048, 8192])],
    "l1a": [("l1a_wf", [2048, 2048]), ("l1a_wzg", [2048, 2048]),
            ("l1a_wz", [2048, 2048]), ("l1a_wga", [2048, 2048]),
            ("l1a_wgb", [2048, 2048]), ("l1a_wout", [2048, 2048]),
            ("l1a_rms", [2048])],
    "l1f": [("l1f_wr", [8, 2048]), ("l1f_wg", [6144, 2048]),
            ("l1f_wu", [6144, 2048]), ("l1f_wd", [16384, 768])],
}
SUB_FAM = {0: "l0a", 1: "l0f", 2: None, 3: "l0f", 4: "l1a", 5: "l1f",
           6: None, 7: "l1f"}


def emit_prep_family(ctx, fam):
    if fam == "l0a":
        emit_wtrans(ctx, "l0a_qdw", "l0a_qdwT", 1024, 2048)
        emit_wtrans(ctx, "l0a_quw", "l0a_quwT", 2048, 1024,
                    scale_name="l0a_qnw")
        emit_wtrans(ctx, "l0a_kdw", "l0a_kdwT", 512, 2048)
        emit_wtrans(ctx, "l0a_kuw", "l0a_kuwT", 3072, 512,
                    scale_name="l0a_knw")
        emit_wtrans(ctx, "l0a_krw", "l0a_krwT", 1024, 2048)
        emit_wtrans(ctx, "l0a_ow", "l0a_owT", 2048, 2048)
    elif fam == "l0f":
        emit_wtrans(ctx, "l0f_wgu", "l0f_wguT", 16384, 2048, quant=True)
        emit_wtrans(ctx, "l0f_wp", "l0f_wpT", 2048, 8192, quant=True)
    elif fam == "l1a":
        for n in ["wf", "wzg", "wz", "wga", "wgb"]:
            emit_wtrans(ctx, f"l1a_{n}", f"l1a_{n}T", 2048, 2048,
                        scale_name="l1a_rms")
        emit_wtrans(ctx, "l1a_wout", "l1a_woutT", 2048, 2048)
    elif fam == "l1f":
        emit_router_prep(ctx, "l1f_wr", "l1f_rT")
        emit_wtrans(ctx, "l1f_wg", "l1f_wgT", 6144, 2048)
        emit_wtrans(ctx, "l1f_wu", "l1f_wuT", 6144, 2048)
        emit_wtrans(ctx, "l1f_wd", "l1f_wdT", 16384, 768)


def build_preps(ctx, fams):
    innames = []
    for fam in fams:
        for n, shp in PREP_RAW[fam]:
            innames.append(n); ctx.inp(n, shp)
        for n, shp in PREP_OUT[fam]:
            ctx.outp(n, shp, BF16)
        emit_prep_family(ctx, fam)
    return innames, [n for fam in fams for n, _ in PREP_OUT[fam]]


def build_module(ctx, lo=0, hi=8, preps="inline"):
    T_, D_ = T, D
    NH, DK, DC, DCQ, DR, DN = 16, 128, 512, 1024, 64, 64
    E, F_ = 8, 768
    innames = []

    def inp(n, shp, dt=F32):
        innames.append(n)
        return ctx.inp(n, shp, dt)

    subs = range(lo, hi)
    need = set(SUB_FAM[i] for i in subs) - {None}
    if 0 in subs:
        need.add("l0a")
    if 2 in subs:
        need.add("l0b")
    if 6 in subs:
        need.add("l1b")

    sc = ctx.scratch
    bn = {}
    for i in range(lo, hi + 1):
        if i == lo:
            bn[i] = "x0"; inp("x0", [T_, D_])
        elif i == hi:
            bn[i] = "xOUT"; ctx.outp("xOUT", [T_, D_])
        else:
            bn[i] = f"xm{i}"; sc(bn[i], [T_, D_], F32)

    for fam in sorted(need & set(PREP_OUT)):
        if preps == "import":
            for n, shp in PREP_OUT[fam]:
                inp(n, shp, BF16)
        else:
            for n, shp in PREP_RAW[fam]:
                inp(n, shp)
            for n, shp in PREP_OUT[fam]:
                sc(n, shp, BF16)
            emit_prep_family(ctx, fam)

    if "l0a" in need:
        inp("l0a_bq_row", [1, DCQ]); inp("l0a_bkv_row", [1, DC])
        inp("l0a_qub_rows", [16, 128]); inp("l0a_kub_rows", [24, 128])
        inp("l0a_krb_rows", [8, 128]); inp("l0a_ob_tile", [128, D_])
        for n, shp in [("xnTd", [D_, T_]), ("cqT", [DCQ, T_]),
                       ("ckvT", [DC, T_]), ("qTf", [NH * DK, T_]),
                       ("kvTf", [NH * (DN + DK), T_]),
                       ("krTf", [NH * DR, T_]), ("kTa", [NH * DK, T_]),
                       ("va", [T_, NH * DK])]:
            sc(n, shp, BF16)
    if "l0f" in need:
        sc("ffn_hT", [8192, T_], BF16); sc("ffn_rstdT", [32, 128], F32)
    if "l0b" in need:
        inp("l0b_wqkv", [D_, 3 * D_]); inp("l0b_wproj", [D_, D_])
        inp("l0b_qkb", [32, 128]); inp("l0b_vbias", [128, D_])
        inp("l0b_pbias", [128, D_])
        sc("qkT", [4096, T_], BF16); sc("l0b_v", [T_, D_], BF16)
    if "l1a" in need:
        inp("l1a_fb_rows", [16, 128]); inp("l1a_ob_tile", [128, D_])
        sc("gated", [2048, T_], BF16); sc("xpl", [T_, D_], F32)
    if "l1f" in need:
        sc("moe_w", [E, T_], F32); sc("moe_hT", [E * F_, T_], BF16)
    if "l1b" in need:
        inp("l1b_wcat", [D_, 3072]); inp("l1b_wout", [D_, D_])
        inp("l1b_qcos", [128, T_]); inp("l1b_qsin", [128, T_])
        inp("l1b_kcos", [128, T_]); inp("l1b_ksin", [128, T_])
        sc("l1b_qkraw", [2560, T_], BF16); sc("l1b_v", [T_, 512], BF16)
        sc("l1b_qT", [2048, T_], BF16); sc("l1b_kT", [512, T_], BF16)
    if need & {"l0a", "l0b", "l1a", "l1b"}:
        sc("attnT", [D_, T_], BF16)

    def ffn(x_in, x_out):
        emit_ln(ctx, x_in)
        emit_ffn_gu(ctx, "l0f_wguT", "ffn_hT", "ffn_rstdT")
        emit_ffn_down(ctx, "ffn_hT", "l0f_wpT", "ffn_rstdT", x_in, x_out)

    def moe(x_in, x_out):
        emit_ln(ctx, x_in)
        emit_moe_router(ctx, "l1f_rT", "moe_w")
        emit_moe_gu(ctx, "l1f_wgT", "l1f_wuT", "moe_hT")
        emit_moe_scale(ctx, "moe_hT", "moe_w")
        emit_moe_down(ctx, "moe_hT", "l1f_wdT", x_in, x_out)

    def sub0(x_in, x_out):
        emit_ln(ctx, x_in, xnT_dram="xnTd")
        emit_mla_cqkv(ctx, "l0a_qdwT", "l0a_kdwT", "l0a_bq_row",
                      "l0a_bkv_row", "xnTd", "cqT", "ckvT")
        emit_fproj(ctx, "l0a_quwT", DCQ, NH * DK, "cqT", "qTf",
                   "l0a_qub_rows")
        emit_fproj(ctx, "l0a_kuwT", DC, NH * (DN + DK), "ckvT", "kvTf",
                   "l0a_kub_rows")
        emit_fproj(ctx, "l0a_krwT", D_, NH * DR, "xnTd", "krTf",
                   "l0a_krb_rows")
        emit_mla_assemble(ctx, "kvTf", "krTf", "kTa", "va")
        emit_attn_core2(ctx, "qTf", "kTa", "va", "attnT", 16, 16, True)
        emit_proj_resid(ctx, "attnT", "l0a_owT", x_in, x_out, "l0a_ob_tile")

    def sub2(x_in, x_out):
        emit_ln(ctx, x_in)
        emit_mha_qkv(ctx, "l0b_wqkv", "qkT", "l0b_v", "l0b_qkb", "l0b_vbias")
        ctx.dram["l0b_qT_v"] = ctx.dram["qkT"][0:2048, :]
        ctx.dram["l0b_kT_v"] = ctx.dram["qkT"][2048:4096, :]
        emit_attn_core2(ctx, "l0b_qT_v", "l0b_kT_v", "l0b_v", "attnT",
                       16, 16, True)
        emit_proj_resid(ctx, "attnT", "l0b_wproj", x_in, x_out, "l0b_pbias")

    def sub4(x_in, x_out):
        emit_ln(ctx, x_in, mode="ln_rms", xplus_dram="xpl")
        emit_l1a_core(ctx, ["l1a_wfT", "l1a_wzgT", "l1a_wzT", "l1a_wgaT",
                            "l1a_wgbT"], "l1a_fb_rows", "gated")
        emit_proj_resid(ctx, "gated", "l1a_woutT", "xpl", x_out,
                        "l1a_ob_tile")

    def sub6(x_in, x_out):
        emit_ln(ctx, x_in)
        emit_mha_qkv(ctx, "l1b_wcat", "l1b_qkraw", "l1b_v", None, None,
                     n_qk_chunks=20, v_col0=2560, n_v_cols=512)
        emit_l1b_rmsrope(ctx, "l1b_qkraw", "l1b_qT", "l1b_kT")
        emit_attn_core2(ctx, "l1b_qT", "l1b_kT", "l1b_v", "attnT", 16, 4,
                       causal=False, scale=1.0)
        emit_proj_resid(ctx, "attnT", "l1b_wout", x_in, x_out, None)

    emitters = {0: sub0, 1: ffn, 2: sub2, 3: ffn, 4: sub4, 5: moe,
                6: sub6, 7: moe}
    for i in subs:
        emitters[i](bn[i], bn[i + 1])
    return innames


PLAN = "fused"
PLAN_MULTI = [("prep", ["l0a"]), ("prep", ["l0f"]), ("prep", ["l1a"]),
              ("prep", ["l1f"])] + [("subs", i, i + 1) for i in range(8)]
_PROG_CACHE = {}


def _get_program(key):
    if key not in _PROG_CACHE:
        es = ExitStack()
        nc = bacc.Bacc("TRN2", target_bir_lowering=False)
        ctx = Ctx(nc, es)
        with es:
            if key[0] == "prep":
                innames, outnames = build_preps(ctx, list(key[1:]))
            elif key[0] == "full":
                innames = build_module(ctx, 0, 8, preps="inline")
                outnames = ["xOUT"]
            else:
                innames = build_module(ctx, key[1], key[2], preps="import")
                outnames = ["xOUT"]
            finalize(ctx)
        nc.finalize()
        _PROG_CACHE[key] = (nc, dict(ctx.consts), innames, outnames)
    return _PROG_CACHE[key]


def host_inputs(inputs):
    H = 128
    fraction = np.arange(0, H, 2, dtype=np.float32) / H
    timescale = (10000.0 ** fraction).astype(np.float32)
    pos = np.arange(T, dtype=np.float32)
    sinus = np.einsum("i,j->ij", pos, 1.0 / timescale)
    sin64 = np.sin(sinus).astype(np.float32)
    cos64 = np.cos(sinus).astype(np.float32)
    qscale = np.float32(1.0) / np.sqrt(np.float32(H)).astype(np.float32)
    i = inputs
    base = {
        "l0a_qdw": i["L0_attn_a_q_down_w"], "l0a_qnw": i["L0_attn_a_q_norm_w"],
        "l0a_quw": i["L0_attn_a_q_up_w"], "l0a_kdw": i["L0_attn_a_kv_down_w"],
        "l0a_knw": i["L0_attn_a_kv_norm_w"], "l0a_kuw": i["L0_attn_a_kv_up_w"],
        "l0a_krw": i["L0_attn_a_k_rope_w"], "l0a_ow": i["L0_attn_a_out_w"],
        "l0b_wqkv": i["L0_attn_b_W_qkv"], "l0b_wproj": i["L0_attn_b_W_proj"],
        "l0f_wgu": i["L0_ffn_w_gate_up"], "l0f_wp": i["L0_ffn_w_proj"],
        "l1a_wf": i["L1_attn_a_Wf_w"], "l1a_wzg": i["L1_attn_a_Wzg_w"],
        "l1a_wz": i["L1_attn_a_Wz_w"], "l1a_wga": i["L1_attn_a_Wga_w"],
        "l1a_wgb": i["L1_attn_a_Wgb_w"], "l1a_wout": i["L1_attn_a_Wout_w"],
        "l1a_rms": i["L1_attn_a_rms_w"],
        "l1b_wcat": np.concatenate([i["L1_attn_b_Wq"], i["L1_attn_b_Wk"],
                                    i["L1_attn_b_Wv"]], axis=1),
        "l1b_wout": i["L1_attn_b_Wout"],
        "l1f_wr": i["L1_ffn_w_gate_router"],
        "l1f_wg": np.ascontiguousarray(i["L1_ffn_w_gate_proj"]).reshape(6144, 2048),
        "l1f_wu": np.ascontiguousarray(i["L1_ffn_w_up_proj"]).reshape(6144, 2048),
        "l1f_wd": np.ascontiguousarray(i["L1_ffn_w_down_proj"]).reshape(16384, 768),
        "l0a_bq_row": np.ascontiguousarray(i["L0_attn_a_q_down_b"]).reshape(1, 1024),
        "l0a_bkv_row": np.ascontiguousarray(i["L0_attn_a_kv_down_b"]).reshape(1, 512),
        "l0a_qub_rows": np.ascontiguousarray(i["L0_attn_a_q_up_b"]).reshape(16, 128),
        "l0a_kub_rows": np.ascontiguousarray(i["L0_attn_a_kv_up_b"]).reshape(24, 128),
        "l0a_krb_rows": np.ascontiguousarray(i["L0_attn_a_k_rope_b"]).reshape(8, 128),
        "l0a_ob_tile": np.broadcast_to(i["L0_attn_a_out_b"], (128, 2048)),
        "l0b_qkb": np.ascontiguousarray(i["L0_attn_b_b_qkv"][:4096]).reshape(32, 128),
        "l0b_vbias": np.broadcast_to(i["L0_attn_b_b_qkv"][4096:], (128, 2048)),
        "l0b_pbias": np.broadcast_to(i["L0_attn_b_b_proj"], (128, 2048)),
        "l1a_fb_rows": np.ascontiguousarray(i["L1_attn_a_Wf_b"]).reshape(16, 128),
        "l1a_ob_tile": np.broadcast_to(i["L1_attn_a_Wout_b"], (128, 2048)),
        "l1b_qcos": np.concatenate([cos64, cos64], 1).T * qscale,
        "l1b_qsin": np.concatenate([sin64, sin64], 1).T * qscale,
        "l1b_kcos": np.concatenate([cos64, cos64], 1).T,
        "l1b_ksin": np.concatenate([sin64, sin64], 1).T,
    }
    return {k: np.ascontiguousarray(v, dtype=np.float32)
            for k, v in base.items()}


def kernel(**inputs):
    base = host_inputs(inputs)
    X = np.ascontiguousarray(inputs["L0_attn_a_x"], dtype=np.float32)
    shards = [X[c] for c in range(8)]
    if PLAN == "fused":
        nc, consts, innames, outnames = _get_program(("full",))
        feed = dict(consts)
        feed.update({k: base[k] for k in innames if k != "x0"})
        in_maps = [dict(feed, x0=np.ascontiguousarray(shards[c]))
                   for c in range(8)]
        res = run_bass_kernel_spmd(nc, in_maps, core_ids=list(range(8)))
        shards = [np.asarray(res.results[c]["xOUT"]) for c in range(8)]
        return np.stack(shards, axis=0).astype(np.float32)
    for entry in PLAN_MULTI:
        if entry[0] == "prep":
            key = tuple(entry[:1]) + tuple(entry[1])
            nc, consts, innames, outnames = _get_program(key)
            feed = dict(consts)
            feed.update({k: base[k] for k in innames})
            res = run_bass_kernel_spmd(nc, [feed], core_ids=[0])
            for n in outnames:
                base[n] = np.asarray(res.results[0][n])
        else:
            key = entry
            nc, consts, innames, outnames = _get_program(key)
            feed = dict(consts)
            feed.update({k: base[k] for k in innames if k != "x0"})
            in_maps = [dict(feed, x0=np.ascontiguousarray(shards[c]))
                       for c in range(8)]
            res = run_bass_kernel_spmd(nc, in_maps, core_ids=list(range(8)))
            shards = [np.asarray(res.results[c]["xOUT"]) for c in range(8)]
    return np.stack(shards, axis=0).astype(np.float32)
